# version 02
# baseline (speedup 1.0000x reference)
import contextlib
import math
import numpy as np
import ml_dtypes
import concourse.bass as bass
import concourse.mybir as mybir
from concourse.bass_utils import run_bass_kernel_spmd

F32 = mybir.dt.float32
BF16 = mybir.dt.bfloat16
I32 = mybir.dt.int32
U16 = mybir.dt.uint16
AF = mybir.ActivationFunctionType
ALU = mybir.AluOpType
AX = mybir.AxisListType
bf16 = ml_dtypes.bfloat16

T = 2048
D = 2048
NT = T // 128
ND = D // 128

_2PI = 2.0 * math.pi
CW1 = float(np.float32(_2PI))
CW2 = float(np.float32(_2PI - CW1))
CW3 = float(np.float32(_2PI - CW1 - CW2))

NF4_VALS = np.array([-1.0, -0.6962, -0.5251, -0.3949, -0.2844, -0.1848,
                     -0.0911, 0.0, 0.0796, 0.1609, 0.2461, 0.3379, 0.4407,
                     0.5626, 0.723, 1.0], dtype=bf16).astype(np.float64)
NF4_MIDS = ((NF4_VALS[:-1] + NF4_VALS[1:]) / 2.0).astype(np.float32)
NF4_DELTAS = np.diff(NF4_VALS).astype(np.float32)

C_EPS, C_NEGHALF, C_L0TH, C_L1TH, C_INVD, C_NEG1, C_HALF = range(7)
NCONST = 7


def host_consts():
    cst = np.zeros((128, NCONST), dtype=np.float32)
    cst[:, C_EPS] = 1e-5
    cst[:, C_NEGHALF] = -0.5
    cst[:, C_L0TH] = -1.0 / 64.0
    cst[:, C_L1TH] = -1.0 / 32.0
    cst[:, C_INVD] = 1.0 / 2048.0
    cst[:, C_NEG1] = -1.0
    cst[:, C_HALF] = 0.5
    it16 = (np.arange(128)[:, None] + 128 * np.arange(16)[None, :]).astype(np.int32)
    j64 = np.tile(np.arange(64, dtype=np.int32)[None, :], (128, 1))
    j32 = np.tile(np.arange(32, dtype=np.int32)[None, :], (128, 1))
    masks = np.zeros((128, 4, 512), dtype=bf16)
    kk = np.arange(128)[:, None]
    qq = np.arange(512)[None, :]
    for j in range(4):
        masks[:, j, :] = ((j * 128 + kk) <= qq).astype(bf16)
    masks = masks.reshape(128, 4 * 512)
    ident = np.eye(128, dtype=bf16)
    ones_row = np.ones((1, 128), dtype=bf16)
    negones_row = -np.ones((1, 128), dtype=bf16)
    ones_col = np.ones((128, 1), dtype=bf16)
    return dict(cst=cst, it16=it16, j64=j64, j32=j32, masks=masks, ident=ident,
                ones_row=ones_row, negones_row=negones_row, ones_col=ones_col)


class Prog:

    ENGINES = ("sync", "vector", "scalar", "tensor", "gpsimd")

    def __init__(self, nc, es):
        self.nc = nc
        self.es = es
        self.steps = {e: [] for e in self.ENGINES}
        self.sems = {e: es.enter_context(nc.semaphore("S" + e)) for e in self.ENGINES}
        self.cnt = {e: 0 for e in self.ENGINES}
        self.wr = {}
        self.rd = {}
        self.alias = {}

    def _dma_sem(self, key):
        sk = "dma:" + key
        if sk not in self.sems:
            self.sems[sk] = self.es.enter_context(self.nc.semaphore("Sd_" + key))
            self.cnt[sk] = 0
        return sk

    def _waits_from(self, myeng, toks, waits):
        for sk, c in toks.items():
            if sk == myeng:
                pass
            waits[sk] = max(waits.get(sk, 0), c)

    def emit(self, eng, fn, reads=(), writes=(), dma=False):
        al = self.alias
        reads = [al.get(b, b) for b in reads]
        writes = [al.get(b, b) for b in writes]
        waits = {}
        for b in reads:
            self._waits_from(eng, self.wr.get(b, {}), waits)
        for b in writes:
            self._waits_from(eng, self.wr.get(b, {}), waits)
            self._waits_from(eng, self.rd.get(b, {}), waits)
        if dma:
            semkey = self._dma_sem(writes[0])
            self.cnt[semkey] += 16
        else:
            semkey = eng
            self.cnt[semkey] += 1
        tok = (semkey, self.cnt[semkey])
        for b in reads:
            self.rd.setdefault(b, {})[tok[0]] = tok[1]
        for b in writes:
            self.wr.setdefault(b, {})[tok[0]] = tok[1]
            self.rd[b] = {}
        wl = sorted(waits.items())
        sems = self.sems
        mysem = sems[semkey]
        amt = 16 if dma else 1

        def step(e, _fn=fn, _wl=wl, _amt=amt, _mysem=mysem):
            for sk, v in _wl:
                e.wait_ge(sems[sk], v)
            _fn(e).then_inc(_mysem, _amt)
        self.steps[eng].append(step)

    def alias_barrier(self, old_keys, new_keys):
        merged = {}
        for k in old_keys:
            for src in (self.wr.get(k, {}), self.rd.get(k, {})):
                for sk, c in src.items():
                    merged[sk] = max(merged.get(sk, 0), c)
        for nk in new_keys:
            self.wr.setdefault(nk, {}).update(merged)

    def finalize(self, block):
        final_waits = [(sk, self.cnt[sk]) for sk in self.sems
                       if sk.startswith("dma:") and self.cnt.get(sk, 0) > 0]
        for name in self.ENGINES:
            steps = self.steps[name]
            sem = self.sems[name]
            fw = final_waits if name == "sync" else None

            def body(e, steps=steps, fw=fw, sem=sem, name=name):
                for s in steps:
                    s(e)
                if fw is not None:
                    for sk, v in fw:
                        e.wait_ge(self.sems[sk], v)
            getattr(block, name)(body)



def emit_floor_pos(p, eng, out, src, key_out, key_src, tmp, key_tmp,
                   r, key_r, top_bit, nbits):
    p.emit(eng, lambda e: e.tensor_copy(r, src), reads=[key_src], writes=[key_r])
    for i in range(nbits):
        b = float(2 ** (top_bit - i))
        if i == 0:
            p.emit(eng, lambda e, b=b: e.tensor_scalar(out, r, b, b, ALU.is_ge, ALU.mult),
                   reads=[key_r], writes=[key_out])
            p.emit(eng, lambda e: e.tensor_tensor(r, r, out, ALU.subtract),
                   reads=[key_out], writes=[key_r])
        else:
            p.emit(eng, lambda e, b=b: e.tensor_scalar(tmp, r, b, b, ALU.is_ge, ALU.mult),
                   reads=[key_r], writes=[key_tmp])
            p.emit(eng, lambda e: e.tensor_tensor(out, out, tmp, ALU.add),
                   reads=[key_tmp], writes=[key_out])
            if i < nbits - 1:
                p.emit(eng, lambda e: e.tensor_tensor(r, r, tmp, ALU.subtract),
                       reads=[key_tmp], writes=[key_r])


class Tables:
    pass


def emit_prelude(p, S, need_l0_rope=False, need_l1_rope=False):
    dram = S["_dram"]
    for nm in ("cst", "it16", "j64", "j32", "masks", "ident", "ones_row",
               "negones_row", "ones_col"):
        if nm in S:
            p.emit("sync", lambda e, a=S[nm], b=dram[nm]: e.dma_start(out=a[:], in_=b),
                   writes=[nm], dma=True)
    cst = S["cst"]
    p.emit("vector", lambda e: e.memset(S["one_col"][:], 1.0), writes=["one_col"])
    p.emit("scalar", lambda e: e.activation(S["pi_col"][:], S["one_col"][:], AF.Arctan),
           reads=["one_col"], writes=["pi_col"])
    p.emit("vector", lambda e: e.tensor_single_scalar(S["pi_col"][:], S["pi_col"][:], 4.0, ALU.mult),
           reads=["pi_col"], writes=["pi_col"])
    p.emit("vector", lambda e: e.tensor_single_scalar(S["pih_col"][:], S["pi_col"][:], 0.5, ALU.mult),
           reads=["pi_col"], writes=["pih_col"])
    p.emit("vector", lambda e: e.tensor_single_scalar(S["negpi_col"][:], S["pi_col"][:], -1.0, ALU.mult),
           reads=["pi_col"], writes=["negpi_col"])
    p.emit("vector", lambda e: e.tensor_single_scalar(S["tmp_col"][:], S["pi_col"][:], 2.0, ALU.mult),
           reads=["pi_col"], writes=["tmp_col"])
    p.emit("vector", lambda e: e.reciprocal(S["r2pi_col"][:], S["tmp_col"][:]),
           reads=["tmp_col"], writes=["r2pi_col"])
    for nm, v in (("invsqrt128_col", 128.0), ("invsqrt64_col", 64.0),
                  ("invsqrt2048_col", 2048.0), ("invsqrt8192_col", 8192.0)):
        if nm not in S:
            continue
        p.emit("vector", lambda e, v=v: e.memset(S["tmp_col"][:], v), writes=["tmp_col"])
        p.emit("scalar", lambda e: e.activation(S["tmp2_col"][:], S["tmp_col"][:], AF.Sqrt),
               reads=["tmp_col"], writes=["tmp2_col"])
        p.emit("vector", lambda e, nm=nm: e.reciprocal(S[nm][:], S["tmp2_col"][:]),
               reads=["tmp2_col"], writes=[nm])
    p.emit("vector", lambda e: e.memset(S["tmp_col"][:], 10000.0), writes=["tmp_col"])
    p.emit("scalar", lambda e: e.activation(S["ln1e4_col"][:], S["tmp_col"][:], AF.Ln),
           reads=["tmp_col"], writes=["ln1e4_col"])

    def rope_tables(jname, cscale_idx, npairs, cos_name, sin_name, interleaved):
        HF = 8 * npairs
        ang, red, kk, tt2 = S["rt0"], S["rt1"], S["rt2"], S["rt3"]
        tmp, rr, sn, cs = S["rt4"], S["rt5"], S["rt6"], S["rt7"]
        p.emit("vector", lambda e: e.tensor_copy(S["rt_jf"][:, 0:npairs], S[jname][:, 0:npairs]),
               reads=[jname], writes=["rt_jf"])
        p.emit("vector", lambda e: e.tensor_scalar(
            S["lnsc_col"][:], S["ln1e4_col"][:], cst[:, cscale_idx:cscale_idx + 1], None, ALU.mult),
            reads=["ln1e4_col", "cst"], writes=["lnsc_col"])
        p.emit("scalar", lambda e: e.activation(S["rt_th"][:, 0:npairs], S["rt_jf"][:, 0:npairs],
                                                AF.Exp, bias=0.0, scale=S["lnsc_col"][:]),
               reads=["rt_jf", "lnsc_col"], writes=["rt_th"])
        p.emit("vector", lambda e: e.tensor_copy(S["rt_pos"][:], S["it16"][:]),
               reads=["it16"], writes=["rt_pos"])

        def sin_of(src_ap, out_ap):
            p.emit("vector", lambda e: e.tensor_scalar(
                tt2[:, 0:HF], src_ap, S["r2pi_col"][:], 0.5, ALU.mult, ALU.add),
                reads=["big", "r2pi_col"], writes=["big"])
            emit_floor_pos(p, "vector", kk[:, 0:HF], tt2[:, 0:HF], "big", "big",
                           tmp[:, 0:HF], "big", rr[:, 0:HF], "big",
                           top_bit=8, nbits=9)
            for c in (CW1, CW2, CW3):
                first = c is CW1
                p.emit("vector", lambda e, c=c, first=first: e.scalar_tensor_tensor(
                    red[:, 0:HF], kk[:, 0:HF], -c, (src_ap if first else red[:, 0:HF]),
                    ALU.mult, ALU.add),
                    reads=["big"], writes=["big"])
            p.emit("vector", lambda e: e.tensor_scalar(
                red[:, 0:HF], red[:, 0:HF], S["pi_col"][:], S["negpi_col"][:], ALU.min, ALU.max),
                reads=["big", "pi_col", "negpi_col"], writes=["big"])
            p.emit("scalar", lambda e, out_ap=out_ap: e.activation(out_ap, red[:, 0:HF], AF.Sin),
                   reads=["big"], writes=["big"])

        cosT, sinT = S[cos_name], S[sin_name]
        for half in (0, 1):
            t0 = 8 * half
            p.emit("vector", lambda e, t0=t0: e.tensor_tensor(
                ang[:, 0:HF].rearrange("q (t j) -> q t j", j=npairs),
                S["rt_pos"][:, t0:t0 + 8].rearrange("q (t u) -> q t u", u=1).broadcast_to([128, 8, npairs]),
                S["rt_th"][:, 0:npairs].rearrange("q (u j) -> q u j", u=1).broadcast_to([128, 8, npairs]),
                ALU.mult), reads=["rt_pos", "rt_th"], writes=["big"])
            sin_of(ang[:, 0:HF], sn[:, 0:HF])
            p.emit("vector", lambda e: e.tensor_scalar(
                ang[:, 0:HF], ang[:, 0:HF], S["pih_col"][:], None, ALU.add),
                reads=["big", "pih_col"], writes=["big"])
            sin_of(ang[:, 0:HF], cs[:, 0:HF])
            dh = npairs * 2
            cview = cosT[:, t0 * dh:(t0 + 8) * dh]
            sview = sinT[:, t0 * dh:(t0 + 8) * dh]
            pat = "q (t j two) -> q t j two" if interleaved else "q (t two j) -> q t two j"
            for sub, sgn in ((0, -1.0), (1, 1.0)):
                cdst = (cview.rearrange(pat, two=2, j=npairs)[:, :, :, sub]
                        if interleaved else
                        cview.rearrange(pat, two=2, j=npairs)[:, :, sub, :])
                sdst = (sview.rearrange(pat, two=2, j=npairs)[:, :, :, sub]
                        if interleaved else
                        sview.rearrange(pat, two=2, j=npairs)[:, :, sub, :])
                p.emit("vector", lambda e, cdst=cdst: e.tensor_copy(
                    cdst, cs[:, 0:HF].rearrange("q (t j) -> q t j", j=npairs)),
                    reads=["big"], writes=[cos_name])
                p.emit("vector", lambda e, sdst=sdst, sgn=sgn: e.tensor_single_scalar(
                    sdst, sn[:, 0:HF].rearrange("q (t j) -> q t j", j=npairs), sgn, ALU.mult),
                    reads=["big"], writes=[sin_name])

    if need_l0_rope:
        rope_tables("j64", C_L0TH, 64, "cosT0", "sinT0", interleaved=True)
    if need_l1_rope:
        rope_tables("j32", C_L1TH, 32, "cosT1", "sinT1", interleaved=False)



def _dram_consts(nc, dram):
    hc = host_consts()
    for nm, arr in hc.items():
        dt = {np.dtype(np.float32): F32, np.dtype(np.int32): I32,
              np.dtype(bf16): BF16}[arr.dtype]
        dram[nm] = nc.dram_tensor("hc_" + nm, list(arr.shape), dt,
                                  kind="ExternalInput").ap()


def _alloc_common(S, sb, big):
    sb("cst", (128, NCONST), F32)
    sb("it16", (128, 16), I32); sb("j64", (128, 64), I32); sb("j32", (128, 32), I32)
    sb("ident", (128, 128), BF16)
    sb("ones_row", (1, 128), BF16); sb("negones_row", (1, 128), BF16)
    sb("ones_col", (128, 1), BF16)
    for nm in ("one_col", "pi_col", "pih_col", "negpi_col", "r2pi_col",
               "ln1e4_col", "lnsc_col", "tmp_col", "tmp2_col"):
        sb(nm, (128, 1), F32)
    for i in range(8):
        S["rt%d" % i] = big[:, i * 512:(i + 1) * 512]
    sb("rt_jf", (128, 64), F32); sb("rt_th", (128, 64), F32)
    sb("rt_pos", (128, 16), F32)


_L0_ALIAS = {"xst": "big", "zt0": "big", "zt1": "big", "sqb": "attn_h",
             "rowf": "rowf", "rowb": "rowb"}


def build_l0_attn():
    nc = bass.Bass("TRN2", target_bir_lowering=False)
    es = contextlib.ExitStack()
    dram = {}
    dram["x"] = nc.dram_tensor("x", [T, D], F32, kind="ExternalInput").ap()
    for nm in ("Wq", "Wk", "Wv", "Wo"):
        dram[nm] = nc.dram_tensor(nm, [D, D], F32, kind="ExternalInput").ap()
    for nm in ("bq", "bk", "bv", "bo"):
        dram[nm] = nc.dram_tensor(nm, [1, D], F32, kind="ExternalInput").ap()
    _dram_consts(nc, dram)
    dram["y"] = nc.dram_tensor("y", [T, D], F32, kind="ExternalOutput").ap()
    dram["attn_scr"] = nc.dram_tensor("attn_scr", [D, T], BF16, kind="Internal").ap()

    S = {"_dram": dram}

    def sb(name, shape, dt):
        S[name] = es.enter_context(nc.sbuf_tensor("s_" + name, list(shape), dt))
        return S[name]

    bigh = es.enter_context(nc.sbuf_tensor("s_big", [128, 4096], F32))
    big = bigh[:]
    _alloc_common(S, sb, big)
    sb("invsqrt128_col", (128, 1), F32)
    sb("masks", (128, 4 * 512), BF16)
    sb("cosT0", (128, NT * 128), BF16); sb("sinT0", (128, NT * 128), BF16)
    sb("Xf", (128, ND * T), BF16)
    sb("xb_row", (128, D), BF16)
    sb("wst", (128, 512), F32)
    sb("wslab", (128, ND * 512), BF16)
    sb("rowf", (1, D), F32)
    sb("rowb", (1, D), BF16)
    sb("qsc", (128, 512), BF16); sb("rot", (128, 512), BF16)
    sb("tmp512", (128, 512), BF16); sb("ropebuf", (128, 512), BF16)
    sb("QTf", (128, 4 * T), BF16); sb("KTf", (128, 4 * T), BF16)
    sb("Vb", (128, NT * 512), BF16)
    sb("kmax", (1, 1), F32)
    sb("PT", (128, 2 * 512), BF16)
    sb("rs_row", (1, 512), F32); sb("rsb", (1, 512), BF16)
    sb("sigb", (128, 512), BF16)
    sb("attn_h", (128, T), BF16)
    S["sqb"] = S["attn_h"]
    sb("xres", (128, D), F32)

    psA = es.enter_context(nc.psum_tensor("psA", [128, 512], F32))
    psB = es.enter_context(nc.psum_tensor("psB", [128, 512], F32))
    psC = es.enter_context(nc.psum_tensor("psC", [128, 512], F32))
    psAV = es.enter_context(nc.psum_tensor("psAV", [128, 512], F32))
    psSIG = es.enter_context(nc.psum_tensor("psSIG", [1, 512], F32))
    psT = es.enter_context(nc.psum_tensor("psT", [128, 128], BF16))

    with es:
        p = Prog(nc, es)
        p.alias = dict(_L0_ALIAS)
        emit_prelude(p, S, need_l0_rope=True)
        cst = S["cst"]
        xst = big[:, 0:2048]
        zt = [big[:, 0:2048], big[:, 2048:4096]]
        invd = S["invsqrt128_col"]

        def load_bias_row(nm):
            p.emit("sync", lambda e: e.dma_start(out=S["rowf"][:], in_=dram[nm]),
                   writes=["rowf"], dma=True)
            p.emit("vector", lambda e: e.tensor_copy(S["rowb"][:], S["rowf"][:]),
                   reads=["rowf"], writes=["rowb"])

        Xf = S["Xf"]
        for tt in range(NT):
            p.emit("sync", lambda e, tt=tt: e.dma_start(
                out=xst, in_=dram["x"][tt * 128:(tt + 1) * 128, :]),
                writes=["xst"], dma=True)
            p.emit("vector", lambda e: e.tensor_copy(S["xb_row"][:], xst),
                   reads=["xst"], writes=["xb_row"])
            for dc in range(ND):
                p.emit("tensor", lambda e, dc=dc: e.transpose(
                    psT[:], S["xb_row"][:, dc * 128:(dc + 1) * 128], S["ident"][:]),
                    reads=["xb_row", "ident"], writes=["psT"])
                p.emit("vector", lambda e, dc=dc, tt=tt: e.tensor_copy(
                    Xf[:, dc * T + tt * 128: dc * T + (tt + 1) * 128], psT[:]),
                    reads=["psT"], writes=["Xf"])

        def load_w_slab(wname, col0):
            for dc in range(ND):
                p.emit("sync", lambda e, wname=wname, col0=col0, dc=dc: e.dma_start(
                    out=S["wst"][:],
                    in_=dram[wname][dc * 128:(dc + 1) * 128, col0:col0 + 512]),
                    writes=["wst"], dma=True)
                p.emit("vector", lambda e, dc=dc: e.tensor_copy(
                    S["wslab"][:, dc * 512:(dc + 1) * 512], S["wst"][:]),
                    reads=["wst"], writes=["wslab"])

        for hg in range(4):
            fg0 = hg * 512
            for wname, bnm, ps, pskey, kind in (
                    ("Wq", "bq", psA, "psA", "q"),
                    ("Wk", "bk", psB, "psB", "k"),
                    ("Wv", "bv", psC, "psC", "v")):
                load_w_slab(wname, fg0)
                load_bias_row(bnm)
                for tt in range(NT):
                    for dc in range(ND):
                        p.emit("tensor", lambda e, ps=ps, dc=dc, tt=tt: e.matmul(
                            ps[:], Xf[:, dc * T + tt * 128: dc * T + (tt + 1) * 128],
                            S["wslab"][:, dc * 512:(dc + 1) * 512],
                            start=(dc == 0), stop=False),
                            reads=["Xf", "wslab"], writes=[pskey])
                    p.emit("tensor", lambda e, ps=ps, fg0=fg0: e.matmul(
                        ps[:], S["ones_row"][0:1, :], S["rowb"][0:1, fg0:fg0 + 512],
                        start=False, stop=True),
                        reads=["ones_row", "rowb"], writes=[pskey])
                    if kind == "v":
                        p.emit("scalar", lambda e, ps=ps, tt=tt: e.activation(
                            S["Vb"][:, tt * 512:(tt + 1) * 512], ps[:], AF.Copy),
                            reads=[pskey], writes=["Vb"])
                    else:
                        p.emit("scalar", lambda e, ps=ps: e.activation(
                            S["qsc"][:], ps[:], AF.Copy),
                            reads=[pskey], writes=["qsc"])
                        ct = S["cosT0"][:, tt * 128:(tt + 1) * 128]
                        st = S["sinT0"][:, tt * 128:(tt + 1) * 128]
                        q4 = S["qsc"][:].rearrange("p (h j two) -> p h j two", two=2, j=64)
                        r4 = S["rot"][:].rearrange("p (h j two) -> p h j two", two=2, j=64)
                        st4 = st.rearrange("q (u d) -> q u d", u=1).broadcast_to([128, 4, 128]) \
                                .rearrange("q h (j two) -> q h j two", two=2)
                        ct4 = ct.rearrange("q (u d) -> q u d", u=1).broadcast_to([128, 4, 128]) \
                                .rearrange("q h (j two) -> q h j two", two=2)
                        p.emit("vector", lambda e, q4=q4, r4=r4, st4=st4: e.tensor_tensor(
                            r4[:, :, :, 0], q4[:, :, :, 1], st4[:, :, :, 0], ALU.mult),
                            reads=["qsc", "sinT0"], writes=["rot"])
                        p.emit("vector", lambda e, q4=q4, r4=r4, st4=st4: e.tensor_tensor(
                            r4[:, :, :, 1], q4[:, :, :, 0], st4[:, :, :, 1], ALU.mult),
                            reads=["qsc", "sinT0"], writes=["rot"])
                        p.emit("vector", lambda e, q4=q4, ct4=ct4: e.tensor_tensor(
                            S["tmp512"][:].rearrange("p (h j two) -> p h j two", two=2, j=64),
                            q4, ct4, ALU.mult),
                            reads=["qsc", "cosT0"], writes=["tmp512"])
                        p.emit("vector", lambda e: e.tensor_tensor(
                            S["ropebuf"][:], S["tmp512"][:], S["rot"][:], ALU.add),
                            reads=["tmp512", "rot"], writes=["ropebuf"])
                        dstf, dkey = (S["QTf"], "QTf") if kind == "q" else (S["KTf"], "KTf")
                        for hl in range(4):
                            p.emit("tensor", lambda e, hl=hl: e.transpose(
                                psT[:], S["ropebuf"][:, hl * 128:(hl + 1) * 128], S["ident"][:]),
                                reads=["ropebuf", "ident"], writes=["psT"])
                            p.emit("vector", lambda e, dstf=dstf, hl=hl, tt=tt: e.tensor_copy(
                                dstf[:, hl * T + tt * 128: hl * T + (tt + 1) * 128], psT[:]),
                                reads=["psT"], writes=[dkey])
            for hl in range(4):
                QTh = S["QTf"][:, hl * T:(hl + 1) * T]
                KTh = S["KTf"][:, hl * T:(hl + 1) * T]
                p.emit("scalar", lambda e, KTh=KTh: e.activation(S["sqb"][:], KTh, AF.Square),
                       reads=["KTf"], writes=["sqb"])
                for qt in range(4):
                    p.emit("tensor", lambda e, qt=qt: e.matmul(
                        psSIG[0:1, :], S["ones_col"][:, 0:1], S["sqb"][:, qt * 512:(qt + 1) * 512],
                        start=True, stop=True),
                        reads=["ones_col", "sqb"], writes=["psSIG"])
                    p.emit("vector", lambda e, qt=qt: e.tensor_copy(
                        S["rowf"][0:1, qt * 512:(qt + 1) * 512], psSIG[0:1, :]),
                        reads=["psSIG"], writes=["rowf"])
                p.emit("scalar", lambda e: e.activation(S["rowf"][0:1, :], S["rowf"][0:1, :], AF.Sqrt),
                       reads=["rowf"], writes=["rowf"])
                p.emit("vector", lambda e: e.reduce_max(S["kmax"][0:1, 0:1], S["rowf"][0:1, :], AX.X),
                       reads=["rowf"], writes=["kmax"])
                p.emit("scalar", lambda e, QTh=QTh: e.activation(S["sqb"][:], QTh, AF.Square),
                       reads=["QTf"], writes=["sqb"])
                for qt in range(4):
                    p.emit("tensor", lambda e, qt=qt: e.matmul(
                        psSIG[0:1, :], S["ones_col"][:, 0:1], S["sqb"][:, qt * 512:(qt + 1) * 512],
                        start=True, stop=True),
                        reads=["ones_col", "sqb"], writes=["psSIG"])
                    p.emit("vector", lambda e, qt=qt: e.tensor_copy(
                        S["rowf"][0:1, qt * 512:(qt + 1) * 512], psSIG[0:1, :]),
                        reads=["psSIG"], writes=["rowf"])
                p.emit("scalar", lambda e: e.activation(S["rowf"][0:1, :], S["rowf"][0:1, :], AF.Sqrt),
                       reads=["rowf"], writes=["rowf"])
                p.emit("vector", lambda e: e.tensor_scalar(
                    S["rowf"][0:1, :], S["rowf"][0:1, :], S["kmax"][0:1, 0:1], None, ALU.mult),
                    reads=["rowf", "kmax"], writes=["rowf"])
                p.emit("vector", lambda e: e.tensor_copy(S["rowb"][0:1, :], S["rowf"][0:1, :]),
                       reads=["rowf"], writes=["rowb"])
                for qt in range(4):
                    ktmax = 4 * (qt + 1)
                    for kt in range(ktmax):
                        ps, pskey = (psA, "psA") if kt % 2 == 0 else (psB, "psB")
                        slot = kt % 2
                        ptk = "PT%d" % slot
                        pts = S["PT"][:, slot * 512:(slot + 1) * 512]
                        p.emit("tensor", lambda e, ps=ps, KTh=KTh, QTh=QTh, kt=kt, qt=qt: e.matmul(
                            ps[:], KTh[:, kt * 128:(kt + 1) * 128],
                            QTh[:, qt * 512:(qt + 1) * 512],
                            start=True, stop=False),
                            reads=["KTf", "QTf"], writes=[pskey])
                        p.emit("tensor", lambda e, ps=ps, qt=qt: e.matmul(
                            ps[:], S["negones_row"][0:1, :], S["rowb"][0:1, qt * 512:(qt + 1) * 512],
                            start=False, stop=True),
                            reads=["negones_row", "rowb"], writes=[pskey])
                        p.emit("scalar", lambda e, ps=ps, pts=pts: e.activation(
                            pts, ps[:], AF.Exp, bias=0.0, scale=invd[:]),
                            reads=[pskey, "invsqrt128_col"], writes=[ptk])
                        if kt >= 4 * qt:
                            j = kt - 4 * qt
                            p.emit("vector", lambda e, pts=pts, j=j: e.tensor_tensor(
                                pts, pts, S["masks"][:, j * 512:(j + 1) * 512], ALU.mult),
                                reads=[ptk, "masks"], writes=[ptk])
                        p.emit("tensor", lambda e, pts=pts, first=(kt == 0), last=(kt == ktmax - 1): e.matmul(
                            psSIG[0:1, :], S["ones_col"][:, 0:1], pts,
                            start=first, stop=last),
                            reads=["ones_col", ptk], writes=["psSIG"])
                        p.emit("tensor", lambda e, pts=pts, kt=kt, hl=hl, first=(kt == 0), last=(kt == ktmax - 1): e.matmul(
                            psAV[:], S["Vb"][:, kt * 512 + hl * 128: kt * 512 + (hl + 1) * 128],
                            pts, start=first, stop=last),
                            reads=["Vb", ptk], writes=["psAV"])
                    p.emit("vector", lambda e: e.reciprocal(S["rs_row"][0:1, :], psSIG[0:1, :]),
                           reads=["psSIG"], writes=["rs_row"])
                    p.emit("vector", lambda e: e.tensor_copy(S["rsb"][0:1, :], S["rs_row"][0:1, :]),
                           reads=["rs_row"], writes=["rsb"])
                    p.emit("tensor", lambda e: e.matmul(
                        psC[:], S["ones_row"][0:1, :], S["rsb"][0:1, :], start=True, stop=True),
                        reads=["ones_row", "rsb"], writes=["psC"])
                    p.emit("vector", lambda e: e.tensor_copy(S["sigb"][:], psC[:]),
                           reads=["psC"], writes=["sigb"])
                    p.emit("vector", lambda e, qt=qt: e.tensor_tensor(
                        S["attn_h"][:, qt * 512:(qt + 1) * 512], psAV[:], S["sigb"][:], ALU.mult),
                        reads=["psAV", "sigb"], writes=["attn_h"])
                h = hg * 4 + hl
                p.emit("sync", lambda e, h=h: e.dma_start(
                    out=dram["attn_scr"][h * 128:(h + 1) * 128, :], in_=S["attn_h"][:]),
                    reads=["attn_h"], writes=["attn_scr"], dma=True)

        load_bias_row("bo")
        for tg in range(8):
            p.emit("sync", lambda e, tg=tg: e.dma_start(
                out=S["Vb"][:, 0:ND * 256].rearrange("p (c q) -> p c q", q=256),
                in_=dram["attn_scr"].rearrange("(c p) q -> p c q", p=128)[:, :, tg * 256:(tg + 1) * 256]),
                reads=["attn_scr"], writes=["Vb"], dma=True)
            for fg in range(4):
                load_w_slab("Wo", fg * 512)
                for ti in range(2):
                    ps, pskey = (psA, "psA") if ti % 2 == 0 else (psB, "psB")
                    for dc in range(ND):
                        p.emit("tensor", lambda e, ps=ps, dc=dc, ti=ti: e.matmul(
                            ps[:], S["Vb"][:, dc * 256 + ti * 128: dc * 256 + (ti + 1) * 128],
                            S["wslab"][:, dc * 512:(dc + 1) * 512],
                            start=(dc == 0), stop=False),
                            reads=["Vb", "wslab"], writes=[pskey])
                    p.emit("tensor", lambda e, ps=ps, fg=fg: e.matmul(
                        ps[:], S["ones_row"][0:1, :], S["rowb"][0:1, fg * 512:(fg + 1) * 512],
                        start=False, stop=True),
                        reads=["ones_row", "rowb"], writes=[pskey])
                    p.emit("vector", lambda e, ps=ps, ti=ti, fg=fg: e.tensor_copy(
                        zt[ti][:, fg * 512:(fg + 1) * 512], ps[:]),
                        reads=[pskey], writes=["zt%d" % ti])
            for ti in range(2):
                tt = tg * 2 + ti
                zti = zt[ti]
                zkey = "zt%d" % ti
                p.emit("sync", lambda e, tt=tt: e.dma_start(
                    out=S["xres"][:], in_=dram["x"][tt * 128:(tt + 1) * 128, :]),
                    writes=["xres"], dma=True)
                p.emit("vector", lambda e, zti=zti: e.tensor_tensor(
                    zti, zti, S["xres"][:], ALU.add),
                    reads=[zkey, "xres"], writes=[zkey])
                p.emit("vector", lambda e, zti=zti: e.reduce_sum(S["tmp_col"][:], zti, AX.X),
                       reads=[zkey], writes=["tmp_col"])
                p.emit("scalar", lambda e, zti=zti: e.activation(
                    S["xb_row"][:], zti, AF.Square, accum_out=S["tmp2_col"][:]),
                    reads=[zkey], writes=["xb_row", "tmp2_col"])
                p.emit("vector", lambda e: e.tensor_single_scalar(
                    S["tmp_col"][:], S["tmp_col"][:], 1.0 / D, ALU.mult),
                    reads=["tmp_col"], writes=["tmp_col"])
                p.emit("scalar", lambda e: e.activation(S["one_col"][:], S["tmp_col"][:], AF.Square),
                       reads=["tmp_col"], writes=["one_col"])
                p.emit("vector", lambda e: e.scalar_tensor_tensor(
                    S["tmp2_col"][:], S["tmp2_col"][:], 1.0 / D, S["one_col"][:],
                    ALU.mult, ALU.subtract),
                    reads=["tmp2_col", "one_col"], writes=["tmp2_col"])
                p.emit("scalar", lambda e: e.activation(
                    S["tmp2_col"][:], S["tmp2_col"][:], AF.Ln, bias=cst[:, C_EPS:C_EPS + 1], scale=1.0),
                    reads=["tmp2_col", "cst"], writes=["tmp2_col"])
                p.emit("scalar", lambda e: e.activation(
                    S["tmp2_col"][:], S["tmp2_col"][:], AF.Exp, bias=0.0,
                    scale=cst[:, C_NEGHALF:C_NEGHALF + 1]),
                    reads=["tmp2_col", "cst"], writes=["tmp2_col"])
                p.emit("vector", lambda e, zti=zti: e.tensor_scalar(
                    zti, zti, S["tmp_col"][:], S["tmp2_col"][:], ALU.subtract, ALU.mult),
                    reads=[zkey, "tmp_col", "tmp2_col"], writes=[zkey])
                p.emit("sync", lambda e, zti=zti, tt=tt: e.dma_start(
                    out=dram["y"][tt * 128:(tt + 1) * 128, :], in_=zti),
                    reads=[zkey], writes=["y_out"], dma=True)

        with nc.Block() as block:
            p.finalize(block)
    return nc


def _hc_maps():
    hc = host_consts()
    return {("hc_" + k): v for k, v in hc.items()}


def _in_maps_l0attn(x_core, inputs, hc):
    m = dict(x=np.ascontiguousarray(x_core),
             Wq=inputs["L0_attn_Wq"], Wk=inputs["L0_attn_Wk"],
             Wv=inputs["L0_attn_Wv"], Wo=inputs["L0_attn_Wo"],
             bq=inputs["L0_attn_bq"].reshape(1, D),
             bk=inputs["L0_attn_bk"].reshape(1, D),
             bv=inputs["L0_attn_bv"].reshape(1, D),
             bo=inputs["L0_attn_bo"].reshape(1, D))
    m.update(hc)
    return m



def build_l1_attn():
    nc = bass.Bass("TRN2", target_bir_lowering=False)
    es = contextlib.ExitStack()
    dram = {}
    dram["x"] = nc.dram_tensor("x", [T, D], F32, kind="ExternalInput").ap()
    dram["Wq"] = nc.dram_tensor("Wq", [D, D], F32, kind="ExternalInput").ap()
    dram["Wk"] = nc.dram_tensor("Wk", [D, 512], F32, kind="ExternalInput").ap()
    dram["Wv"] = nc.dram_tensor("Wv", [D, 512], F32, kind="ExternalInput").ap()
    dram["Wo"] = nc.dram_tensor("Wo", [D, D], F32, kind="ExternalInput").ap()
    dram["bo"] = nc.dram_tensor("bo", [1, D], F32, kind="ExternalInput").ap()
    _dram_consts(nc, dram)
    dram["y"] = nc.dram_tensor("y", [T, D], F32, kind="ExternalOutput").ap()
    dram["attn_scr"] = nc.dram_tensor("attn_scr", [D, T], BF16, kind="Internal").ap()

    S = {"_dram": dram}

    def sb(name, shape, dt):
        S[name] = es.enter_context(nc.sbuf_tensor("s_" + name, list(shape), dt))
        return S[name]

    bigh = es.enter_context(nc.sbuf_tensor("s_big", [128, 4096], F32))
    big = bigh[:]
    _alloc_common(S, sb, big)
    sb("invsqrt64_col", (128, 1), F32)
    sb("masks", (128, 4 * 512), BF16)
    sb("cosT1", (128, NT * 64), BF16); sb("sinT1", (128, NT * 64), BF16)
    sb("Xf", (128, ND * T), BF16)
    sb("xb_row", (128, D), BF16)
    sb("wst", (128, 512), F32)
    sb("wslab", (128, ND * 512), BF16)
    sb("rowf", (1, D), F32)
    sb("rowb", (1, D), BF16)
    sb("qsc", (128, 512), BF16); sb("rot", (128, 512), BF16)
    sb("tmp512", (128, 512), BF16); sb("ropebuf", (128, 512), BF16)
    sb("QTf", (128, 4 * T), BF16)
    sb("KTf", (128, 4 * T), BF16)
    sb("Vb", (128, NT * 512), BF16)
    sb("kmax", (1, 1), F32)
    sb("PT", (128, 2 * 512), BF16)
    sb("rs_row", (1, 512), F32); sb("rsb", (1, 512), BF16)
    sb("sigb", (128, 512), BF16)
    sb("attn_h", (128, T), BF16)
    S["sqb"] = S["attn_h"]
    sb("xres", (128, D), F32)

    psA = es.enter_context(nc.psum_tensor("psA", [128, 512], F32))
    psB = es.enter_context(nc.psum_tensor("psB", [128, 512], F32))
    psC = es.enter_context(nc.psum_tensor("psC", [128, 512], F32))
    psAV = es.enter_context(nc.psum_tensor("psAV", [128, 512], F32))
    psSIG = es.enter_context(nc.psum_tensor("psSIG", [1, 512], F32))
    psT = es.enter_context(nc.psum_tensor("psT", [128, 128], BF16))

    with es:
        p = Prog(nc, es)
        p.alias = dict(_L0_ALIAS)
        emit_prelude(p, S, need_l1_rope=True)
        cst = S["cst"]
        xst = big[:, 0:2048]
        zt = [big[:, 0:2048], big[:, 2048:4096]]
        invd = S["invsqrt64_col"]

        Xf = S["Xf"]
        for tt in range(NT):
            p.emit("sync", lambda e, tt=tt: e.dma_start(
                out=xst, in_=dram["x"][tt * 128:(tt + 1) * 128, :]),
                writes=["xst"], dma=True)
            p.emit("vector", lambda e: e.tensor_copy(S["xb_row"][:], xst),
                   reads=["xst"], writes=["xb_row"])
            for dc in range(ND):
                p.emit("tensor", lambda e, dc=dc: e.transpose(
                    psT[:], S["xb_row"][:, dc * 128:(dc + 1) * 128], S["ident"][:]),
                    reads=["xb_row", "ident"], writes=["psT"])
                p.emit("vector", lambda e, dc=dc, tt=tt: e.tensor_copy(
                    Xf[:, dc * T + tt * 128: dc * T + (tt + 1) * 128], psT[:]),
                    reads=["psT"], writes=["Xf"])

        def load_w_slab(wname, col0):
            for dc in range(ND):
                p.emit("sync", lambda e, wname=wname, col0=col0, dc=dc: e.dma_start(
                    out=S["wst"][:],
                    in_=dram[wname][dc * 128:(dc + 1) * 128, col0:col0 + 512]),
                    writes=["wst"], dma=True)
                p.emit("vector", lambda e, dc=dc: e.tensor_copy(
                    S["wslab"][:, dc * 512:(dc + 1) * 512], S["wst"][:]),
                    reads=["wst"], writes=["wslab"])

        def project_group(wname, ps, pskey, dstf, dkey, col0, do_rope, to_vb=False, is_q=False):
            load_w_slab(wname, col0)
            for tt in range(NT):
                for dc in range(ND):
                    p.emit("tensor", lambda e, ps=ps, dc=dc, tt=tt: e.matmul(
                        ps[:], Xf[:, dc * T + tt * 128: dc * T + (tt + 1) * 128],
                        S["wslab"][:, dc * 512:(dc + 1) * 512],
                        start=(dc == 0), stop=(dc == ND - 1)),
                        reads=["Xf", "wslab"], writes=[pskey])
                if to_vb:
                    p.emit("scalar", lambda e, ps=ps, tt=tt: e.activation(
                        S["Vb"][:, tt * 512:(tt + 1) * 512], ps[:], AF.Copy),
                        reads=[pskey], writes=["Vb"])
                    continue
                p.emit("scalar", lambda e, ps=ps: e.activation(S["qsc"][:], ps[:], AF.Copy),
                       reads=[pskey], writes=["qsc"])
                ct = S["cosT1"][:, tt * 64:(tt + 1) * 64]
                st = S["sinT1"][:, tt * 64:(tt + 1) * 64]
                q4 = S["qsc"][:].rearrange("p (h two j) -> p h two j", two=2, j=32)
                r4 = S["rot"][:].rearrange("p (h two j) -> p h two j", two=2, j=32)
                ct4 = ct.rearrange("q (u d) -> q u d", u=1).broadcast_to([128, 8, 64]) \
                        .rearrange("q h (two j) -> q h two j", two=2)
                st4 = st.rearrange("q (u d) -> q u d", u=1).broadcast_to([128, 8, 64]) \
                        .rearrange("q h (two j) -> q h two j", two=2)
                p.emit("vector", lambda e, q4=q4, r4=r4, st4=st4: e.tensor_tensor(
                    r4[:, :, 0, :], q4[:, :, 1, :], st4[:, :, 0, :], ALU.mult),
                    reads=["qsc", "sinT1"], writes=["rot"])
                p.emit("vector", lambda e, q4=q4, r4=r4, st4=st4: e.tensor_tensor(
                    r4[:, :, 1, :], q4[:, :, 0, :], st4[:, :, 1, :], ALU.mult),
                    reads=["qsc", "sinT1"], writes=["rot"])
                p.emit("vector", lambda e, q4=q4, ct4=ct4: e.tensor_tensor(
                    S["tmp512"][:].rearrange("p (h two j) -> p h two j", two=2, j=32),
                    q4, ct4, ALU.mult),
                    reads=["qsc", "cosT1"], writes=["tmp512"])
                p.emit("vector", lambda e: e.tensor_tensor(
                    S["ropebuf"][:], S["tmp512"][:], S["rot"][:], ALU.add),
                    reads=["tmp512", "rot"], writes=["ropebuf"])
                for hl in range(8):
                    if is_q:
                        pbase, cblk = (hl // 4) * 64, hl % 4
                    else:
                        pbase, cblk = (hl % 2) * 64, hl // 2
                    p.emit("tensor", lambda e, hl=hl: e.transpose(
                        psT[0:64, :], S["ropebuf"][:, hl * 64:(hl + 1) * 64], S["ident"][:]),
                        reads=["ropebuf", "ident"], writes=["psT"])
                    p.emit("vector", lambda e, dstf=dstf, pbase=pbase, cblk=cblk, tt=tt: e.tensor_copy(
                        dstf[pbase:pbase + 64, cblk * T + tt * 128: cblk * T + (tt + 1) * 128],
                        psT[0:64, :]),
                        reads=["psT"], writes=[dkey])

        project_group("Wk", psB, "psB", S["KTf"], "KTf", 0, True)
        project_group("Wv", psC, "psC", None, None, 0, False, to_vb=True)

        load_bias = None
        for qg in range(4):
            project_group("Wq", psA, "psA", S["QTf"], "QTf", qg * 512, True, is_q=True)
            for hl in range(8):
                h = qg * 8 + hl
                kv = h // 4
                qpb = (hl // 4) * 64
                qcb = hl % 4
                kpb = (kv % 2) * 64
                kcb = kv // 2
                assert qpb == kpb, (h, qpb, kpb)
                QTh = S["QTf"][qpb:qpb + 64, qcb * T:(qcb + 1) * T]
                KTh = S["KTf"][kpb:kpb + 64, kcb * T:(kcb + 1) * T]
                onescol64 = S["ones_col"][qpb:qpb + 64, 0:1]
                konescol = S["ones_col"][kpb:kpb + 64, 0:1]
                p.emit("scalar", lambda e, KTh=KTh, kpb=kpb: e.activation(
                    S["sqb"][kpb:kpb + 64, :], KTh, AF.Square),
                    reads=["KTf"], writes=["attn_h"])
                for qt in range(4):
                    p.emit("tensor", lambda e, qt=qt, kpb=kpb, konescol=konescol: e.matmul(
                        psSIG[0:1, :], konescol, S["sqb"][kpb:kpb + 64, qt * 512:(qt + 1) * 512],
                        start=True, stop=True),
                        reads=["ones_col", "attn_h"], writes=["psSIG"])
                    p.emit("vector", lambda e, qt=qt: e.tensor_copy(
                        S["rowf"][0:1, qt * 512:(qt + 1) * 512], psSIG[0:1, :]),
                        reads=["psSIG"], writes=["rowf"])
                p.emit("scalar", lambda e: e.activation(S["rowf"][0:1, :], S["rowf"][0:1, :], AF.Sqrt),
                       reads=["rowf"], writes=["rowf"])
                p.emit("vector", lambda e: e.reduce_max(S["kmax"][0:1, 0:1], S["rowf"][0:1, :], AX.X),
                       reads=["rowf"], writes=["kmax"])
                p.emit("scalar", lambda e, QTh=QTh, qpb=qpb: e.activation(
                    S["sqb"][qpb:qpb + 64, :], QTh, AF.Square),
                    reads=["QTf"], writes=["attn_h"])
                for qt in range(4):
                    p.emit("tensor", lambda e, qt=qt, qpb=qpb, onescol64=onescol64: e.matmul(
                        psSIG[0:1, :], onescol64, S["sqb"][qpb:qpb + 64, qt * 512:(qt + 1) * 512],
                        start=True, stop=True),
                        reads=["ones_col", "attn_h"], writes=["psSIG"])
                    p.emit("vector", lambda e, qt=qt: e.tensor_copy(
                        S["rowf"][0:1, qt * 512:(qt + 1) * 512], psSIG[0:1, :]),
                        reads=["psSIG"], writes=["rowf"])
                p.emit("scalar", lambda e: e.activation(S["rowf"][0:1, :], S["rowf"][0:1, :], AF.Sqrt),
                       reads=["rowf"], writes=["rowf"])
                p.emit("vector", lambda e: e.tensor_scalar(
                    S["rowf"][0:1, :], S["rowf"][0:1, :], S["kmax"][0:1, 0:1], None, ALU.mult),
                    reads=["rowf", "kmax"], writes=["rowf"])
                p.emit("vector", lambda e: e.tensor_copy(S["rowb"][0:1, :], S["rowf"][0:1, :]),
                       reads=["rowf"], writes=["rowb"])
                for qt in range(4):
                    ktmax = 4 * (qt + 1)
                    for kt in range(ktmax):
                        ps, pskey = (psA, "psA") if kt % 2 == 0 else (psB, "psB")
                        slot = kt % 2
                        ptk = "PT%d" % slot
                        pts = S["PT"][:, slot * 512:(slot + 1) * 512]
                        p.emit("tensor", lambda e, ps=ps, KTh=KTh, QTh=QTh, kt=kt, qt=qt: e.matmul(
                            ps[:], KTh[:, kt * 128:(kt + 1) * 128],
                            QTh[:, qt * 512:(qt + 1) * 512],
                            start=True, stop=False),
                            reads=["KTf", "QTf"], writes=[pskey])
                        p.emit("tensor", lambda e, ps=ps, qt=qt: e.matmul(
                            ps[:], S["negones_row"][0:1, :], S["rowb"][0:1, qt * 512:(qt + 1) * 512],
                            start=False, stop=True),
                            reads=["negones_row", "rowb"], writes=[pskey])
                        p.emit("scalar", lambda e, ps=ps, pts=pts: e.activation(
                            pts, ps[:], AF.Exp, bias=0.0, scale=invd[:]),
                            reads=[pskey, "invsqrt64_col"], writes=[ptk])
                        if kt >= 4 * qt:
                            j = kt - 4 * qt
                            p.emit("vector", lambda e, pts=pts, j=j: e.tensor_tensor(
                                pts, pts, S["masks"][:, j * 512:(j + 1) * 512], ALU.mult),
                                reads=[ptk, "masks"], writes=[ptk])
                        p.emit("tensor", lambda e, pts=pts, first=(kt == 0), last=(kt == ktmax - 1): e.matmul(
                            psSIG[0:1, :], S["ones_col"][:, 0:1], pts,
                            start=first, stop=last),
                            reads=["ones_col", ptk], writes=["psSIG"])
                        p.emit("tensor", lambda e, pts=pts, kt=kt, kv=kv, first=(kt == 0), last=(kt == ktmax - 1): e.matmul(
                            psAV[0:64, :], S["Vb"][:, kt * 512 + kv * 64: kt * 512 + (kv + 1) * 64],
                            pts, start=first, stop=last),
                            reads=["Vb", ptk], writes=["psAV"])
                    p.emit("vector", lambda e: e.reciprocal(S["rs_row"][0:1, :], psSIG[0:1, :]),
                           reads=["psSIG"], writes=["rs_row"])
                    p.emit("vector", lambda e: e.tensor_copy(S["rsb"][0:1, :], S["rs_row"][0:1, :]),
                           reads=["rs_row"], writes=["rsb"])
                    p.emit("tensor", lambda e: e.matmul(
                        psC[0:64, :], S["ones_row"][0:1, 0:64], S["rsb"][0:1, :], start=True, stop=True),
                        reads=["ones_row", "rsb"], writes=["psC"])
                    p.emit("vector", lambda e: e.tensor_copy(S["sigb"][0:64, :], psC[0:64, :]),
                           reads=["psC"], writes=["sigb"])
                    p.emit("vector", lambda e, qt=qt: e.tensor_tensor(
                        S["attn_h"][0:64, qt * 512:(qt + 1) * 512], psAV[0:64, :],
                        S["sigb"][0:64, :], ALU.mult),
                        reads=["psAV", "sigb"], writes=["attn_h"])
                p.emit("sync", lambda e, h=h: e.dma_start(
                    out=dram["attn_scr"][h * 64:(h + 1) * 64, :], in_=S["attn_h"][0:64, :]),
                    reads=["attn_h"], writes=["attn_scr"], dma=True)

        p.emit("sync", lambda e: e.dma_start(out=S["rowf"][:], in_=dram["bo"]),
               writes=["rowf"], dma=True)
        p.emit("vector", lambda e: e.tensor_copy(S["rowb"][:], S["rowf"][:]),
               reads=["rowf"], writes=["rowb"])
        for tg in range(8):
            p.emit("sync", lambda e, tg=tg: e.dma_start(
                out=S["Vb"][:, 0:ND * 256].rearrange("p (c q) -> p c q", q=256),
                in_=dram["attn_scr"].rearrange("(c p) q -> p c q", p=128)[:, :, tg * 256:(tg + 1) * 256]),
                reads=["attn_scr"], writes=["Vb"], dma=True)
            for fg in range(4):
                load_w_slab("Wo", fg * 512)
                for ti in range(2):
                    ps, pskey = (psA, "psA") if ti % 2 == 0 else (psB, "psB")
                    for dc in range(ND):
                        p.emit("tensor", lambda e, ps=ps, dc=dc, ti=ti: e.matmul(
                            ps[:], S["Vb"][:, dc * 256 + ti * 128: dc * 256 + (ti + 1) * 128],
                            S["wslab"][:, dc * 512:(dc + 1) * 512],
                            start=(dc == 0), stop=False),
                            reads=["Vb", "wslab"], writes=[pskey])
                    p.emit("tensor", lambda e, ps=ps, fg=fg: e.matmul(
                        ps[:], S["ones_row"][0:1, :], S["rowb"][0:1, fg * 512:(fg + 1) * 512],
                        start=False, stop=True),
                        reads=["ones_row", "rowb"], writes=[pskey])
                    p.emit("vector", lambda e, ps=ps, ti=ti, fg=fg: e.tensor_copy(
                        zt[ti][:, fg * 512:(fg + 1) * 512], ps[:]),
                        reads=[pskey], writes=["zt%d" % ti])
            for ti in range(2):
                tt = tg * 2 + ti
                zti = zt[ti]
                zkey = "zt%d" % ti
                p.emit("sync", lambda e, tt=tt: e.dma_start(
                    out=S["xres"][:], in_=dram["x"][tt * 128:(tt + 1) * 128, :]),
                    writes=["xres"], dma=True)
                p.emit("vector", lambda e, zti=zti: e.tensor_tensor(
                    zti, zti, S["xres"][:], ALU.add),
                    reads=[zkey, "xres"], writes=[zkey])
                p.emit("vector", lambda e, zti=zti: e.reduce_sum(S["tmp_col"][:], zti, AX.X),
                       reads=[zkey], writes=["tmp_col"])
                p.emit("scalar", lambda e, zti=zti: e.activation(
                    S["xb_row"][:], zti, AF.Square, accum_out=S["tmp2_col"][:]),
                    reads=[zkey], writes=["xb_row", "tmp2_col"])
                p.emit("vector", lambda e: e.tensor_single_scalar(
                    S["tmp_col"][:], S["tmp_col"][:], 1.0 / D, ALU.mult),
                    reads=["tmp_col"], writes=["tmp_col"])
                p.emit("scalar", lambda e: e.activation(S["one_col"][:], S["tmp_col"][:], AF.Square),
                       reads=["tmp_col"], writes=["one_col"])
                p.emit("vector", lambda e: e.scalar_tensor_tensor(
                    S["tmp2_col"][:], S["tmp2_col"][:], 1.0 / D, S["one_col"][:],
                    ALU.mult, ALU.subtract),
                    reads=["tmp2_col", "one_col"], writes=["tmp2_col"])
                p.emit("scalar", lambda e: e.activation(
                    S["tmp2_col"][:], S["tmp2_col"][:], AF.Ln, bias=cst[:, C_EPS:C_EPS + 1], scale=1.0),
                    reads=["tmp2_col", "cst"], writes=["tmp2_col"])
                p.emit("scalar", lambda e: e.activation(
                    S["tmp2_col"][:], S["tmp2_col"][:], AF.Exp, bias=0.0,
                    scale=cst[:, C_NEGHALF:C_NEGHALF + 1]),
                    reads=["tmp2_col", "cst"], writes=["tmp2_col"])
                p.emit("vector", lambda e, zti=zti: e.tensor_scalar(
                    zti, zti, S["tmp_col"][:], S["tmp2_col"][:], ALU.subtract, ALU.mult),
                    reads=[zkey, "tmp_col", "tmp2_col"], writes=[zkey])
                p.emit("sync", lambda e, zti=zti, tt=tt: e.dma_start(
                    out=dram["y"][tt * 128:(tt + 1) * 128, :], in_=zti),
                    reads=[zkey], writes=["y_out"], dma=True)

        with nc.Block() as block:
            p.finalize(block)
    return nc


def _in_maps_l1attn(x_core, inputs, hc):
    m = dict(x=np.ascontiguousarray(x_core),
             Wq=inputs["L1_attn_Wq"], Wk=inputs["L1_attn_Wk"],
             Wv=inputs["L1_attn_Wv"], Wo=inputs["L1_attn_Wo"],
             bo=inputs["L1_attn_b_out"].reshape(1, D))
    m.update(hc)
    return m



NF4_C0 = float(NF4_VALS[0])
NSC = 180224

def _mid_col_consts():
    arr = np.zeros((128, 16), dtype=np.float32)
    for i in range(15):
        arr[:, i] = -NF4_MIDS[i]
    arr[:, 15] = 0.5
    return arr


def build_l0_ffn():
    nc = bass.Bass("TRN2", target_bir_lowering=False)
    es = contextlib.ExitStack()
    dram = {}
    dram["x"] = nc.dram_tensor("x", [T, D], F32, kind="ExternalInput").ap()
    dram["w1"] = nc.dram_tensor("w1", [5632, D], BF16, kind="ExternalInput").ap()
    dram["w2"] = nc.dram_tensor("w2", [5632, D], BF16, kind="ExternalInput").ap()
    dram["w3"] = nc.dram_tensor("w3", [D, 5632], BF16, kind="ExternalInput").ap()
    _dram_consts(nc, dram)
    dram["mids"] = nc.dram_tensor("hc_mids", [128, 16], F32, kind="ExternalInput").ap()
    dram["y"] = nc.dram_tensor("y", [T, D], F32, kind="ExternalOutput").ap()
    dram["w1T"] = nc.dram_tensor("w1T", [D, 5632], BF16, kind="Internal").ap()
    dram["w2T"] = nc.dram_tensor("w2T", [D, 5632], BF16, kind="Internal").ap()
    dram["w3T"] = nc.dram_tensor("w3T", [5632, D], BF16, kind="Internal").ap()
    dram["scl"] = nc.dram_tensor("sclscr", [3, NSC], F32, kind="Internal").ap()
    dram["deq"] = nc.dram_tensor("deqscr", [3, NSC], F32, kind="Internal").ap()

    S = {"_dram": dram}

    def sb(name, shape, dt):
        S[name] = es.enter_context(nc.sbuf_tensor("s_" + name, list(shape), dt))
        return S[name]

    AR_COLS = 17200
    arh = es.enter_context(nc.sbuf_tensor("s_arena", [128, AR_COLS], F32))
    ar = arh[:]
    big = ar[:, 0:4096]
    _alloc_common(S, sb, big)
    sb("mids", (128, 16), F32)
    sc32 = ar[:, 0:2816]
    chA = ar[0:64, 2816:5632]
    chB = ar[0:64, 5632:8448]
    chC = ar[0:64, 8448:11264]
    chD = ar[0:64, 11264:14080]
    chE = ar[0:64, 14080:16896]
    am_t = ar[0:64, 16896:16907]
    rq_t = ar[0:64, 16907:16918]
    zacc = [ar[:, i * 2048:(i + 1) * 2048] for i in range(4)]
    acc32 = ar[:, 2816:5632]
    sgn32 = ar[:, 5632:8448]
    xres = ar[:, 4 * 2048:5 * 2048]
    scl_t = ar[:, 16918:16918 + 88]
    deq_t = ar[:, 17006:17006 + 88]
    BR_COLS = 26000
    brh = es.enter_context(nc.sbuf_tensor("s_barena", [128, BR_COLS], BF16))
    br = brh[:]
    wtile = br[:, 0:2816]
    sc_bf = br[:, 2816:5632]
    acc = br[:, 5632:8448]
    sgn = br[:, 8448:11264]
    tstage = br[:, 11264:14080]
    repb = br[:, 14080:14168]
    w1s = br[:, 0:8192]
    w2s = br[:, 8192:16384]
    w3s = br[:, 16384:24576]
    htile = br[:, 24576:25088]
    hslab = br[:, 25088:25600]
    mean_bf = es.enter_context(nc.sbuf_tensor("s_meanbf", [1, 1], BF16))

    sb("Xf", (128, ND * T), BF16)
    sb("xb_row", (128, D), BF16)
    sb("mean1", (1, 1), F32)
    sb("ones_row_f", (1, 128), F32)
    sb("cnt_col", (128, 1), F32)
    sb("mean_col", (128, 1), F32)
    sb("ln2_col", (128, 1), F32)
    sb("invln2_col", (128, 1), F32)

    psA = es.enter_context(nc.psum_tensor("psA", [128, 512], F32))
    psB = es.enter_context(nc.psum_tensor("psB", [128, 512], F32))
    psC = es.enter_context(nc.psum_tensor("psC", [128, 512], F32))
    psD = es.enter_context(nc.psum_tensor("psD", [128, 512], F32))
    psSIG = es.enter_context(nc.psum_tensor("psSIG", [1, 512], F32))
    psT = es.enter_context(nc.psum_tensor("psT", [128, 128], BF16))

    WCFG = [
        ("w1", 44, 2048, 32, "w1T"),
        ("w2", 44, 2048, 32, "w2T"),
        ("w3", 16, 5632, 88, "w3T"),
    ]

    SPATH = list(range(8))
    VPATH = list(range(8, 15))
    CPRIME = NF4_C0 + sum(float(NF4_DELTAS[i]) / 2.0 for i in SPATH)

    with es:
        p = Prog(nc, es)
        p.alias = {"xst": "big", "rt_ang": "big"}
        emit_prelude(p, S)
        cst = S["cst"]
        p.emit("sync", lambda e: e.dma_start(out=S["mids"][:], in_=dram["mids"]),
               writes=["mids"], dma=True)
        p.emit("vector", lambda e: e.memset(S["tmp_col"][:], 2.0), writes=["tmp_col"])
        p.emit("scalar", lambda e: e.activation(S["ln2_col"][:], S["tmp_col"][:], AF.Ln),
               reads=["tmp_col"], writes=["ln2_col"])
        p.emit("vector", lambda e: e.reciprocal(S["invln2_col"][:], S["ln2_col"][:]),
               reads=["ln2_col"], writes=["invln2_col"])
        p.emit("vector", lambda e: e.memset(S["ones_row_f"][0:1, :], 1.0), writes=["ones_row_f"])
        p.emit("vector", lambda e: e.memset(S["cnt_col"][:], float(NSC)), writes=["cnt_col"])
        p.emit("vector", lambda e: e.reciprocal(S["cnt_col"][:], S["cnt_col"][:]),
               reads=["cnt_col"], writes=["cnt_col"])

        def floor_chain(eng, out, src, tmp, r, keys, top_bit, nbits):
            ko, ks, ktmp, kr = keys
            emit_floor_pos(p, eng, out, src, ko, ks, tmp, ktmp, r, kr, top_bit, nbits)

        for wi, (wn, ntile, W, nbr, wTn) in enumerate(WCFG):
            for t in range(ntile):
                nsub = W // 2816 if W % 2816 == 0 else 1
                if W == 2048:
                    p.emit("sync", lambda e, wn=wn, t=t: e.dma_start(
                        out=wtile[:, 0:2048], in_=dram[wn][t * 128:(t + 1) * 128, :]),
                        writes=["wtile"], dma=True)
                    p.emit("vector", lambda e: e.tensor_reduce(
                        scl_t[:, 0:32],
                        wtile[:, 0:2048].rearrange("p (b k) -> p b k", k=64),
                        AX.X, ALU.max, apply_absolute_value=True),
                        reads=["wtile"], writes=["scl_t"])
                    p.emit("sync", lambda e, wi=wi, t=t: e.dma_start(
                        out=dram["scl"][wi:wi + 1, t * 4096:(t + 1) * 4096]
                            .rearrange("u (p b) -> (u p) b", p=128),
                        in_=scl_t[:, 0:32]),
                        reads=["scl_t"], writes=["sclscr"], dma=True)
                else:
                    for hf in range(2):
                        p.emit("sync", lambda e, wn=wn, t=t, hf=hf: e.dma_start(
                            out=wtile[:, 0:2816],
                            in_=dram[wn][t * 128:(t + 1) * 128, hf * 2816:(hf + 1) * 2816]),
                            writes=["wtile"], dma=True)
                        p.emit("vector", lambda e: e.tensor_reduce(
                            scl_t[:, 0:44],
                            wtile[:, 0:2816].rearrange("p (b k) -> p b k", k=64),
                            AX.X, ALU.max, apply_absolute_value=True),
                            reads=["wtile"], writes=["scl_t"])
                        p.emit("sync", lambda e, wi=wi, t=t, hf=hf: e.dma_start(
                            out=dram["scl"][wi:wi + 1, t * 11264:(t + 1) * 11264]
                                .rearrange("u (p b) -> (u p) b", p=128)[:, hf * 44:(hf + 1) * 44],
                            in_=scl_t[:, 0:44]),
                            reads=["scl_t"], writes=["sclscr"], dma=True)
            p.emit("sync", lambda e, wi=wi: e.dma_start(
                out=chA, in_=dram["scl"][wi:wi + 1, :].rearrange("u (p i) -> (u p) i", p=64)),
                reads=["sclscr"], writes=["chA"], dma=True)
            p.emit("vector", lambda e: e.reduce_sum(S["tmp_col"][0:64, :], chA, AX.X),
                   reads=["chA"], writes=["tmp_col"])
            p.emit("tensor", lambda e: e.matmul(
                psSIG[0:1, 0:1], S["tmp_col"][0:64, :], S["cnt_col"][0:64, 0:1],
                start=True, stop=True),
                reads=["tmp_col", "cnt_col"], writes=["psSIG"])
            p.emit("vector", lambda e: e.tensor_copy(S["mean1"][0:1, 0:1], psSIG[0:1, 0:1]),
                   reads=["psSIG"], writes=["mean1"])
            p.emit("tensor", lambda e: e.matmul(
                psD[:, 0:1], S["ones_row_f"][0:1, :], S["mean1"][0:1, 0:1],
                start=True, stop=True),
                reads=["ones_row_f", "mean1"], writes=["psD"])
            p.emit("vector", lambda e: e.tensor_copy(S["mean_col"][:], psD[:, 0:1]),
                   reads=["psD"], writes=["mean_col"])
            p.emit("vector", lambda e: e.tensor_scalar(
                chA, chA, S["mean_col"][0:64, :], None, ALU.subtract),
                reads=["chA", "mean_col"], writes=["chA"])
            p.emit("vector", lambda e: e.tensor_reduce(
                am_t, chA.rearrange("p (g k) -> p g k", k=256), AX.X, ALU.max,
                apply_absolute_value=True),
                reads=["chA"], writes=["amrq"])
            p.emit("vector", lambda e: e.reciprocal(rq_t, am_t), reads=["amrq"], writes=["amrq"])
            p.emit("vector", lambda e: e.tensor_tensor(
                chB.rearrange("p (g k) -> p g k", k=256), chA.rearrange("p (g k) -> p g k", k=256),
                rq_t.rearrange("p (g u) -> p g u", u=1).broadcast_to([64, 11, 256]), ALU.mult),
                reads=["chA", "amrq"], writes=["chB"])
            p.emit("vector", lambda e: e.tensor_scalar(
                chB, chB, 128.0, 256.5, ALU.mult, ALU.add),
                reads=["chB"], writes=["chB"])
            floor_chain("vector", chC, chB, chD, chB, ("chC", "chB", "chD", "chB"), 8, 9)
            p.emit("vector", lambda e: e.tensor_scalar(
                chC, chC, 256.0, None, ALU.subtract), reads=["chC"], writes=["chC"])
            p.emit("vector", lambda e: e.tensor_scalar(
                chC, chC, -128.0, 127.0, ALU.max, ALU.min), reads=["chC"], writes=["chC"])
            p.emit("vector", lambda e: e.tensor_tensor(
                chB.rearrange("p (g k) -> p g k", k=256), chC.rearrange("p (g k) -> p g k", k=256),
                am_t.rearrange("p (g u) -> p g u", u=1).broadcast_to([64, 11, 256]), ALU.mult),
                reads=["chC", "amrq"], writes=["chB"])
            p.emit("vector", lambda e: e.tensor_single_scalar(
                chB, chB, 1.0 / 128.0, ALU.mult), reads=["chB"], writes=["chB"])
            p.emit("scalar", lambda e: e.activation(chD, chB, AF.Sign),
                   reads=["chB"], writes=["chD"])
            p.emit("vector", lambda e: e.tensor_single_scalar(
                chA, chB, -1.0, ALU.mult), reads=["chB"], writes=["chA"])
            p.emit("vector", lambda e: e.tensor_tensor(
                chA, chA, chB, ALU.max), reads=["chB", "chA"], writes=["chA"])
            p.emit("vector", lambda e: e.tensor_single_scalar(
                chA, chA, 1e-30, ALU.max), reads=["chA"], writes=["chA"])
            p.emit("scalar", lambda e: e.activation(chB, chA, AF.Ln),
                   reads=["chA"], writes=["chB"])
            p.emit("vector", lambda e: e.tensor_scalar(
                chB, chB, S["invln2_col"][0:64, :], 192.0, ALU.mult, ALU.add),
                reads=["chB", "invln2_col"], writes=["chB"])
            floor_chain("vector", chC, chB, chE, chB, ("chC", "chB", "chE", "chB"), 7, 8)
            p.emit("vector", lambda e: e.tensor_scalar(
                chC, chC, 199.0, -120.0, ALU.subtract, ALU.max),
                reads=["chC"], writes=["chC"])
            p.emit("scalar", lambda e: e.activation(
                chB, chC, AF.Exp, bias=0.0, scale=S["ln2_col"][0:64, :]),
                reads=["chC", "ln2_col"], writes=["chB"])
            p.emit("vector", lambda e: e.reciprocal(chC, chB), reads=["chB"], writes=["chC"])
            p.emit("vector", lambda e: e.tensor_tensor(chC, chA, chC, ALU.mult),
                   reads=["chA", "chC"], writes=["chC"])
            p.emit("vector", lambda e: e.tensor_single_scalar(
                chC, chC, 0.5, ALU.add), reads=["chC"], writes=["chC"])
            floor_chain("vector", chE, chC, chA, chC, ("chE", "chC", "chA", "chC"), 8, 9)
            p.emit("vector", lambda e: e.tensor_tensor(chE, chE, chB, ALU.mult),
                   reads=["chE", "chB"], writes=["chE"])
            p.emit("vector", lambda e: e.tensor_tensor(chE, chE, chD, ALU.mult),
                   reads=["chE", "chD"], writes=["chE"])
            p.emit("vector", lambda e: e.tensor_scalar(
                chE, chE, S["mean_col"][0:64, :], None, ALU.add),
                reads=["chE", "mean_col"], writes=["chE"])
            p.emit("sync", lambda e, wi=wi: e.dma_start(
                out=dram["deq"][wi:wi + 1, :].rearrange("u (p i) -> (u p) i", p=64),
                in_=chE), reads=["chE"], writes=["deqscr"], dma=True)

            for t in range(ntile):
                halves = 1 if W == 2048 else 2
                for hf in range(halves):
                    cw = 2048 if W == 2048 else 2816
                    nb = nbr if W == 2048 else 44
                    if W == 2048:
                        src = dram[wn][t * 128:(t + 1) * 128, :]
                        sclsrc = dram["scl"][wi:wi + 1, t * 4096:(t + 1) * 4096] \
                            .rearrange("u (p b) -> (u p) b", p=128)
                        deqsrc = dram["deq"][wi:wi + 1, t * 4096:(t + 1) * 4096] \
                            .rearrange("u (p b) -> (u p) b", p=128)
                    else:
                        src = dram[wn][t * 128:(t + 1) * 128, hf * 2816:(hf + 1) * 2816]
                        sclsrc = dram["scl"][wi:wi + 1, t * 11264:(t + 1) * 11264] \
                            .rearrange("u (p b) -> (u p) b", p=128)[:, hf * 44:(hf + 1) * 44]
                        deqsrc = dram["deq"][wi:wi + 1, t * 11264:(t + 1) * 11264] \
                            .rearrange("u (p b) -> (u p) b", p=128)[:, hf * 44:(hf + 1) * 44]
                    p.emit("sync", lambda e, src=src, cw=cw: e.dma_start(
                        out=wtile[:, 0:cw], in_=src), writes=["wtile"], dma=True)
                    p.emit("sync", lambda e, sclsrc=sclsrc, nb=nb: e.dma_start(
                        out=scl_t[:, 0:nb], in_=sclsrc),
                        reads=["sclscr"], writes=["scl_t"], dma=True)
                    p.emit("sync", lambda e, deqsrc=deqsrc, nb=nb: e.dma_start(
                        out=deq_t[:, 0:nb], in_=deqsrc),
                        reads=["deqscr"], writes=["deq_t"], dma=True)
                    p.emit("vector", lambda e, nb=nb: e.reciprocal(
                        scl_t[:, 0:nb], scl_t[:, 0:nb]), reads=["scl_t"], writes=["scl_t"])
                    p.emit("vector", lambda e, cw=cw, nb=nb: e.tensor_tensor(
                        sc32[:, 0:cw].rearrange("p (b k) -> p b k", k=64),
                        wtile[:, 0:cw].rearrange("p (b k) -> p b k", k=64),
                        scl_t[:, 0:nb].rearrange("p (b u) -> p b u", u=1).broadcast_to([128, nb, 64]),
                        ALU.mult), reads=["wtile", "scl_t"], writes=["sc32"])
                    first = True
                    for i in VPATH:
                        if first:
                            p.emit("vector", lambda e, i=i, cw=cw: e.tensor_scalar(
                                acc32[:, 0:cw], sc32[:, 0:cw], float(NF4_MIDS[i]),
                                float(NF4_DELTAS[i]), ALU.is_gt, ALU.mult),
                                reads=["sc32"], writes=["chA"])
                            first = False
                        else:
                            p.emit("vector", lambda e, i=i, cw=cw: e.tensor_scalar(
                                sgn32[:, 0:cw], sc32[:, 0:cw], float(NF4_MIDS[i]),
                                float(NF4_DELTAS[i]), ALU.is_gt, ALU.mult),
                                reads=["sc32"], writes=["chB"])
                            p.emit("vector", lambda e, cw=cw: e.tensor_tensor(
                                acc32[:, 0:cw], acc32[:, 0:cw], sgn32[:, 0:cw], ALU.add),
                                reads=["chB", "chA"], writes=["chA"])
                    for i in SPATH:
                        p.emit("scalar", lambda e, i=i, cw=cw: e.activation(
                            sgn32[:, 0:cw], sc32[:, 0:cw], AF.Sign,
                            bias=S["mids"][:, i:i + 1]),
                            reads=["sc32", "mids"], writes=["chB"])
                        p.emit("vector", lambda e, i=i, cw=cw: e.scalar_tensor_tensor(
                            acc32[:, 0:cw], sgn32[:, 0:cw], float(NF4_DELTAS[i]) / 2.0,
                            acc32[:, 0:cw], ALU.mult, ALU.add),
                            reads=["chB", "chA"], writes=["chA"])
                    p.emit("vector", lambda e, cw=cw: e.tensor_scalar(
                        acc32[:, 0:cw], acc32[:, 0:cw], CPRIME, None, ALU.add),
                        reads=["chA"], writes=["chA"])
                    p.emit("vector", lambda e, cw=cw, nb=nb: e.tensor_tensor(
                        acc[:, 0:cw].rearrange("p (b k) -> p b k", k=64),
                        acc32[:, 0:cw].rearrange("p (b k) -> p b k", k=64),
                        deq_t[:, 0:nb].rearrange("p (b u) -> p b u", u=1).broadcast_to([128, nb, 64]),
                        ALU.mult), reads=["chA", "deq_t"], writes=["acc"])
                    nck = cw // 128
                    for ck in range(nck):
                        p.emit("tensor", lambda e, ck=ck: e.transpose(
                            psT[:], acc[:, ck * 128:(ck + 1) * 128], S["ident"][:]),
                            reads=["acc", "ident"], writes=["psT"])
                        p.emit("vector", lambda e, ck=ck: e.tensor_copy(
                            tstage[:, ck * 128:(ck + 1) * 128], psT[:]),
                            reads=["psT"], writes=["tstage"])
                    if W == 2048:
                        p.emit("sync", lambda e, wTn=wTn, t=t, nck=nck: e.dma_start(
                            out=dram[wTn].rearrange("(c p) f -> p c f", p=128)[:, :, t * 128:(t + 1) * 128],
                            in_=tstage[:, 0:2048].rearrange("p (c f) -> p c f", f=128)),
                            reads=["tstage"], writes=[wTn], dma=True)
                    else:
                        p.emit("sync", lambda e, t=t, hf=hf, nck=nck: e.dma_start(
                            out=dram["w3T"].rearrange("(c p) d -> p c d", p=128)[:, hf * 22:(hf + 1) * 22, t * 128:(t + 1) * 128],
                            in_=tstage[:, 0:2816].rearrange("p (c f) -> p c f", f=128)),
                            reads=["tstage"], writes=["w3T"], dma=True)

        p.alias_barrier(["chA", "chB", "chC", "chD", "chE", "sc32", "amrq", "big"],
                        ["zacc0", "zacc1", "zacc2", "zacc3", "xres"])
        p.alias_barrier(["wtile", "sc_bf", "acc", "sgn", "tstage", "repb"],
                        ["w1s", "w2s", "w3s", "htile", "hslab"])
        Xf = S["Xf"]
        for tt in range(NT):
            p.emit("sync", lambda e, tt=tt: e.dma_start(
                out=xres, in_=dram["x"][tt * 128:(tt + 1) * 128, :]),
                writes=["xres"], dma=True)
            p.emit("vector", lambda e: e.tensor_copy(S["xb_row"][:], xres),
                   reads=["xres"], writes=["xb_row"])
            for dc in range(ND):
                p.emit("tensor", lambda e, dc=dc: e.transpose(
                    psT[:], S["xb_row"][:, dc * 128:(dc + 1) * 128], S["ident"][:]),
                    reads=["xb_row", "ident"], writes=["psT"])
                p.emit("vector", lambda e, dc=dc, tt=tt: e.tensor_copy(
                    Xf[:, dc * T + tt * 128: dc * T + (tt + 1) * 128], psT[:]),
                    reads=["psT"], writes=["Xf"])

        for tg in range(4):
            for fb in range(11):
                f0 = fb * 512
                for wTn, slab, skey in (("w1T", w1s, "w1s"), ("w2T", w2s, "w2s")):
                    p.emit("sync", lambda e, wTn=wTn, slab=slab, f0=f0: e.dma_start(
                        out=slab.rearrange("p (c f) -> p c f", f=512),
                        in_=dram[wTn].rearrange("(c p) f -> p c f", p=128)[:, :, f0:f0 + 512]),
                        reads=[wTn], writes=[skey], dma=True)
                p.emit("sync", lambda e, f0=f0: e.dma_start(
                    out=w3s.rearrange("p (c d) -> p c d", d=2048),
                    in_=dram["w3T"].rearrange("(c p) d -> p c d", p=128)[:, (f0 // 128):(f0 // 128) + 4, :]),
                    reads=["w3T"], writes=["w3s"], dma=True)
                for ti in range(4):
                    tt = tg * 4 + ti
                    for ps, pskey, slab, skey in ((psA, "psA", w1s, "w1s"),
                                                  (psB, "psB", w2s, "w2s")):
                        for dc in range(ND):
                            p.emit("tensor", lambda e, ps=ps, dc=dc, tt=tt, slab=slab: e.matmul(
                                ps[:], Xf[:, dc * T + tt * 128: dc * T + (tt + 1) * 128],
                                slab[:, dc * 512:(dc + 1) * 512],
                                start=(dc == 0), stop=(dc == ND - 1)),
                                reads=["Xf", skey], writes=[pskey])
                    p.emit("scalar", lambda e: e.activation(htile[:, 0:512], psA[:], AF.Silu),
                           reads=["psA"], writes=["htile"])
                    p.emit("vector", lambda e: e.tensor_copy(hslab[:, 0:512], psB[:]),
                           reads=["psB"], writes=["hslab"])
                    p.emit("vector", lambda e: e.tensor_tensor(
                        htile[:, 0:512], htile[:, 0:512], hslab[:, 0:512], ALU.mult),
                        reads=["htile", "hslab"], writes=["htile"])
                    for fc in range(4):
                        p.emit("tensor", lambda e, fc=fc: e.transpose(
                            psT[:], htile[:, fc * 128:(fc + 1) * 128], S["ident"][:]),
                            reads=["htile", "ident"], writes=["psT"])
                        p.emit("vector", lambda e, fc=fc: e.tensor_copy(
                            hslab[:, fc * 128:(fc + 1) * 128], psT[:]),
                            reads=["psT"], writes=["hslab"])
                    for dg in range(4):
                        for fc in range(4):
                            p.emit("tensor", lambda e, dg=dg, fc=fc: e.matmul(
                                psD[:], hslab[:, fc * 128:(fc + 1) * 128],
                                w3s[:, fc * 2048 + dg * 512: fc * 2048 + (dg + 1) * 512],
                                start=(fc == 0), stop=(fc == 3)),
                                reads=["hslab", "w3s"], writes=["psD"])
                        zkey = "zacc%d" % ti
                        if fb == 0:
                            p.emit("vector", lambda e, ti=ti, dg=dg: e.tensor_copy(
                                zacc[ti][:, dg * 512:(dg + 1) * 512], psD[:]),
                                reads=["psD"], writes=[zkey])
                        else:
                            p.emit("vector", lambda e, ti=ti, dg=dg: e.tensor_tensor(
                                zacc[ti][:, dg * 512:(dg + 1) * 512],
                                zacc[ti][:, dg * 512:(dg + 1) * 512], psD[:], ALU.add),
                                reads=["psD", zkey], writes=[zkey])
            for ti in range(4):
                tt = tg * 4 + ti
                zti = zacc[ti]
                zkey = "zacc%d" % ti
                p.emit("sync", lambda e, tt=tt: e.dma_start(
                    out=xres, in_=dram["x"][tt * 128:(tt + 1) * 128, :]),
                    writes=["xres"], dma=True)
                p.emit("vector", lambda e, zti=zti: e.tensor_tensor(
                    zti, zti, xres, ALU.add), reads=[zkey, "xres"], writes=[zkey])
                p.emit("vector", lambda e, zti=zti: e.reduce_sum(S["tmp_col"][:], zti, AX.X),
                       reads=[zkey], writes=["tmp_col"])
                p.emit("scalar", lambda e, zti=zti: e.activation(
                    S["xb_row"][:], zti, AF.Square, accum_out=S["tmp2_col"][:]),
                    reads=[zkey], writes=["xb_row", "tmp2_col"])
                p.emit("vector", lambda e: e.tensor_single_scalar(
                    S["tmp_col"][:], S["tmp_col"][:], 1.0 / D, ALU.mult),
                    reads=["tmp_col"], writes=["tmp_col"])
                p.emit("scalar", lambda e: e.activation(S["one_col"][:], S["tmp_col"][:], AF.Square),
                       reads=["tmp_col"], writes=["one_col"])
                p.emit("vector", lambda e: e.scalar_tensor_tensor(
                    S["tmp2_col"][:], S["tmp2_col"][:], 1.0 / D, S["one_col"][:],
                    ALU.mult, ALU.subtract),
                    reads=["tmp2_col", "one_col"], writes=["tmp2_col"])
                p.emit("scalar", lambda e: e.activation(
                    S["tmp2_col"][:], S["tmp2_col"][:], AF.Ln, bias=cst[:, C_EPS:C_EPS + 1], scale=1.0),
                    reads=["tmp2_col", "cst"], writes=["tmp2_col"])
                p.emit("scalar", lambda e: e.activation(
                    S["tmp2_col"][:], S["tmp2_col"][:], AF.Exp, bias=0.0,
                    scale=cst[:, C_NEGHALF:C_NEGHALF + 1]),
                    reads=["tmp2_col", "cst"], writes=["tmp2_col"])
                p.emit("vector", lambda e, zti=zti: e.tensor_scalar(
                    zti, zti, S["tmp_col"][:], S["tmp2_col"][:], ALU.subtract, ALU.mult),
                    reads=[zkey, "tmp_col", "tmp2_col"], writes=[zkey])
                p.emit("sync", lambda e, zti=zti, tt=tt: e.dma_start(
                    out=dram["y"][tt * 128:(tt + 1) * 128, :], in_=zti),
                    reads=[zkey], writes=["y_out"], dma=True)

        with nc.Block() as block:
            p.finalize(block)
    return nc


def _in_maps_l0ffn(x_core, inputs, hc):
    m = dict(x=np.ascontiguousarray(x_core),
             w1=inputs["L0_ffn_weight1"], w2=inputs["L0_ffn_weight2"],
             w3=inputs["L0_ffn_weight3"])
    m.update(hc)
    m["hc_mids"] = _mid_col_consts()
    return m



def _sylvester(n):
    H = np.array([[1.0]], dtype=np.float64)
    while H.shape[0] < n:
        H = np.block([[H, H], [H, -H]])
    return H


def _l1ffn_consts():
    H2048 = _sylvester(2048).astype(bf16)
    H64 = _sylvester(64)
    bd = np.zeros((128, 128), dtype=np.float64)
    bd[0:64, 0:64] = H64
    bd[64:128, 64:128] = H64
    H64bd = bd.astype(bf16)
    H128 = _sylvester(128).astype(bf16)
    return dict(Hb=np.ascontiguousarray(H2048), H64bd=H64bd, H128=H128)


def build_l1_ffn():
    nc = bass.Bass("TRN2", target_bir_lowering=False)
    es = contextlib.ExitStack()
    dram = {}
    dram["x"] = nc.dram_tensor("x", [T, D], F32, kind="ExternalInput").ap()
    dram["wg"] = nc.dram_tensor("wg", [8192, D], F32, kind="ExternalInput").ap()
    dram["wu"] = nc.dram_tensor("wu", [8192, D], F32, kind="ExternalInput").ap()
    dram["wd"] = nc.dram_tensor("wd", [D, 8192], F32, kind="ExternalInput").ap()
    _dram_consts(nc, dram)
    dram["Hb"] = nc.dram_tensor("hc_Hb", [D, D], BF16, kind="ExternalInput").ap()
    dram["H64bd"] = nc.dram_tensor("hc_H64bd", [128, 128], BF16, kind="ExternalInput").ap()
    dram["H128"] = nc.dram_tensor("hc_H128", [128, 128], BF16, kind="ExternalInput").ap()
    dram["y"] = nc.dram_tensor("y", [T, D], F32, kind="ExternalOutput").ap()
    dram["wgT"] = nc.dram_tensor("wgT", [D, 8192], BF16, kind="Internal").ap()
    dram["wuT"] = nc.dram_tensor("wuT", [D, 8192], BF16, kind="Internal").ap()
    dram["wdT"] = nc.dram_tensor("wdT", [8192, D], BF16, kind="Internal").ap()
    dram["h_scr"] = nc.dram_tensor("h_scr", [T, 8192], F32, kind="Internal").ap()
    dram["y1_scr"] = nc.dram_tensor("y1_scr", [128, 64 * T], BF16, kind="Internal").ap()
    dram["xq_scr"] = nc.dram_tensor("xq_scr", [128, 64 * T], BF16, kind="Internal").ap()

    S = {"_dram": dram}

    def sb(name, shape, dt):
        S[name] = es.enter_context(nc.sbuf_tensor("s_" + name, list(shape), dt))
        return S[name]

    arh = es.enter_context(nc.sbuf_tensor("s_arena", [128, 18000], F32))
    ar = arh[:]
    big = ar[:, 0:4096]
    _alloc_common(S, sb, big)
    sb("invsqrt2048_col", (128, 1), F32)
    sb("invsqrt8192_col", (128, 1), F32)
    fw32 = ar[:, 0:2048]
    htile = ar[:, 4096:12288]
    y2f = ar[:, 4096:12288]
    zacc = [ar[:, 12288 + i * 2048:12288 + (i + 1) * 2048] for i in range(2)]
    xres = ar[:, 2048:4096]
    mx_col = es.enter_context(nc.sbuf_tensor("s_mx", [128, 8], F32))

    brh = es.enter_context(nc.sbuf_tensor("s_barena", [128, 33000], BF16))
    br = brh[:]
    wslab1 = br[:, 0:8192]
    wslab2 = br[:, 8192:16384]
    wdslab = br[:, 0:16384]
    ternb = br[:, 16384:16384 + 8192]
    ternc = br[:, 24576:24576 + 8192]
    Hconst = es.enter_context(nc.sbuf_tensor("s_Hconst", [128, 256], BF16))[:]
    S["Hconst"] = Hconst

    sb("Xf", (128, ND * T), BF16)
    sb("xb_row", (128, D), BF16)
    sb("ones_row_f", (1, 128), F32)
    sb("cnt_col", (128, 1), F32)
    sb("sw_col", (128, 3), F32)
    sb("rc254_col", (128, 1), F32)
    sb("sc_col", (128, 1), F32)
    sb("sc_row", (1, 128), BF16)
    sb("fcol", (128, 3), F32)
    sb("fscl", (128, 32), F32)
    sb("fscl2", (128, 16), F32)

    psA = es.enter_context(nc.psum_tensor("psA", [128, 512], F32))
    psB = es.enter_context(nc.psum_tensor("psB", [128, 512], F32))
    psC = es.enter_context(nc.psum_tensor("psC", [128, 512], F32))
    psD = es.enter_context(nc.psum_tensor("psD", [128, 512], F32))
    psT = es.enter_context(nc.psum_tensor("psT", [128, 128], BF16))

    with es:
        p = Prog(nc, es)
        p.alias = {"xst": "big", "rt_ang": "big"}
        emit_prelude(p, S)
        cst = S["cst"]
        p.emit("sync", lambda e: e.dma_start(out=Hconst[:, 0:128], in_=dram["H64bd"]),
               writes=["bHc"], dma=True)
        p.emit("sync", lambda e: e.dma_start(out=Hconst[:, 128:256], in_=dram["H128"]),
               writes=["bHc"], dma=True)
        p.emit("vector", lambda e: e.memset(S["ones_row_f"][0:1, :], 1.0), writes=["ones_row_f"])
        p.emit("vector", lambda e: e.memset(S["rc254_col"][:], 254.0), writes=["rc254_col"])
        p.emit("vector", lambda e: e.reciprocal(S["rc254_col"][:], S["rc254_col"][:]),
               reads=["rc254_col"], writes=["rc254_col"])

        def floor_frac(eng, r, key_r, tmp, key_tmp, top_bit, nbits):
            for i in range(nbits):
                b = float(2 ** (top_bit - i))
                p.emit(eng, lambda e, b=b: e.tensor_scalar(tmp, r, b, b, ALU.is_ge, ALU.mult),
                       reads=[key_r], writes=[key_tmp])
                p.emit(eng, lambda e: e.tensor_tensor(r, r, tmp, ALU.subtract),
                       reads=[key_tmp, key_r], writes=[key_r])

        WL = [("wg", 64, 2048, 0), ("wu", 64, 2048, 1), ("wd", 16, 8192, 2)]
        for wn, ntile, W, wi in WL:
            for t in range(ntile):
                p.emit("sync", lambda e, wn=wn, t=t, W=W: e.dma_start(
                    out=ar[:, 4096:4096 + W], in_=dram[wn][t * 128:(t + 1) * 128, :]),
                    writes=["arB"], dma=True)
                p.emit("vector", lambda e, W=W: e.tensor_reduce(
                    ar[:, 0:1], ar[:, 4096:4096 + W], AX.X, ALU.add,
                    apply_absolute_value=True),
                    reads=["arB"], writes=["arA"])
                if t == 0:
                    p.emit("vector", lambda e: e.tensor_copy(ar[:, 1:2], ar[:, 0:1]),
                           reads=["arA"], writes=["arA"])
                else:
                    p.emit("vector", lambda e: e.tensor_tensor(
                        ar[:, 1:2], ar[:, 1:2], ar[:, 0:1], ALU.add),
                        reads=["arA"], writes=["arA"])
            p.emit("vector", lambda e, W=W, ntile=ntile: e.memset(
                S["cnt_col"][:], float(W * ntile * 128)), writes=["cnt_col"])
            p.emit("vector", lambda e: e.reciprocal(S["cnt_col"][:], S["cnt_col"][:]),
                   reads=["cnt_col"], writes=["cnt_col"])
            p.emit("tensor", lambda e: e.matmul(
                psD[0:1, 0:1], ar[:, 1:2], S["cnt_col"][:, 0:1], start=True, stop=True),
                reads=["arA", "cnt_col"], writes=["psD"])
            p.emit("vector", lambda e: e.tensor_single_scalar(
                ar[:, 2:3][0:1, :], psD[0:1, 0:1], 1e-5, ALU.max),
                reads=["psD"], writes=["arA"])
            p.emit("tensor", lambda e: e.matmul(
                psC[:, 0:1], S["ones_row_f"][0:1, :], ar[0:1, 2:3], start=True, stop=True),
                reads=["ones_row_f", "arA"], writes=["psC"])
            p.emit("vector", lambda e, wi=wi: e.tensor_copy(
                S["sw_col"][:, wi:wi + 1], psC[:, 0:1]), reads=["psC"], writes=["sw_col"])
        p.emit("vector", lambda e: e.reciprocal(S["fcol"][:, 0:3], S["sw_col"][:, 0:3]),
               reads=["sw_col"], writes=["fcol"])

        for wn, ntile, W, wi in WL:
            wTn = {"wg": "wgT", "wu": "wuT", "wd": "wdT"}[wn]
            for t in range(ntile):
                p.emit("sync", lambda e, wn=wn, t=t, W=W: e.dma_start(
                    out=ar[:, 4096:4096 + W], in_=dram[wn][t * 128:(t + 1) * 128, :]),
                    writes=["arB"], dma=True)
                p.emit("scalar", lambda e, W=W, wi=wi: e.activation(
                    ternb[:, 0:W], ar[:, 4096:4096 + W], AF.Sign,
                    bias=cst[:, C_NEGHALF:C_NEGHALF + 1], scale=S["fcol"][:, wi:wi + 1]),
                    reads=["arB", "fcol", "cst"], writes=["bt1"])
                p.emit("scalar", lambda e, W=W, wi=wi: e.activation(
                    ternc[:, 0:W], ar[:, 4096:4096 + W], AF.Sign,
                    bias=cst[:, C_HALF:C_HALF + 1], scale=S["fcol"][:, wi:wi + 1]),
                    reads=["arB", "fcol", "cst"], writes=["bt2"])
                p.emit("vector", lambda e, W=W: e.tensor_tensor(
                    ternb[:, 0:W], ternb[:, 0:W], ternc[:, 0:W], ALU.add),
                    reads=["bt1", "bt2"], writes=["bt1"])
                nck = W // 128
                for ck in range(nck):
                    p.emit("tensor", lambda e, ck=ck: e.transpose(
                        psT[:], ternb[:, ck * 128:(ck + 1) * 128], S["ident"][:]),
                        reads=["bt1", "ident"], writes=["psT"])
                    p.emit("vector", lambda e, ck=ck: e.tensor_copy(
                        ternc[:, ck * 128:(ck + 1) * 128], psT[:]),
                        reads=["psT"], writes=["bt2"])
                if wn in ("wg", "wu"):
                    p.emit("sync", lambda e, wTn=wTn, t=t: e.dma_start(
                        out=dram[wTn].rearrange("(c p) f -> p c f", p=128)[:, :, t * 128:(t + 1) * 128],
                        in_=ternc[:, 0:2048].rearrange("p (c f) -> p c f", f=128)),
                        reads=["bt2"], writes=[wTn], dma=True)
                else:
                    for j in (0, 1):
                        p.emit("sync", lambda e, t=t, j=j: e.dma_start(
                            out=dram["wdT"].rearrange("(b a2 j) d -> j b a2 d", b=64, j=2)[j, :, :, t * 128:(t + 1) * 128],
                            in_=ternc[j * 64:(j + 1) * 64, 0:8192].rearrange("p (c f) -> p c f", f=128)),
                            reads=["bt2"], writes=["wdT"], dma=True)

        _emit_l1ffn_main(p, S, dram, ar, br, zacc, xres, mx_col,
                         (psA, psB, psC, psD, psT), cst)

        with nc.Block() as block:
            p.finalize(block)
    return nc




def _emit_l1ffn_main(p, S, dram, ar, br, zacc, xres, mx_col, pss, cst):
    psA, psB, psC, psD, psT = pss
    fw32 = ar[:, 0:2048]
    htile = ar[:, 4096:12288]
    y2f = ar[:, 4096:12288]
    wslab1 = br[:, 0:8192]
    wslab2 = br[:, 8192:16384]
    wdslab = br[:, 0:16384]
    ternb = br[:, 16384:24576]
    ternc = br[:, 24576:32768]
    Hconst = S["Hconst"]
    Xf = S["Xf"]
    r2048 = S["invsqrt2048_col"]
    r8192 = S["invsqrt8192_col"]

    def floor_frac(eng, r, key_r, tmp, key_tmp, top_bit, nbits):
        for i in range(nbits):
            b = float(2 ** (top_bit - i))
            p.emit(eng, lambda e, b=b, r=r, tmp=tmp: e.tensor_scalar(
                tmp, r, b, b, ALU.is_ge, ALU.mult),
                reads=[key_r], writes=[key_tmp])
            p.emit(eng, lambda e, r=r, tmp=tmp: e.tensor_tensor(
                r, r, tmp, ALU.subtract),
                reads=[key_tmp, key_r], writes=[key_r])

    p.alias_barrier(["big"], ["arA", "arB"])

    for tt in range(NT):
        p.emit("sync", lambda e, tt=tt: e.dma_start(
            out=xres, in_=dram["x"][tt * 128:(tt + 1) * 128, :]),
            writes=["big"], dma=True)
        p.emit("vector", lambda e: e.tensor_copy(S["xb_row"][:], xres),
               reads=["big"], writes=["xb_row"])
        for dc in range(ND):
            p.emit("tensor", lambda e, dc=dc: e.transpose(
                psT[:], S["xb_row"][:, dc * 128:(dc + 1) * 128], S["ident"][:]),
                reads=["xb_row", "ident"], writes=["psT"])
            p.emit("vector", lambda e, dc=dc, tt=tt: e.tensor_copy(
                Xf[:, dc * T + tt * 128: dc * T + (tt + 1) * 128], psT[:]),
                reads=["psT"], writes=["Xf"])

    for pr in range(8):
        for fs in range(4):
            p.emit("sync", lambda e, fs=fs: e.dma_start(
                out=wslab1.rearrange("p (c f) -> p c f", f=512),
                in_=dram["Hb"].rearrange("(c p) f -> p c f", p=128)[:, :, fs * 512:(fs + 1) * 512]),
                writes=["bsl1"], dma=True)
            for u in range(2):
                tt = pr * 2 + u
                ps, pskey = (psA, "psA") if u == 0 else (psB, "psB")
                for dc in range(ND):
                    p.emit("tensor", lambda e, ps=ps, dc=dc, tt=tt: e.matmul(
                        ps[:], Xf[:, dc * T + tt * 128: dc * T + (tt + 1) * 128],
                        wslab1[:, dc * 512:(dc + 1) * 512],
                        start=(dc == 0), stop=(dc == ND - 1)),
                        reads=["Xf", "bsl1"], writes=[pskey])
                p.emit("vector", lambda e, ps=ps, u=u, fs=fs: e.tensor_scalar(
                    ar[:, u * 2048 + fs * 512: u * 2048 + (fs + 1) * 512],
                    ps[:], r2048[:], None, ALU.mult),
                    reads=[pskey, "invsqrt2048_col"], writes=["arA"])
        for u in range(2):
            tt = pr * 2 + u
            fw = ar[:, u * 2048:(u + 1) * 2048]
            p.emit("vector", lambda e, fw=fw: e.tensor_reduce(
                mx_col[:, 0:1], fw, AX.X, ALU.max, apply_absolute_value=True),
                reads=["arA"], writes=["mx"])
            p.emit("vector", lambda e: e.tensor_single_scalar(
                mx_col[:, 0:1], mx_col[:, 0:1], 1e-5, ALU.max), reads=["mx"], writes=["mx"])
            p.emit("vector", lambda e: e.reciprocal(mx_col[:, 1:2], mx_col[:, 0:1]),
                   reads=["mx"], writes=["mx"])
            p.emit("vector", lambda e: e.tensor_single_scalar(
                mx_col[:, 1:2], mx_col[:, 1:2], 127.0, ALU.mult), reads=["mx"], writes=["mx"])
            yb = ar[:, 4096:6144]
            tmpb = ar[:, 6144:8192]
            rb = ar[:, 8192:10240]
            p.emit("vector", lambda e, fw=fw, yb=yb: e.tensor_scalar(
                yb, fw, mx_col[:, 1:2], 256.5, ALU.mult, ALU.add),
                reads=["arA", "mx"], writes=["arB"])
            p.emit("vector", lambda e, yb=yb, rb=rb: e.tensor_copy(rb, yb),
                   reads=["arB"], writes=["arB"])
            floor_frac("vector", rb, "arB", tmpb, "arB", 8, 9)
            p.emit("vector", lambda e, yb=yb, rb=rb: e.tensor_tensor(
                yb, yb, rb, ALU.subtract), reads=["arB"], writes=["arB"])
            p.emit("vector", lambda e, yb=yb: e.tensor_scalar(
                yb, yb, 256.0, None, ALU.subtract), reads=["arB"], writes=["arB"])
            p.emit("vector", lambda e, yb=yb: e.tensor_scalar(
                yb, yb, -128.0, 127.0, ALU.max, ALU.min), reads=["arB"], writes=["arB"])
            p.emit("vector", lambda e, yb=yb: e.tensor_copy(ternb[:, 0:2048], yb),
                   reads=["arB"], writes=["bt1"])
            for dc in range(ND):
                p.emit("tensor", lambda e, dc=dc: e.transpose(
                    psT[:], ternb[:, dc * 128:(dc + 1) * 128], S["ident"][:]),
                    reads=["bt1", "ident"], writes=["psT"])
                p.emit("vector", lambda e, dc=dc, tt=tt: e.tensor_copy(
                    Xf[:, dc * T + tt * 128: dc * T + (tt + 1) * 128], psT[:]),
                    reads=["psT"], writes=["Xf"])
            for wi in (0, 1):
                p.emit("vector", lambda e, wi=wi, tt=tt: e.tensor_tensor(
                    S["fscl"][:, tt * 2 + wi: tt * 2 + wi + 1], mx_col[:, 0:1],
                    S["sw_col"][:, wi:wi + 1], ALU.mult),
                    reads=["mx", "sw_col"], writes=["fscl"])
                p.emit("vector", lambda e, wi=wi, tt=tt: e.tensor_scalar(
                    S["fscl"][:, tt * 2 + wi: tt * 2 + wi + 1],
                    S["fscl"][:, tt * 2 + wi: tt * 2 + wi + 1],
                    S["rc254_col"][:], None, ALU.mult),
                    reads=["fscl", "rc254_col"], writes=["fscl"])

    for fo in range(16):
        for wTn, slab, skey in (("wgT", wslab1, "bsl1"), ("wuT", wslab2, "bsl2")):
            p.emit("sync", lambda e, wTn=wTn, slab=slab, fo=fo: e.dma_start(
                out=slab.rearrange("p (c f) -> p c f", f=512),
                in_=dram[wTn].rearrange("(c p) f -> p c f", p=128)[:, :, fo * 512:(fo + 1) * 512]),
                reads=[wTn], writes=[skey], dma=True)
        for tt in range(NT):
            for ps, pskey, slab, skey in ((psA, "psA", wslab1, "bsl1"),
                                          (psB, "psB", wslab2, "bsl2")):
                for dc in range(ND):
                    p.emit("tensor", lambda e, ps=ps, dc=dc, tt=tt, slab=slab: e.matmul(
                        ps[:], Xf[:, dc * T + tt * 128: dc * T + (tt + 1) * 128],
                        slab[:, dc * 512:(dc + 1) * 512],
                        start=(dc == 0), stop=(dc == ND - 1)),
                        reads=["Xf", skey], writes=[pskey])
            p.emit("scalar", lambda e, tt=tt: e.activation(
                ar[:, 0:512], psA[:], AF.Silu, bias=0.0,
                scale=S["fscl"][:, tt * 2: tt * 2 + 1]),
                reads=["psA", "fscl"], writes=["arA"])
            p.emit("vector", lambda e, tt=tt: e.tensor_scalar(
                ar[:, 512:1024], psB[:], S["fscl"][:, tt * 2 + 1: tt * 2 + 2],
                None, ALU.mult),
                reads=["psB", "fscl"], writes=["arA"])
            p.emit("vector", lambda e: e.tensor_tensor(
                ar[:, 1024:1536], ar[:, 0:512], ar[:, 512:1024], ALU.mult),
                reads=["arA"], writes=["arA"])
            p.emit("sync", lambda e, tt=tt, fo=fo: e.dma_start(
                out=dram["h_scr"][tt * 128:(tt + 1) * 128, fo * 512:(fo + 1) * 512],
                in_=ar[:, 1024:1536]),
                reads=["arA"], writes=["h_scr"], dma=True)

    for tt in range(NT):
        p.emit("sync", lambda e, tt=tt: e.dma_start(
            out=htile, in_=dram["h_scr"][tt * 128:(tt + 1) * 128, :]),
            reads=["h_scr"], writes=["arB"], dma=True)
        p.emit("vector", lambda e: e.tensor_copy(ternb[:, 0:8192], htile),
               reads=["arB"], writes=["bt1"])
        for c in range(64):
            p.emit("tensor", lambda e, c=c: e.transpose(
                psT[:], ternb[:, c * 128:(c + 1) * 128], S["ident"][:]),
                reads=["bt1", "ident"], writes=["psT"])
            p.emit("vector", lambda e, c=c: e.tensor_copy(
                ternc[:, c * 128:(c + 1) * 128], psT[:]),
                reads=["psT"], writes=["bt2"])
        for c in range(64):
            p.emit("tensor", lambda e, c=c: e.matmul(
                psC[:, 0:128], Hconst[:, 0:128], ternc[:, c * 128:(c + 1) * 128],
                start=True, stop=True),
                reads=["bHc", "bt2"], writes=["psC"])
            p.emit("vector", lambda e, c=c: e.tensor_copy(
                ternb[:, c * 128:(c + 1) * 128], psC[:, 0:128]),
                reads=["psC"], writes=["bt1"])
        for j in (0, 1):
            p.emit("sync", lambda e, tt=tt, j=j: e.dma_start(
                out=dram["y1_scr"].rearrange("(c2 j) (b t) -> j b c2 t", j=2, t=T)[j, :, :, tt * 128:(tt + 1) * 128],
                in_=ternb[j * 64:(j + 1) * 64, 0:8192].rearrange("p (c f) -> p c f", f=128)),
                reads=["bt1"], writes=["y1_scr"], dma=True)
    for tt in range(NT):
        p.emit("sync", lambda e, tt=tt: e.dma_start(
            out=ternc[:, 0:8192].rearrange("p (b t) -> p b t", t=128),
            in_=dram["y1_scr"].rearrange("a (b t) -> a b t", t=T)[:, :, tt * 128:(tt + 1) * 128]),
            reads=["y1_scr"], writes=["bt2"], dma=True)
        for ns in range(16):
            p.emit("tensor", lambda e, ns=ns: e.matmul(
                psC[:], Hconst[:, 128:256], ternc[:, ns * 512:(ns + 1) * 512],
                start=True, stop=True),
                reads=["bHc", "bt2"], writes=["psC"])
            p.emit("vector", lambda e, ns=ns: e.tensor_scalar(
                y2f[:, ns * 512:(ns + 1) * 512], psC[:], r8192[:], None, ALU.mult),
                reads=["psC", "invsqrt8192_col"], writes=["arB"])
        p.emit("vector", lambda e: e.tensor_reduce(
            ar[:, 0:128], y2f.rearrange("p (b t) -> p t b", t=128), AX.X, ALU.max,
            apply_absolute_value=True),
            reads=["arB"], writes=["arA"])
        p.emit("vector", lambda e: e.tensor_copy(ternb[:, 0:128], ar[:, 0:128]),
               reads=["arA"], writes=["bt1"])
        p.emit("tensor", lambda e: e.transpose(
            psT[:], ternb[:, 0:128], S["ident"][:]),
            reads=["bt1", "ident"], writes=["psT"])
        p.emit("vector", lambda e: e.tensor_copy(ternb[:, 128:256], psT[:]),
               reads=["psT"], writes=["bt1"])
        p.emit("vector", lambda e: e.tensor_reduce(
            mx_col[:, 0:1], ternb[:, 128:256], AX.X, ALU.max, apply_absolute_value=True),
            reads=["bt1"], writes=["mx"])
        p.emit("vector", lambda e: e.tensor_single_scalar(
            mx_col[:, 0:1], mx_col[:, 0:1], 1e-5, ALU.max), reads=["mx"], writes=["mx"])
        p.emit("vector", lambda e: e.reciprocal(mx_col[:, 1:2], mx_col[:, 0:1]),
               reads=["mx"], writes=["mx"])
        p.emit("vector", lambda e: e.tensor_single_scalar(
            mx_col[:, 1:2], mx_col[:, 1:2], 127.0, ALU.mult), reads=["mx"], writes=["mx"])
        p.emit("vector", lambda e, tt=tt: e.tensor_tensor(
            S["fscl2"][:, tt:tt + 1], mx_col[:, 0:1], S["sw_col"][:, 2:3], ALU.mult),
            reads=["mx", "sw_col"], writes=["fscl2"])
        p.emit("vector", lambda e, tt=tt: e.tensor_scalar(
            S["fscl2"][:, tt:tt + 1], S["fscl2"][:, tt:tt + 1], S["rc254_col"][:],
            None, ALU.mult),
            reads=["fscl2", "rc254_col"], writes=["fscl2"])
        p.emit("vector", lambda e: e.tensor_copy(ternb[:, 0:1], mx_col[:, 1:2]),
               reads=["mx"], writes=["bt1"])
        p.emit("tensor", lambda e: e.transpose(
            psT[0:1, :], ternb[:, 0:1], S["ident"][:]),
            reads=["bt1", "ident"], writes=["psT"])
        p.emit("vector", lambda e: e.tensor_copy(S["sc_row"][0:1, :], psT[0:1, :]),
               reads=["psT"], writes=["sc_row"])
        p.emit("tensor", lambda e: e.matmul(
            psC[:, 0:128], S["ones_row"][0:1, :], S["sc_row"][0:1, :],
            start=True, stop=True),
            reads=["ones_row", "sc_row"], writes=["psC"])
        p.emit("vector", lambda e: e.tensor_copy(ar[:, 0:128], psC[:, 0:128]),
               reads=["psC"], writes=["arA"])
        p.emit("vector", lambda e: e.tensor_tensor(
            y2f.rearrange("p (b t) -> p b t", t=128),
            y2f.rearrange("p (b t) -> p b t", t=128),
            ar[:, 0:128].rearrange("p (u t) -> p u t", u=1).broadcast_to([128, 64, 128]),
            ALU.mult),
            reads=["arB", "arA"], writes=["arB"])
        p.emit("vector", lambda e: e.tensor_scalar(
            y2f, y2f, 256.5, None, ALU.add), reads=["arB"], writes=["arB"])
        for cg in range(4):
            seg = y2f[:, cg * 2048:(cg + 1) * 2048]
            rseg = ar[:, 0:2048]
            tseg = ar[:, 2048:4096]
            p.emit("vector", lambda e, seg=seg, rseg=rseg: e.tensor_copy(rseg, seg),
                   reads=["arB"], writes=["arA"])
            floor_frac("vector", rseg, "arA", tseg, "big", 8, 9)
            p.emit("vector", lambda e, seg=seg, rseg=rseg: e.tensor_tensor(
                seg, seg, rseg, ALU.subtract), reads=["arA", "arB"], writes=["arB"])
            p.emit("vector", lambda e, seg=seg: e.tensor_scalar(
                seg, seg, 256.0, None, ALU.subtract), reads=["arB"], writes=["arB"])
            p.emit("vector", lambda e, seg=seg: e.tensor_scalar(
                seg, seg, -128.0, 127.0, ALU.max, ALU.min), reads=["arB"], writes=["arB"])
        p.emit("vector", lambda e: e.tensor_copy(ternb[:, 0:8192], y2f),
               reads=["arB"], writes=["bt1"])
        p.emit("sync", lambda e, tt=tt: e.dma_start(
            out=dram["xq_scr"].rearrange("a (b t) -> a b t", t=T)[:, :, tt * 128:(tt + 1) * 128],
            in_=ternb[:, 0:8192].rearrange("p (b t) -> p b t", t=128)),
            reads=["bt1"], writes=["xq_scr"], dma=True)

    for tg in range(8):
        for ti in range(2):
            tt = tg * 2 + ti
            dst = ternb if ti == 0 else ternc
            dkey = "bt1" if ti == 0 else "bt2"
            p.emit("sync", lambda e, tt=tt, dst=dst: e.dma_start(
                out=dst[:, 0:8192].rearrange("p (b t) -> p b t", t=128),
                in_=dram["xq_scr"].rearrange("a (b t) -> a b t", t=T)[:, :, tt * 128:(tt + 1) * 128]),
                reads=["xq_scr"], writes=[dkey], dma=True)
        for dg in range(4):
            for hb in range(2):
                p.emit("sync", lambda e, dg=dg, hb=hb: e.dma_start(
                    out=wdslab.rearrange("p (c f) -> p c f", f=512),
                    in_=dram["wdT"].rearrange("(c p) d -> p c d", p=128)[:, hb * 32:(hb + 1) * 32, dg * 512:(dg + 1) * 512]),
                    reads=["wdT"], writes=["bsl1", "bsl2"], dma=True)
                for ti in range(2):
                    xsrc = ternb if ti == 0 else ternc
                    skey2 = "bt1" if ti == 0 else "bt2"
                    ps, pskey = (psA, "psA") if ti == 0 else (psB, "psB")
                    for bi in range(32):
                        b = hb * 32 + bi
                        p.emit("tensor", lambda e, ps=ps, xsrc=xsrc, b=b, bi=bi,
                               first=(hb == 0 and bi == 0), last=(hb == 1 and bi == 31): e.matmul(
                            ps[:], xsrc[:, b * 128:(b + 1) * 128],
                            wdslab[:, bi * 512:(bi + 1) * 512],
                            start=first, stop=last),
                            reads=[skey2, "bsl1", "bsl2"], writes=[pskey])
            for ti in range(2):
                tt = tg * 2 + ti
                ps, pskey = (psA, "psA") if ti == 0 else (psB, "psB")
                if dg == 0:
                    p.emit("sync", lambda e, tt=tt: e.dma_start(
                        out=xres, in_=dram["x"][tt * 128:(tt + 1) * 128, :]),
                        writes=["big"], dma=True)
                    p.emit("vector", lambda e, ti=ti: e.tensor_copy(
                        zacc[ti], xres), reads=["big"], writes=["zacc%d" % ti])
                p.emit("vector", lambda e, ps=ps, ti=ti, dg=dg, tt=tt: e.scalar_tensor_tensor(
                    zacc[ti][:, dg * 512:(dg + 1) * 512], ps[:],
                    S["fscl2"][:, tt:tt + 1], zacc[ti][:, dg * 512:(dg + 1) * 512],
                    ALU.mult, ALU.add),
                    reads=[pskey, "fscl2", "zacc%d" % ti], writes=["zacc%d" % ti])
        for ti in range(2):
            tt = tg * 2 + ti
            zti = zacc[ti]
            zkey = "zacc%d" % ti
            p.emit("vector", lambda e, zti=zti: e.reduce_sum(S["tmp_col"][:], zti, AX.X),
                   reads=[zkey], writes=["tmp_col"])
            p.emit("scalar", lambda e, zti=zti: e.activation(
                S["xb_row"][:], zti, AF.Square, accum_out=S["tmp2_col"][:]),
                reads=[zkey], writes=["xb_row", "tmp2_col"])
            p.emit("vector", lambda e: e.tensor_single_scalar(
                S["tmp_col"][:], S["tmp_col"][:], 1.0 / D, ALU.mult),
                reads=["tmp_col"], writes=["tmp_col"])
            p.emit("scalar", lambda e: e.activation(S["one_col"][:], S["tmp_col"][:], AF.Square),
                   reads=["tmp_col"], writes=["one_col"])
            p.emit("vector", lambda e: e.scalar_tensor_tensor(
                S["tmp2_col"][:], S["tmp2_col"][:], 1.0 / D, S["one_col"][:],
                ALU.mult, ALU.subtract),
                reads=["tmp2_col", "one_col"], writes=["tmp2_col"])
            p.emit("scalar", lambda e: e.activation(
                S["tmp2_col"][:], S["tmp2_col"][:], AF.Ln, bias=cst[:, C_EPS:C_EPS + 1], scale=1.0),
                reads=["tmp2_col", "cst"], writes=["tmp2_col"])
            p.emit("scalar", lambda e: e.activation(
                S["tmp2_col"][:], S["tmp2_col"][:], AF.Exp, bias=0.0,
                scale=cst[:, C_NEGHALF:C_NEGHALF + 1]),
                reads=["tmp2_col", "cst"], writes=["tmp2_col"])
            p.emit("vector", lambda e, zti=zti: e.tensor_scalar(
                zti, zti, S["tmp_col"][:], S["tmp2_col"][:], ALU.subtract, ALU.mult),
                reads=[zkey, "tmp_col", "tmp2_col"], writes=[zkey])
            p.emit("sync", lambda e, zti=zti, tt=tt: e.dma_start(
                out=dram["y"][tt * 128:(tt + 1) * 128, :], in_=zti),
                reads=[zkey], writes=["y_out"], dma=True)


def _in_maps_l1ffn(x_core, inputs, hc):
    m = dict(x=np.ascontiguousarray(x_core),
             wg=inputs["L1_ffn_w_gate"], wu=inputs["L1_ffn_w_up"],
             wd=inputs["L1_ffn_w_down"])
    m.update(hc)
    for k, v in _l1ffn_consts().items():
        m["hc_" + k] = v
    return m



_NC_CACHE = {}


def _get_nc(name):
    if name not in _NC_CACHE:
        _NC_CACHE[name] = {
            "l0a": build_l0_attn, "l0f": build_l0_ffn,
            "l1a": build_l1_attn, "l1f": build_l1_ffn,
        }[name]()
    return _NC_CACHE[name]


def kernel(**inputs):
    hc = _hc_maps()
    x = np.asarray(inputs["L0_attn_x"])
    cores = list(range(8))

    nc = _get_nc("l0a")
    maps = [_in_maps_l0attn(x[c], inputs, hc) for c in cores]
    res = run_bass_kernel_spmd(nc, maps, core_ids=cores)
    x1 = [r["y"] for r in res.results]

    nc = _get_nc("l0f")
    maps = [_in_maps_l0ffn(x1[c], inputs, hc) for c in cores]
    res = run_bass_kernel_spmd(nc, maps, core_ids=cores)
    x2 = [r["y"] for r in res.results]

    nc = _get_nc("l1a")
    maps = [_in_maps_l1attn(x2[c], inputs, hc) for c in cores]
    res = run_bass_kernel_spmd(nc, maps, core_ids=cores)
    x3 = [r["y"] for r in res.results]

    nc = _get_nc("l1f")
    maps = [_in_maps_l1ffn(x3[c], inputs, hc) for c in cores]
    res = run_bass_kernel_spmd(nc, maps, core_ids=cores)
    return np.stack([r["y"] for r in res.results]).astype(np.float32)


# version 06 (final)
# speedup vs baseline: 1.0589x; 1.0589x over previous
import contextlib
import math
import numpy as np
import ml_dtypes
import concourse.bass as bass
import concourse.mybir as mybir
from concourse.bass_utils import run_bass_kernel_spmd

F32 = mybir.dt.float32
BF16 = mybir.dt.bfloat16
I32 = mybir.dt.int32
U16 = mybir.dt.uint16
AF = mybir.ActivationFunctionType
ALU = mybir.AluOpType
AX = mybir.AxisListType
bf16 = ml_dtypes.bfloat16

T = 2048
D = 2048
NT = T // 128
ND = D // 128

_2PI = 2.0 * math.pi
CW1 = float(np.float32(_2PI))
CW2 = float(np.float32(_2PI - CW1))
CW3 = float(np.float32(_2PI - CW1 - CW2))

NF4_VALS = np.array([-1.0, -0.6962, -0.5251, -0.3949, -0.2844, -0.1848,
                     -0.0911, 0.0, 0.0796, 0.1609, 0.2461, 0.3379, 0.4407,
                     0.5626, 0.723, 1.0], dtype=bf16).astype(np.float64)
NF4_MIDS = ((NF4_VALS[:-1] + NF4_VALS[1:]) / 2.0).astype(np.float32)
NF4_DELTAS = np.diff(NF4_VALS).astype(np.float32)

C_EPS, C_NEGHALF, C_L0TH, C_L1TH, C_INVD, C_NEG1, C_HALF = range(7)
NCONST = 7


def host_consts():
    cst = np.zeros((128, NCONST), dtype=np.float32)
    cst[:, C_EPS] = 1e-5
    cst[:, C_NEGHALF] = -0.5
    cst[:, C_L0TH] = -1.0 / 64.0
    cst[:, C_L1TH] = -1.0 / 32.0
    cst[:, C_INVD] = 1.0 / 2048.0
    cst[:, C_NEG1] = -1.0
    cst[:, C_HALF] = 0.5
    it16 = (np.arange(128)[:, None] + 128 * np.arange(16)[None, :]).astype(np.int32)
    j64 = np.tile(np.arange(64, dtype=np.int32)[None, :], (128, 1))
    j32 = np.tile(np.arange(32, dtype=np.int32)[None, :], (128, 1))
    kk = np.arange(128)[:, None]
    cc = np.arange(896)[None, :]
    masks = (kk <= cc - 384).astype(bf16)
    ident = np.eye(128, dtype=bf16)
    ones_row = np.ones((1, 128), dtype=bf16)
    negones_row = -np.ones((1, 128), dtype=bf16)
    ones_col = np.ones((128, 1), dtype=bf16)
    return dict(cst=cst, it16=it16, j64=j64, j32=j32, masks=masks, ident=ident,
                ones_row=ones_row, negones_row=negones_row, ones_col=ones_col)


class Prog:

    ENGINES = ("sync", "vector", "scalar", "tensor", "gpsimd")

    def __init__(self, nc, es):
        self.nc = nc
        self.es = es
        self.steps = {e: [] for e in self.ENGINES}
        self.sems = {e: es.enter_context(nc.semaphore("S" + e)) for e in self.ENGINES}
        self.cnt = {e: 0 for e in self.ENGINES}
        self.wr = {}
        self.rd = {}
        self.alias = {}
        self.watermark = {}

    def _dma_sem(self, key):
        sk = "dma:" + key
        if sk not in self.sems:
            self.sems[sk] = self.es.enter_context(self.nc.semaphore("Sd_" + key))
            self.cnt[sk] = 0
        return sk

    def _waits_from(self, myeng, toks, waits):
        for sk, c in toks.items():
            if sk == myeng:
                pass
            waits[sk] = max(waits.get(sk, 0), c)

    def epoch(self):
        merged = {}
        for d in (self.wr, self.rd):
            for toks in d.values():
                for sk, c in toks.items():
                    merged[sk] = max(merged.get(sk, 0), c)
        self.floor = dict(merged)

    def emit(self, eng, fn, reads=(), writes=(), dma=False):
        al = self.alias
        reads = [al.get(b, b) for b in reads]
        writes = [al.get(b, b) for b in writes]
        waits = dict(getattr(self, "floor", {}))
        for b in reads:
            self._waits_from(eng, self.wr.get(b, {}), waits)
        for b in writes:
            self._waits_from(eng, self.wr.get(b, {}), waits)
            self._waits_from(eng, self.rd.get(b, {}), waits)
        if dma:
            semkey = self._dma_sem(writes[0])
            self.cnt[semkey] += 16
        else:
            semkey = eng
            self.cnt[semkey] += 1
        tok = (semkey, self.cnt[semkey])
        for b in reads:
            self.rd.setdefault(b, {})[tok[0]] = tok[1]
        for b in writes:
            self.wr.setdefault(b, {})[tok[0]] = tok[1]
            self.rd[b] = {}
        wm = self.watermark.setdefault(eng, {})
        wl = sorted((sk, v) for sk, v in waits.items() if v > wm.get(sk, 0))
        for sk, v in wl:
            wm[sk] = v
        sems = self.sems
        mysem = sems[semkey]
        amt = 16 if dma else 1

        def step(e, _fn=fn, _wl=wl, _amt=amt, _mysem=mysem):
            for sk, v in _wl:
                e.wait_ge(sems[sk], v)
            _fn(e).then_inc(_mysem, _amt)
        self.steps[eng].append(step)

    def alias_barrier(self, old_keys, new_keys):
        merged = {}
        for k in old_keys:
            for src in (self.wr.get(k, {}), self.rd.get(k, {})):
                for sk, c in src.items():
                    merged[sk] = max(merged.get(sk, 0), c)
        for nk in new_keys:
            self.wr.setdefault(nk, {}).update(merged)

    def finalize(self, block):
        final_waits = [(sk, self.cnt[sk]) for sk in self.sems
                       if sk.startswith("dma:") and self.cnt.get(sk, 0) > 0]
        for name in self.ENGINES:
            steps = self.steps[name]
            sem = self.sems[name]
            fw = final_waits if name == "sync" else None

            def body(e, steps=steps, fw=fw, sem=sem, name=name):
                for s in steps:
                    s(e)
                if fw is not None:
                    for sk, v in fw:
                        e.wait_ge(self.sems[sk], v)
            getattr(block, name)(body)



def emit_floor_pos(p, eng, out, src, key_out, key_src, tmp, key_tmp,
                   r, key_r, top_bit, nbits):
    p.emit(eng, lambda e: e.tensor_copy(r, src), reads=[key_src], writes=[key_r])
    for i in range(nbits):
        b = float(2 ** (top_bit - i))
        if i == 0:
            p.emit(eng, lambda e, b=b: e.tensor_scalar(out, r, b, b, ALU.is_ge, ALU.mult),
                   reads=[key_r], writes=[key_out])
            p.emit(eng, lambda e: e.tensor_tensor(r, r, out, ALU.subtract),
                   reads=[key_out], writes=[key_r])
        else:
            p.emit(eng, lambda e, b=b: e.tensor_scalar(tmp, r, b, b, ALU.is_ge, ALU.mult),
                   reads=[key_r], writes=[key_tmp])
            p.emit(eng, lambda e: e.tensor_tensor(out, out, tmp, ALU.add),
                   reads=[key_tmp], writes=[key_out])
            if i < nbits - 1:
                p.emit(eng, lambda e: e.tensor_tensor(r, r, tmp, ALU.subtract),
                       reads=[key_tmp], writes=[key_r])


class Tables:
    pass


def emit_prelude(p, S, need_l0_rope=False, need_l1_rope=False):
    dram = S["_dram"]
    for nm in ("cst", "it16", "j64", "j32", "masks", "ident", "ones_row",
               "negones_row", "ones_col"):
        if nm in S:
            p.emit("sync", lambda e, a=S[nm], b=dram[nm]: e.dma_start(out=a[:], in_=b),
                   writes=[nm], dma=True)
    cst = S["cst"]
    p.emit("vector", lambda e: e.memset(S["one_col"][:], 1.0), writes=["one_col"])
    p.emit("scalar", lambda e: e.activation(S["pi_col"][:], S["one_col"][:], AF.Arctan),
           reads=["one_col"], writes=["pi_col"])
    p.emit("vector", lambda e: e.tensor_single_scalar(S["pi_col"][:], S["pi_col"][:], 4.0, ALU.mult),
           reads=["pi_col"], writes=["pi_col"])
    p.emit("vector", lambda e: e.tensor_single_scalar(S["pih_col"][:], S["pi_col"][:], 0.5, ALU.mult),
           reads=["pi_col"], writes=["pih_col"])
    p.emit("vector", lambda e: e.tensor_single_scalar(S["negpi_col"][:], S["pi_col"][:], -1.0, ALU.mult),
           reads=["pi_col"], writes=["negpi_col"])
    p.emit("vector", lambda e: e.tensor_single_scalar(S["tmp_col"][:], S["pi_col"][:], 2.0, ALU.mult),
           reads=["pi_col"], writes=["tmp_col"])
    p.emit("vector", lambda e: e.reciprocal(S["r2pi_col"][:], S["tmp_col"][:]),
           reads=["tmp_col"], writes=["r2pi_col"])
    for nm, v in (("invsqrt128_col", 128.0), ("invsqrt64_col", 64.0),
                  ("invsqrt2048_col", 2048.0), ("invsqrt8192_col", 8192.0)):
        if nm not in S:
            continue
        p.emit("vector", lambda e, v=v: e.memset(S["tmp_col"][:], v), writes=["tmp_col"])
        p.emit("scalar", lambda e: e.activation(S["tmp2_col"][:], S["tmp_col"][:], AF.Sqrt),
               reads=["tmp_col"], writes=["tmp2_col"])
        p.emit("vector", lambda e, nm=nm: e.reciprocal(S[nm][:], S["tmp2_col"][:]),
               reads=["tmp2_col"], writes=[nm])
    p.emit("vector", lambda e: e.memset(S["tmp_col"][:], 10000.0), writes=["tmp_col"])
    p.emit("scalar", lambda e: e.activation(S["ln1e4_col"][:], S["tmp_col"][:], AF.Ln),
           reads=["tmp_col"], writes=["ln1e4_col"])

    def rope_tables(jname, cscale_idx, npairs, cos_name, sin_name, interleaved):
        HF = 8 * npairs
        ang, red, kk, tt2 = S["rt0"], S["rt1"], S["rt2"], S["rt3"]
        tmp, rr, sn, cs = S["rt4"], S["rt5"], S["rt6"], S["rt7"]
        p.emit("vector", lambda e: e.tensor_copy(S["rt_jf"][:, 0:npairs], S[jname][:, 0:npairs]),
               reads=[jname], writes=["rt_jf"])
        p.emit("vector", lambda e: e.tensor_scalar(
            S["lnsc_col"][:], S["ln1e4_col"][:], cst[:, cscale_idx:cscale_idx + 1], None, ALU.mult),
            reads=["ln1e4_col", "cst"], writes=["lnsc_col"])
        p.emit("scalar", lambda e: e.activation(S["rt_th"][:, 0:npairs], S["rt_jf"][:, 0:npairs],
                                                AF.Exp, bias=0.0, scale=S["lnsc_col"][:]),
               reads=["rt_jf", "lnsc_col"], writes=["rt_th"])
        p.emit("vector", lambda e: e.tensor_copy(S["rt_pos"][:], S["it16"][:]),
               reads=["it16"], writes=["rt_pos"])

        def sin_of(src_ap, out_ap):
            p.emit("vector", lambda e: e.tensor_scalar(
                tt2[:, 0:HF], src_ap, S["r2pi_col"][:], 0.5, ALU.mult, ALU.add),
                reads=["big", "r2pi_col"], writes=["big"])
            emit_floor_pos(p, "vector", kk[:, 0:HF], tt2[:, 0:HF], "big", "big",
                           tmp[:, 0:HF], "big", rr[:, 0:HF], "big",
                           top_bit=8, nbits=9)
            for c in (CW1, CW2, CW3):
                first = c is CW1
                p.emit("vector", lambda e, c=c, first=first: e.scalar_tensor_tensor(
                    red[:, 0:HF], kk[:, 0:HF], -c, (src_ap if first else red[:, 0:HF]),
                    ALU.mult, ALU.add),
                    reads=["big"], writes=["big"])
            p.emit("vector", lambda e: e.tensor_scalar(
                red[:, 0:HF], red[:, 0:HF], S["pi_col"][:], S["negpi_col"][:], ALU.min, ALU.max),
                reads=["big", "pi_col", "negpi_col"], writes=["big"])
            p.emit("scalar", lambda e, out_ap=out_ap: e.activation(out_ap, red[:, 0:HF], AF.Sin),
                   reads=["big"], writes=["big"])

        cosT, sinT = S[cos_name], S[sin_name]
        for half in (0, 1):
            t0 = 8 * half
            p.emit("vector", lambda e, t0=t0: e.tensor_tensor(
                ang[:, 0:HF].rearrange("q (t j) -> q t j", j=npairs),
                S["rt_pos"][:, t0:t0 + 8].rearrange("q (t u) -> q t u", u=1).broadcast_to([128, 8, npairs]),
                S["rt_th"][:, 0:npairs].rearrange("q (u j) -> q u j", u=1).broadcast_to([128, 8, npairs]),
                ALU.mult), reads=["rt_pos", "rt_th"], writes=["big"])
            sin_of(ang[:, 0:HF], sn[:, 0:HF])
            p.emit("vector", lambda e: e.tensor_scalar(
                ang[:, 0:HF], ang[:, 0:HF], S["pih_col"][:], None, ALU.add),
                reads=["big", "pih_col"], writes=["big"])
            sin_of(ang[:, 0:HF], cs[:, 0:HF])
            dh = npairs * 2
            cview = cosT[:, t0 * dh:(t0 + 8) * dh]
            sview = sinT[:, t0 * dh:(t0 + 8) * dh]
            pat = "q (t j two) -> q t j two" if interleaved else "q (t two j) -> q t two j"
            for sub, sgn in ((0, -1.0), (1, 1.0)):
                cdst = (cview.rearrange(pat, two=2, j=npairs)[:, :, :, sub]
                        if interleaved else
                        cview.rearrange(pat, two=2, j=npairs)[:, :, sub, :])
                sdst = (sview.rearrange(pat, two=2, j=npairs)[:, :, :, sub]
                        if interleaved else
                        sview.rearrange(pat, two=2, j=npairs)[:, :, sub, :])
                p.emit("vector", lambda e, cdst=cdst: e.tensor_copy(
                    cdst, cs[:, 0:HF].rearrange("q (t j) -> q t j", j=npairs)),
                    reads=["big"], writes=[cos_name])
                p.emit("vector", lambda e, sdst=sdst, sgn=sgn: e.tensor_single_scalar(
                    sdst, sn[:, 0:HF].rearrange("q (t j) -> q t j", j=npairs), sgn, ALU.mult),
                    reads=["big"], writes=[sin_name])

    if need_l0_rope:
        rope_tables("j64", C_L0TH, 64, "cosT0", "sinT0", interleaved=True)
    if need_l1_rope:
        rope_tables("j32", C_L1TH, 32, "cosT1", "sinT1", interleaved=False)



def _dram_consts(nc, dram):
    hc = host_consts()
    for nm, arr in hc.items():
        dt = {np.dtype(np.float32): F32, np.dtype(np.int32): I32,
              np.dtype(bf16): BF16}[arr.dtype]
        dram[nm] = nc.dram_tensor("hc_" + nm, list(arr.shape), dt,
                                  kind="ExternalInput").ap()


def _alloc_common(S, sb, big):
    sb("cst", (128, NCONST), F32)
    sb("it16", (128, 16), I32); sb("j64", (128, 64), I32); sb("j32", (128, 32), I32)
    sb("ident", (128, 128), BF16)
    sb("ones_row", (1, 128), BF16); sb("negones_row", (1, 128), BF16)
    sb("ones_col", (128, 1), BF16)
    for nm in ("one_col", "pi_col", "pih_col", "negpi_col", "r2pi_col",
               "ln1e4_col", "lnsc_col", "tmp_col", "tmp2_col"):
        sb(nm, (128, 1), F32)
    for i in range(8):
        S["rt%d" % i] = big[:, i * 512:(i + 1) * 512]
    sb("rt_jf", (128, 64), F32); sb("rt_th", (128, 64), F32)
    sb("rt_pos", (128, 16), F32)


_L0_ALIAS = {"xst": "big", "zt0": "big", "zt1": "big", "sqb": "attn_h",
             "rowf": "rowf", "rowb": "rowb"}


def emit_l0a(p, S, dram, big, psA, psB, psC, psAV, psSIG, psT):
    cst = S["cst"]
    xst = big[:, 0:2048]
    zt = [big[:, 0:2048], big[:, 2048:4096]]
    invd = S["invsqrt128_col"]

    def load_bias_row(nm):
        p.emit("sync", lambda e: e.dma_start(out=S["rowf"][:], in_=dram[nm]),
               writes=["rowf"], dma=True)
        p.emit("vector", lambda e: e.tensor_copy(S["rowb"][:], S["rowf"][:]),
               reads=["rowf"], writes=["rowb"])

    Xf = S["Xf"]
    for tt in range(NT):
        p.emit("sync", lambda e, tt=tt: e.dma_start(
            out=xst, in_=dram["x"][tt * 128:(tt + 1) * 128, :]),
            writes=["xst"], dma=True)
        p.emit("vector", lambda e: e.tensor_copy(S["xb_row"][:], xst),
               reads=["xst"], writes=["xb_row"])
        for dc in range(ND):
            p.emit("tensor", lambda e, dc=dc: e.transpose(
                psT[:], S["xb_row"][:, dc * 128:(dc + 1) * 128], S["ident"][:]),
                reads=["xb_row", "ident"], writes=["psT"])
            p.emit("vector", lambda e, dc=dc, tt=tt: e.tensor_copy(
                Xf[:, dc * T + tt * 128: dc * T + (tt + 1) * 128], psT[:]),
                reads=["psT"], writes=["Xf"])

    def load_w_slab(wname, col0):
        for dc in range(ND):
            p.emit("sync", lambda e, wname=wname, col0=col0, dc=dc: e.dma_start(
                out=S["wst"][:],
                in_=dram[wname][dc * 128:(dc + 1) * 128, col0:col0 + 512]),
                writes=["wst"], dma=True)
            p.emit("vector", lambda e, dc=dc: e.tensor_copy(
                S["wslab"][:, dc * 512:(dc + 1) * 512], S["wst"][:]),
                reads=["wst"], writes=["wslab"])

    for hg in range(4):
        fg0 = hg * 512
        for wname, bnm, ps, pskey, kind in (
                ("Wq", "bq", psA, "psA", "q"),
                ("Wk", "bk", psB, "psB", "k"),
                ("Wv", "bv", psC, "psC", "v")):
            load_w_slab(wname, fg0)
            load_bias_row(bnm)
            for tt in range(NT):
                for dc in range(ND):
                    p.emit("tensor", lambda e, ps=ps, dc=dc, tt=tt: e.matmul(
                        ps[:], Xf[:, dc * T + tt * 128: dc * T + (tt + 1) * 128],
                        S["wslab"][:, dc * 512:(dc + 1) * 512],
                        start=(dc == 0), stop=False),
                        reads=["Xf", "wslab"], writes=[pskey])
                p.emit("tensor", lambda e, ps=ps, fg0=fg0: e.matmul(
                    ps[:], S["ones_row"][0:1, :], S["rowb"][0:1, fg0:fg0 + 512],
                    start=False, stop=True),
                    reads=["ones_row", "rowb"], writes=[pskey])
                if kind == "v":
                    p.emit("scalar", lambda e, ps=ps, tt=tt: e.activation(
                        S["Vb"][:, tt * 512:(tt + 1) * 512], ps[:], AF.Copy),
                        reads=[pskey], writes=["Vb"])
                else:
                    p.emit("scalar", lambda e, ps=ps: e.activation(
                        S["qsc"][:], ps[:], AF.Copy),
                        reads=[pskey], writes=["qsc"])
                    ct = S["cosT0"][:, tt * 128:(tt + 1) * 128]
                    st = S["sinT0"][:, tt * 128:(tt + 1) * 128]
                    q4 = S["qsc"][:].rearrange("p (h j two) -> p h j two", two=2, j=64)
                    r4 = S["rot"][:].rearrange("p (h j two) -> p h j two", two=2, j=64)
                    st4 = st.rearrange("q (u d) -> q u d", u=1).broadcast_to([128, 4, 128]) \
                            .rearrange("q h (j two) -> q h j two", two=2)
                    ct4 = ct.rearrange("q (u d) -> q u d", u=1).broadcast_to([128, 4, 128]) \
                            .rearrange("q h (j two) -> q h j two", two=2)
                    p.emit("vector", lambda e, q4=q4, r4=r4, st4=st4: e.tensor_tensor(
                        r4[:, :, :, 0], q4[:, :, :, 1], st4[:, :, :, 0], ALU.mult),
                        reads=["qsc", "sinT0"], writes=["rot"])
                    p.emit("vector", lambda e, q4=q4, r4=r4, st4=st4: e.tensor_tensor(
                        r4[:, :, :, 1], q4[:, :, :, 0], st4[:, :, :, 1], ALU.mult),
                        reads=["qsc", "sinT0"], writes=["rot"])
                    p.emit("vector", lambda e, q4=q4, ct4=ct4: e.tensor_tensor(
                        S["tmp512"][:].rearrange("p (h j two) -> p h j two", two=2, j=64),
                        q4, ct4, ALU.mult),
                        reads=["qsc", "cosT0"], writes=["tmp512"])
                    p.emit("vector", lambda e: e.tensor_tensor(
                        S["ropebuf"][:], S["tmp512"][:], S["rot"][:], ALU.add),
                        reads=["tmp512", "rot"], writes=["ropebuf"])
                    dstf, dkey = (S["QTf"], "QTf") if kind == "q" else (S["KTf"], "KTf")
                    for hl in range(4):
                        p.emit("tensor", lambda e, hl=hl: e.transpose(
                            psT[:], S["ropebuf"][:, hl * 128:(hl + 1) * 128], S["ident"][:]),
                            reads=["ropebuf", "ident"], writes=["psT"])
                        p.emit("vector", lambda e, dstf=dstf, hl=hl, tt=tt: e.tensor_copy(
                            dstf[:, hl * T + tt * 128: hl * T + (tt + 1) * 128], psT[:]),
                            reads=["psT"], writes=[dkey])
        for hl in range(4):
            QTh = S["QTf"][:, hl * T:(hl + 1) * T]
            KTh = S["KTf"][:, hl * T:(hl + 1) * T]
            p.emit("scalar", lambda e, KTh=KTh: e.activation(S["sqb"][:], KTh, AF.Square),
                   reads=["KTf"], writes=["sqb"])
            for qt in range(4):
                p.emit("tensor", lambda e, qt=qt: e.matmul(
                    psSIG[0:1, :], S["ones_col"][:, 0:1], S["sqb"][:, qt * 512:(qt + 1) * 512],
                    start=True, stop=True),
                    reads=["ones_col", "sqb"], writes=["psSIG"])
                p.emit("vector", lambda e, qt=qt: e.tensor_copy(
                    S["rowf"][0:1, qt * 512:(qt + 1) * 512], psSIG[0:1, :]),
                    reads=["psSIG"], writes=["rowf"])
            p.emit("scalar", lambda e: e.activation(S["rowf"][0:1, :], S["rowf"][0:1, :], AF.Sqrt),
                   reads=["rowf"], writes=["rowf"])
            p.emit("vector", lambda e: e.reduce_max(S["kmax"][0:1, 0:1], S["rowf"][0:1, :], AX.X),
                   reads=["rowf"], writes=["kmax"])
            p.emit("scalar", lambda e, QTh=QTh: e.activation(S["sqb"][:], QTh, AF.Square),
                   reads=["QTf"], writes=["sqb"])
            for qt in range(4):
                p.emit("tensor", lambda e, qt=qt: e.matmul(
                    psSIG[0:1, :], S["ones_col"][:, 0:1], S["sqb"][:, qt * 512:(qt + 1) * 512],
                    start=True, stop=True),
                    reads=["ones_col", "sqb"], writes=["psSIG"])
                p.emit("vector", lambda e, qt=qt: e.tensor_copy(
                    S["rowf"][0:1, qt * 512:(qt + 1) * 512], psSIG[0:1, :]),
                    reads=["psSIG"], writes=["rowf"])
            p.emit("scalar", lambda e: e.activation(S["rowf"][0:1, :], S["rowf"][0:1, :], AF.Sqrt),
                   reads=["rowf"], writes=["rowf"])
            p.emit("vector", lambda e: e.tensor_scalar(
                S["rowf"][0:1, :], S["rowf"][0:1, :], S["kmax"][0:1, 0:1], None, ALU.mult),
                reads=["rowf", "kmax"], writes=["rowf"])
            p.emit("vector", lambda e: e.tensor_copy(S["rowb"][0:1, :], S["rowf"][0:1, :]),
                   reads=["rowf"], writes=["rowb"])
            for qt in range(4):
                ktmax = 4 * (qt + 1)
                for kt in range(ktmax):
                    ps, pskey = (psA, "psA") if kt % 2 == 0 else (psB, "psB")
                    slot = kt % 2
                    ptk = "PT%d" % slot
                    pts = S["PT"][:, slot * 512:(slot + 1) * 512]
                    p.emit("tensor", lambda e, ps=ps, KTh=KTh, QTh=QTh, kt=kt, qt=qt: e.matmul(
                        ps[:], KTh[:, kt * 128:(kt + 1) * 128],
                        QTh[:, qt * 512:(qt + 1) * 512],
                        start=True, stop=False),
                        reads=["KTf", "QTf"], writes=[pskey])
                    p.emit("tensor", lambda e, ps=ps, qt=qt: e.matmul(
                        ps[:], S["negones_row"][0:1, :], S["rowb"][0:1, qt * 512:(qt + 1) * 512],
                        start=False, stop=True),
                        reads=["negones_row", "rowb"], writes=[pskey])
                    p.emit("scalar", lambda e, ps=ps, pts=pts: e.activation(
                        pts, ps[:], AF.Exp, bias=0.0, scale=invd[:]),
                        reads=[pskey, "invsqrt128_col"], writes=[ptk])
                    if kt >= 4 * qt:
                        j = kt - 4 * qt
                        p.emit("vector", lambda e, pts=pts, j=j: e.tensor_tensor(
                            pts, pts, S["masks"][:, 384 - 128 * j: 896 - 128 * j], ALU.mult),
                            reads=[ptk, "masks"], writes=[ptk])
                    p.emit("tensor", lambda e, pts=pts, first=(kt == 0), last=(kt == ktmax - 1): e.matmul(
                        psSIG[0:1, :], S["ones_col"][:, 0:1], pts,
                        start=first, stop=last),
                        reads=["ones_col", ptk], writes=["psSIG"])
                    p.emit("tensor", lambda e, pts=pts, kt=kt, hl=hl, first=(kt == 0), last=(kt == ktmax - 1): e.matmul(
                        psAV[:], S["Vb"][:, kt * 512 + hl * 128: kt * 512 + (hl + 1) * 128],
                        pts, start=first, stop=last),
                        reads=["Vb", ptk], writes=["psAV"])
                p.emit("vector", lambda e: e.reciprocal(S["rs_row"][0:1, :], psSIG[0:1, :]),
                       reads=["psSIG"], writes=["rs_row"])
                p.emit("vector", lambda e: e.tensor_copy(S["rsb"][0:1, :], S["rs_row"][0:1, :]),
                       reads=["rs_row"], writes=["rsb"])
                p.emit("tensor", lambda e: e.matmul(
                    psC[:], S["ones_row"][0:1, :], S["rsb"][0:1, :], start=True, stop=True),
                    reads=["ones_row", "rsb"], writes=["psC"])
                p.emit("vector", lambda e: e.tensor_copy(S["sigb"][:], psC[:]),
                       reads=["psC"], writes=["sigb"])
                p.emit("vector", lambda e, qt=qt: e.tensor_tensor(
                    S["attn_h"][:, qt * 512:(qt + 1) * 512], psAV[:], S["sigb"][:], ALU.mult),
                    reads=["psAV", "sigb"], writes=["attn_h"])
            h = hg * 4 + hl
            p.emit("sync", lambda e, h=h: e.dma_start(
                out=dram["attn_scr"][h * 128:(h + 1) * 128, :], in_=S["attn_h"][:]),
                reads=["attn_h"], writes=["attn_scr"], dma=True)

    load_bias_row("bo")
    for tg in range(8):
        p.emit("sync", lambda e, tg=tg: e.dma_start(
            out=S["Vb"][:, 0:ND * 256].rearrange("p (c q) -> p c q", q=256),
            in_=dram["attn_scr"].rearrange("(c p) q -> p c q", p=128)[:, :, tg * 256:(tg + 1) * 256]),
            reads=["attn_scr"], writes=["Vb"], dma=True)
        for fg in range(4):
            load_w_slab("Wo", fg * 512)
            for ti in range(2):
                ps, pskey = (psA, "psA") if ti % 2 == 0 else (psB, "psB")
                for dc in range(ND):
                    p.emit("tensor", lambda e, ps=ps, dc=dc, ti=ti: e.matmul(
                        ps[:], S["Vb"][:, dc * 256 + ti * 128: dc * 256 + (ti + 1) * 128],
                        S["wslab"][:, dc * 512:(dc + 1) * 512],
                        start=(dc == 0), stop=False),
                        reads=["Vb", "wslab"], writes=[pskey])
                p.emit("tensor", lambda e, ps=ps, fg=fg: e.matmul(
                    ps[:], S["ones_row"][0:1, :], S["rowb"][0:1, fg * 512:(fg + 1) * 512],
                    start=False, stop=True),
                    reads=["ones_row", "rowb"], writes=[pskey])
                p.emit("vector", lambda e, ps=ps, ti=ti, fg=fg: e.tensor_copy(
                    zt[ti][:, fg * 512:(fg + 1) * 512], ps[:]),
                    reads=[pskey], writes=["zt%d" % ti])
        for ti in range(2):
            tt = tg * 2 + ti
            zti = zt[ti]
            zkey = "zt%d" % ti
            p.emit("sync", lambda e, tt=tt: e.dma_start(
                out=S["xres"][:], in_=dram["x"][tt * 128:(tt + 1) * 128, :]),
                writes=["xres"], dma=True)
            p.emit("vector", lambda e, zti=zti: e.tensor_tensor(
                zti, zti, S["xres"][:], ALU.add),
                reads=[zkey, "xres"], writes=[zkey])
            p.emit("vector", lambda e, zti=zti: e.reduce_sum(S["tmp_col"][:], zti, AX.X),
                   reads=[zkey], writes=["tmp_col"])
            p.emit("scalar", lambda e, zti=zti: e.activation(
                S["lnjunk"][:], zti, AF.Square, accum_out=S["tmp2_col"][:]),
                reads=[zkey], writes=["lnjunk", "tmp2_col"])
            p.emit("vector", lambda e: e.tensor_single_scalar(
                S["tmp_col"][:], S["tmp_col"][:], 1.0 / D, ALU.mult),
                reads=["tmp_col"], writes=["tmp_col"])
            p.emit("scalar", lambda e: e.activation(S["one_col"][:], S["tmp_col"][:], AF.Square),
                   reads=["tmp_col"], writes=["one_col"])
            p.emit("vector", lambda e: e.scalar_tensor_tensor(
                S["tmp2_col"][:], S["tmp2_col"][:], 1.0 / D, S["one_col"][:],
                ALU.mult, ALU.subtract),
                reads=["tmp2_col", "one_col"], writes=["tmp2_col"])
            p.emit("scalar", lambda e: e.activation(
                S["tmp2_col"][:], S["tmp2_col"][:], AF.Ln, bias=cst[:, C_EPS:C_EPS + 1], scale=1.0),
                reads=["tmp2_col", "cst"], writes=["tmp2_col"])
            p.emit("scalar", lambda e: e.activation(
                S["tmp2_col"][:], S["tmp2_col"][:], AF.Exp, bias=0.0,
                scale=cst[:, C_NEGHALF:C_NEGHALF + 1]),
                reads=["tmp2_col", "cst"], writes=["tmp2_col"])
            p.emit("vector", lambda e, zti=zti: e.tensor_scalar(
                zti, zti, S["tmp_col"][:], S["tmp2_col"][:], ALU.subtract, ALU.mult),
                reads=[zkey, "tmp_col", "tmp2_col"], writes=[zkey])
            p.emit("sync", lambda e, zti=zti, tt=tt: e.dma_start(
                out=dram["y"][tt * 128:(tt + 1) * 128, :], in_=zti),
                reads=[zkey], writes=["y_out"], dma=True)




def build_l0_attn():
    nc = bass.Bass("TRN2", target_bir_lowering=False)
    es = contextlib.ExitStack()
    dram = {}
    dram["x"] = nc.dram_tensor("x", [T, D], F32, kind="ExternalInput").ap()
    for nm in ("Wq", "Wk", "Wv", "Wo"):
        dram[nm] = nc.dram_tensor(nm, [D, D], F32, kind="ExternalInput").ap()
    for nm in ("bq", "bk", "bv", "bo"):
        dram[nm] = nc.dram_tensor(nm, [1, D], F32, kind="ExternalInput").ap()
    _dram_consts(nc, dram)
    dram["y"] = nc.dram_tensor("y", [T, D], F32, kind="ExternalOutput").ap()
    dram["attn_scr"] = nc.dram_tensor("attn_scr", [D, T], BF16, kind="Internal").ap()

    S = {"_dram": dram}

    def sb(name, shape, dt):
        S[name] = es.enter_context(nc.sbuf_tensor("s_" + name, list(shape), dt))
        return S[name]

    bigh = es.enter_context(nc.sbuf_tensor("s_big", [128, 4096], F32))
    big = bigh[:]
    _alloc_common(S, sb, big)
    sb("invsqrt128_col", (128, 1), F32)
    sb("masks", (128, 896), BF16)
    sb("cosT0", (128, NT * 128), BF16); sb("sinT0", (128, NT * 128), BF16)
    sb("Xf", (128, ND * T), BF16)
    sb("xb_row", (128, D), BF16)
    S["lnjunk"] = S["xb_row"]
    sb("wst", (128, 512), F32)
    sb("wslab", (128, ND * 512), BF16)
    sb("rowf", (1, D), F32)
    sb("rowb", (1, D), BF16)
    sb("qsc", (128, 512), BF16); sb("rot", (128, 512), BF16)
    sb("tmp512", (128, 512), BF16); sb("ropebuf", (128, 512), BF16)
    sb("QTf", (128, 4 * T), BF16); sb("KTf", (128, 4 * T), BF16)
    sb("Vb", (128, NT * 512), BF16)
    sb("kmax", (1, 1), F32)
    sb("PT", (128, 2 * 512), BF16)
    sb("rs_row", (1, 512), F32); sb("rsb", (1, 512), BF16)
    sb("sigb", (128, 512), BF16)
    sb("attn_h", (128, T), BF16)
    S["sqb"] = S["attn_h"]
    sb("xres", (128, D), F32)

    psA = es.enter_context(nc.psum_tensor("psA", [128, 512], F32))
    psB = es.enter_context(nc.psum_tensor("psB", [128, 512], F32))
    psC = es.enter_context(nc.psum_tensor("psC", [128, 512], F32))
    psAV = es.enter_context(nc.psum_tensor("psAV", [128, 512], F32))
    psSIG = es.enter_context(nc.psum_tensor("psSIG", [1, 512], F32))
    psT = es.enter_context(nc.psum_tensor("psT", [128, 128], BF16))

    with es:
        p = Prog(nc, es)
        p.alias = dict(_L0_ALIAS)
        p.alias["lnjunk"] = "xb_row"
        emit_prelude(p, S, need_l0_rope=True)

        emit_l0a(p, S, dram, big, psA, psB, psC, psAV, psSIG, psT)
        with nc.Block() as block:
            p.finalize(block)
    return nc


def _hc_maps():
    hc = host_consts()
    return {("hc_" + k): v for k, v in hc.items()}


def _in_maps_l0attn(x_core, inputs, hc):
    m = dict(x=np.ascontiguousarray(x_core),
             Wq=inputs["L0_attn_Wq"], Wk=inputs["L0_attn_Wk"],
             Wv=inputs["L0_attn_Wv"], Wo=inputs["L0_attn_Wo"],
             bq=inputs["L0_attn_bq"].reshape(1, D),
             bk=inputs["L0_attn_bk"].reshape(1, D),
             bv=inputs["L0_attn_bv"].reshape(1, D),
             bo=inputs["L0_attn_bo"].reshape(1, D))
    m.update(hc)
    return m



def emit_l1a(p, S, dram, big, psA, psB, psC, psAV, psSIG, psT):
    cst = S["cst"]
    xst = big[:, 0:2048]
    zt = [big[:, 0:2048], big[:, 2048:4096]]
    invd = S["invsqrt64_col"]

    Xf = S["Xf"]
    for tt in range(NT):
        p.emit("sync", lambda e, tt=tt: e.dma_start(
            out=xst, in_=dram["x"][tt * 128:(tt + 1) * 128, :]),
            writes=["xst"], dma=True)
        p.emit("vector", lambda e: e.tensor_copy(S["xb_row"][:], xst),
               reads=["xst"], writes=["xb_row"])
        for dc in range(ND):
            p.emit("tensor", lambda e, dc=dc: e.transpose(
                psT[:], S["xb_row"][:, dc * 128:(dc + 1) * 128], S["ident"][:]),
                reads=["xb_row", "ident"], writes=["psT"])
            p.emit("vector", lambda e, dc=dc, tt=tt: e.tensor_copy(
                Xf[:, dc * T + tt * 128: dc * T + (tt + 1) * 128], psT[:]),
                reads=["psT"], writes=["Xf"])

    def load_w_slab(wname, col0):
        for dc in range(ND):
            p.emit("sync", lambda e, wname=wname, col0=col0, dc=dc: e.dma_start(
                out=S["wst"][:],
                in_=dram[wname][dc * 128:(dc + 1) * 128, col0:col0 + 512]),
                writes=["wst"], dma=True)
            p.emit("vector", lambda e, dc=dc: e.tensor_copy(
                S["wslab"][:, dc * 512:(dc + 1) * 512], S["wst"][:]),
                reads=["wst"], writes=["wslab"])

    def project_group(wname, ps, pskey, dstf, dkey, col0, do_rope, to_vb=False, is_q=False):
        load_w_slab(wname, col0)
        for tt in range(NT):
            for dc in range(ND):
                p.emit("tensor", lambda e, ps=ps, dc=dc, tt=tt: e.matmul(
                    ps[:], Xf[:, dc * T + tt * 128: dc * T + (tt + 1) * 128],
                    S["wslab"][:, dc * 512:(dc + 1) * 512],
                    start=(dc == 0), stop=(dc == ND - 1)),
                    reads=["Xf", "wslab"], writes=[pskey])
            if to_vb:
                p.emit("scalar", lambda e, ps=ps, tt=tt: e.activation(
                    S["Vb"][:, tt * 512:(tt + 1) * 512], ps[:], AF.Copy),
                    reads=[pskey], writes=["Vb"])
                continue
            p.emit("scalar", lambda e, ps=ps: e.activation(S["qsc"][:], ps[:], AF.Copy),
                   reads=[pskey], writes=["qsc"])
            ct = S["cosT1"][:, tt * 64:(tt + 1) * 64]
            st = S["sinT1"][:, tt * 64:(tt + 1) * 64]
            q4 = S["qsc"][:].rearrange("p (h two j) -> p h two j", two=2, j=32)
            r4 = S["rot"][:].rearrange("p (h two j) -> p h two j", two=2, j=32)
            ct4 = ct.rearrange("q (u d) -> q u d", u=1).broadcast_to([128, 8, 64]) \
                    .rearrange("q h (two j) -> q h two j", two=2)
            st4 = st.rearrange("q (u d) -> q u d", u=1).broadcast_to([128, 8, 64]) \
                    .rearrange("q h (two j) -> q h two j", two=2)
            p.emit("vector", lambda e, q4=q4, r4=r4, st4=st4: e.tensor_tensor(
                r4[:, :, 0, :], q4[:, :, 1, :], st4[:, :, 0, :], ALU.mult),
                reads=["qsc", "sinT1"], writes=["rot"])
            p.emit("vector", lambda e, q4=q4, r4=r4, st4=st4: e.tensor_tensor(
                r4[:, :, 1, :], q4[:, :, 0, :], st4[:, :, 1, :], ALU.mult),
                reads=["qsc", "sinT1"], writes=["rot"])
            p.emit("vector", lambda e, q4=q4, ct4=ct4: e.tensor_tensor(
                S["tmp512"][:].rearrange("p (h two j) -> p h two j", two=2, j=32),
                q4, ct4, ALU.mult),
                reads=["qsc", "cosT1"], writes=["tmp512"])
            p.emit("vector", lambda e: e.tensor_tensor(
                S["ropebuf"][:], S["tmp512"][:], S["rot"][:], ALU.add),
                reads=["tmp512", "rot"], writes=["ropebuf"])
            for hl in range(8):
                if is_q:
                    pbase, cblk = (hl // 4) * 64, hl % 4
                else:
                    pbase, cblk = (hl % 2) * 64, hl // 2
                p.emit("tensor", lambda e, hl=hl: e.transpose(
                    psT[0:64, :], S["ropebuf"][:, hl * 64:(hl + 1) * 64], S["ident"][:]),
                    reads=["ropebuf", "ident"], writes=["psT"])
                p.emit("vector", lambda e, dstf=dstf, pbase=pbase, cblk=cblk, tt=tt: e.tensor_copy(
                    dstf[pbase:pbase + 64, cblk * T + tt * 128: cblk * T + (tt + 1) * 128],
                    psT[0:64, :]),
                    reads=["psT"], writes=[dkey])

    project_group("Wk", psB, "psB", S["KTf"], "KTf", 0, True)
    project_group("Wv", psC, "psC", None, None, 0, False, to_vb=True)

    load_bias = None
    for qg in range(4):
        project_group("Wq", psA, "psA", S["QTf"], "QTf", qg * 512, True, is_q=True)
        for hl in range(8):
            h = qg * 8 + hl
            kv = h // 4
            qpb = (hl // 4) * 64
            qcb = hl % 4
            kpb = (kv % 2) * 64
            kcb = kv // 2
            assert qpb == kpb, (h, qpb, kpb)
            QTh = S["QTf"][qpb:qpb + 64, qcb * T:(qcb + 1) * T]
            KTh = S["KTf"][kpb:kpb + 64, kcb * T:(kcb + 1) * T]
            onescol64 = S["ones_col"][qpb:qpb + 64, 0:1]
            konescol = S["ones_col"][kpb:kpb + 64, 0:1]
            p.emit("scalar", lambda e, KTh=KTh, kpb=kpb: e.activation(
                S["sqb"][kpb:kpb + 64, :], KTh, AF.Square),
                reads=["KTf"], writes=["attn_h"])
            for qt in range(4):
                p.emit("tensor", lambda e, qt=qt, kpb=kpb, konescol=konescol: e.matmul(
                    psSIG[0:1, :], konescol, S["sqb"][kpb:kpb + 64, qt * 512:(qt + 1) * 512],
                    start=True, stop=True),
                    reads=["ones_col", "attn_h"], writes=["psSIG"])
                p.emit("vector", lambda e, qt=qt: e.tensor_copy(
                    S["rowf"][0:1, qt * 512:(qt + 1) * 512], psSIG[0:1, :]),
                    reads=["psSIG"], writes=["rowf"])
            p.emit("scalar", lambda e: e.activation(S["rowf"][0:1, :], S["rowf"][0:1, :], AF.Sqrt),
                   reads=["rowf"], writes=["rowf"])
            p.emit("vector", lambda e: e.reduce_max(S["kmax"][0:1, 0:1], S["rowf"][0:1, :], AX.X),
                   reads=["rowf"], writes=["kmax"])
            p.emit("scalar", lambda e, QTh=QTh, qpb=qpb: e.activation(
                S["sqb"][qpb:qpb + 64, :], QTh, AF.Square),
                reads=["QTf"], writes=["attn_h"])
            for qt in range(4):
                p.emit("tensor", lambda e, qt=qt, qpb=qpb, onescol64=onescol64: e.matmul(
                    psSIG[0:1, :], onescol64, S["sqb"][qpb:qpb + 64, qt * 512:(qt + 1) * 512],
                    start=True, stop=True),
                    reads=["ones_col", "attn_h"], writes=["psSIG"])
                p.emit("vector", lambda e, qt=qt: e.tensor_copy(
                    S["rowf"][0:1, qt * 512:(qt + 1) * 512], psSIG[0:1, :]),
                    reads=["psSIG"], writes=["rowf"])
            p.emit("scalar", lambda e: e.activation(S["rowf"][0:1, :], S["rowf"][0:1, :], AF.Sqrt),
                   reads=["rowf"], writes=["rowf"])
            p.emit("vector", lambda e: e.tensor_scalar(
                S["rowf"][0:1, :], S["rowf"][0:1, :], S["kmax"][0:1, 0:1], None, ALU.mult),
                reads=["rowf", "kmax"], writes=["rowf"])
            p.emit("vector", lambda e: e.tensor_copy(S["rowb"][0:1, :], S["rowf"][0:1, :]),
                   reads=["rowf"], writes=["rowb"])
            for qt in range(4):
                ktmax = 4 * (qt + 1)
                for kt in range(ktmax):
                    ps, pskey = (psA, "psA") if kt % 2 == 0 else (psB, "psB")
                    slot = kt % 2
                    ptk = "PT%d" % slot
                    pts = S["PT"][:, slot * 512:(slot + 1) * 512]
                    p.emit("tensor", lambda e, ps=ps, KTh=KTh, QTh=QTh, kt=kt, qt=qt: e.matmul(
                        ps[:], KTh[:, kt * 128:(kt + 1) * 128],
                        QTh[:, qt * 512:(qt + 1) * 512],
                        start=True, stop=False),
                        reads=["KTf", "QTf"], writes=[pskey])
                    p.emit("tensor", lambda e, ps=ps, qt=qt: e.matmul(
                        ps[:], S["negones_row"][0:1, :], S["rowb"][0:1, qt * 512:(qt + 1) * 512],
                        start=False, stop=True),
                        reads=["negones_row", "rowb"], writes=[pskey])
                    p.emit("scalar", lambda e, ps=ps, pts=pts: e.activation(
                        pts, ps[:], AF.Exp, bias=0.0, scale=invd[:]),
                        reads=[pskey, "invsqrt64_col"], writes=[ptk])
                    if kt >= 4 * qt:
                        j = kt - 4 * qt
                        p.emit("vector", lambda e, pts=pts, j=j: e.tensor_tensor(
                            pts, pts, S["masks"][:, 384 - 128 * j: 896 - 128 * j], ALU.mult),
                            reads=[ptk, "masks"], writes=[ptk])
                    p.emit("tensor", lambda e, pts=pts, first=(kt == 0), last=(kt == ktmax - 1): e.matmul(
                        psSIG[0:1, :], S["ones_col"][:, 0:1], pts,
                        start=first, stop=last),
                        reads=["ones_col", ptk], writes=["psSIG"])
                    p.emit("tensor", lambda e, pts=pts, kt=kt, kv=kv, first=(kt == 0), last=(kt == ktmax - 1): e.matmul(
                        psAV[0:64, :], S["Vb"][:, kt * 512 + kv * 64: kt * 512 + (kv + 1) * 64],
                        pts, start=first, stop=last),
                        reads=["Vb", ptk], writes=["psAV"])
                p.emit("vector", lambda e: e.reciprocal(S["rs_row"][0:1, :], psSIG[0:1, :]),
                       reads=["psSIG"], writes=["rs_row"])
                p.emit("vector", lambda e: e.tensor_copy(S["rsb"][0:1, :], S["rs_row"][0:1, :]),
                       reads=["rs_row"], writes=["rsb"])
                p.emit("tensor", lambda e: e.matmul(
                    psC[0:64, :], S["ones_row"][0:1, 0:64], S["rsb"][0:1, :], start=True, stop=True),
                    reads=["ones_row", "rsb"], writes=["psC"])
                p.emit("vector", lambda e: e.tensor_copy(S["sigb"][0:64, :], psC[0:64, :]),
                       reads=["psC"], writes=["sigb"])
                p.emit("vector", lambda e, qt=qt: e.tensor_tensor(
                    S["attn_h"][0:64, qt * 512:(qt + 1) * 512], psAV[0:64, :],
                    S["sigb"][0:64, :], ALU.mult),
                    reads=["psAV", "sigb"], writes=["attn_h"])
            p.emit("sync", lambda e, h=h: e.dma_start(
                out=dram["attn_scr"][h * 64:(h + 1) * 64, :], in_=S["attn_h"][0:64, :]),
                reads=["attn_h"], writes=["attn_scr"], dma=True)

    p.emit("sync", lambda e: e.dma_start(out=S["rowf"][:], in_=dram["bo"]),
           writes=["rowf"], dma=True)
    p.emit("vector", lambda e: e.tensor_copy(S["rowb"][:], S["rowf"][:]),
           reads=["rowf"], writes=["rowb"])
    for tg in range(8):
        p.emit("sync", lambda e, tg=tg: e.dma_start(
            out=S["Vb"][:, 0:ND * 256].rearrange("p (c q) -> p c q", q=256),
            in_=dram["attn_scr"].rearrange("(c p) q -> p c q", p=128)[:, :, tg * 256:(tg + 1) * 256]),
            reads=["attn_scr"], writes=["Vb"], dma=True)
        for fg in range(4):
            load_w_slab("Wo", fg * 512)
            for ti in range(2):
                ps, pskey = (psA, "psA") if ti % 2 == 0 else (psB, "psB")
                for dc in range(ND):
                    p.emit("tensor", lambda e, ps=ps, dc=dc, ti=ti: e.matmul(
                        ps[:], S["Vb"][:, dc * 256 + ti * 128: dc * 256 + (ti + 1) * 128],
                        S["wslab"][:, dc * 512:(dc + 1) * 512],
                        start=(dc == 0), stop=False),
                        reads=["Vb", "wslab"], writes=[pskey])
                p.emit("tensor", lambda e, ps=ps, fg=fg: e.matmul(
                    ps[:], S["ones_row"][0:1, :], S["rowb"][0:1, fg * 512:(fg + 1) * 512],
                    start=False, stop=True),
                    reads=["ones_row", "rowb"], writes=[pskey])
                p.emit("vector", lambda e, ps=ps, ti=ti, fg=fg: e.tensor_copy(
                    zt[ti][:, fg * 512:(fg + 1) * 512], ps[:]),
                    reads=[pskey], writes=["zt%d" % ti])
        for ti in range(2):
            tt = tg * 2 + ti
            zti = zt[ti]
            zkey = "zt%d" % ti
            p.emit("sync", lambda e, tt=tt: e.dma_start(
                out=S["xres"][:], in_=dram["x"][tt * 128:(tt + 1) * 128, :]),
                writes=["xres"], dma=True)
            p.emit("vector", lambda e, zti=zti: e.tensor_tensor(
                zti, zti, S["xres"][:], ALU.add),
                reads=[zkey, "xres"], writes=[zkey])
            p.emit("vector", lambda e, zti=zti: e.reduce_sum(S["tmp_col"][:], zti, AX.X),
                   reads=[zkey], writes=["tmp_col"])
            p.emit("scalar", lambda e, zti=zti: e.activation(
                S["lnjunk"][:], zti, AF.Square, accum_out=S["tmp2_col"][:]),
                reads=[zkey], writes=["lnjunk", "tmp2_col"])
            p.emit("vector", lambda e: e.tensor_single_scalar(
                S["tmp_col"][:], S["tmp_col"][:], 1.0 / D, ALU.mult),
                reads=["tmp_col"], writes=["tmp_col"])
            p.emit("scalar", lambda e: e.activation(S["one_col"][:], S["tmp_col"][:], AF.Square),
                   reads=["tmp_col"], writes=["one_col"])
            p.emit("vector", lambda e: e.scalar_tensor_tensor(
                S["tmp2_col"][:], S["tmp2_col"][:], 1.0 / D, S["one_col"][:],
                ALU.mult, ALU.subtract),
                reads=["tmp2_col", "one_col"], writes=["tmp2_col"])
            p.emit("scalar", lambda e: e.activation(
                S["tmp2_col"][:], S["tmp2_col"][:], AF.Ln, bias=cst[:, C_EPS:C_EPS + 1], scale=1.0),
                reads=["tmp2_col", "cst"], writes=["tmp2_col"])
            p.emit("scalar", lambda e: e.activation(
                S["tmp2_col"][:], S["tmp2_col"][:], AF.Exp, bias=0.0,
                scale=cst[:, C_NEGHALF:C_NEGHALF + 1]),
                reads=["tmp2_col", "cst"], writes=["tmp2_col"])
            p.emit("vector", lambda e, zti=zti: e.tensor_scalar(
                zti, zti, S["tmp_col"][:], S["tmp2_col"][:], ALU.subtract, ALU.mult),
                reads=[zkey, "tmp_col", "tmp2_col"], writes=[zkey])
            p.emit("sync", lambda e, zti=zti, tt=tt: e.dma_start(
                out=dram["y"][tt * 128:(tt + 1) * 128, :], in_=zti),
                reads=[zkey], writes=["y_out"], dma=True)




def build_l1_attn():
    nc = bass.Bass("TRN2", target_bir_lowering=False)
    es = contextlib.ExitStack()
    dram = {}
    dram["x"] = nc.dram_tensor("x", [T, D], F32, kind="ExternalInput").ap()
    dram["Wq"] = nc.dram_tensor("Wq", [D, D], F32, kind="ExternalInput").ap()
    dram["Wk"] = nc.dram_tensor("Wk", [D, 512], F32, kind="ExternalInput").ap()
    dram["Wv"] = nc.dram_tensor("Wv", [D, 512], F32, kind="ExternalInput").ap()
    dram["Wo"] = nc.dram_tensor("Wo", [D, D], F32, kind="ExternalInput").ap()
    dram["bo"] = nc.dram_tensor("bo", [1, D], F32, kind="ExternalInput").ap()
    _dram_consts(nc, dram)
    dram["y"] = nc.dram_tensor("y", [T, D], F32, kind="ExternalOutput").ap()
    dram["attn_scr"] = nc.dram_tensor("attn_scr", [D, T], BF16, kind="Internal").ap()

    S = {"_dram": dram}

    def sb(name, shape, dt):
        S[name] = es.enter_context(nc.sbuf_tensor("s_" + name, list(shape), dt))
        return S[name]

    bigh = es.enter_context(nc.sbuf_tensor("s_big", [128, 4096], F32))
    big = bigh[:]
    _alloc_common(S, sb, big)
    sb("invsqrt64_col", (128, 1), F32)
    sb("masks", (128, 896), BF16)
    sb("cosT1", (128, NT * 64), BF16); sb("sinT1", (128, NT * 64), BF16)
    sb("Xf", (128, ND * T), BF16)
    sb("xb_row", (128, D), BF16)
    S["lnjunk"] = S["xb_row"]
    sb("wst", (128, 512), F32)
    sb("wslab", (128, ND * 512), BF16)
    sb("rowf", (1, D), F32)
    sb("rowb", (1, D), BF16)
    sb("qsc", (128, 512), BF16); sb("rot", (128, 512), BF16)
    sb("tmp512", (128, 512), BF16); sb("ropebuf", (128, 512), BF16)
    sb("QTf", (128, 4 * T), BF16)
    sb("KTf", (128, 4 * T), BF16)
    sb("Vb", (128, NT * 512), BF16)
    sb("kmax", (1, 1), F32)
    sb("PT", (128, 2 * 512), BF16)
    sb("rs_row", (1, 512), F32); sb("rsb", (1, 512), BF16)
    sb("sigb", (128, 512), BF16)
    sb("attn_h", (128, T), BF16)
    S["sqb"] = S["attn_h"]
    sb("xres", (128, D), F32)

    psA = es.enter_context(nc.psum_tensor("psA", [128, 512], F32))
    psB = es.enter_context(nc.psum_tensor("psB", [128, 512], F32))
    psC = es.enter_context(nc.psum_tensor("psC", [128, 512], F32))
    psAV = es.enter_context(nc.psum_tensor("psAV", [128, 512], F32))
    psSIG = es.enter_context(nc.psum_tensor("psSIG", [1, 512], F32))
    psT = es.enter_context(nc.psum_tensor("psT", [128, 128], BF16))

    with es:
        p = Prog(nc, es)
        p.alias = dict(_L0_ALIAS)
        p.alias["lnjunk"] = "xb_row"
        emit_prelude(p, S, need_l1_rope=True)

        emit_l1a(p, S, dram, big, psA, psB, psC, psAV, psSIG, psT)
        with nc.Block() as block:
            p.finalize(block)
    return nc


def _in_maps_l1attn(x_core, inputs, hc):
    m = dict(x=np.ascontiguousarray(x_core),
             Wq=inputs["L1_attn_Wq"], Wk=inputs["L1_attn_Wk"],
             Wv=inputs["L1_attn_Wv"], Wo=inputs["L1_attn_Wo"],
             bo=inputs["L1_attn_b_out"].reshape(1, D))
    m.update(hc)
    return m



NF4_C0 = float(NF4_VALS[0])
NSC = 180224

def _mid_col_consts():
    arr = np.zeros((128, 16), dtype=np.float32)
    for i in range(15):
        arr[:, i] = -NF4_MIDS[i]
    arr[:, 15] = 0.5
    return arr


def emit_l0f(p, S, dram, ar, br, mean_bf, psA, psB, psC, psD, psSIG, psT):
    big = ar[:, 0:4096]
    sc32 = ar[:, 0:2816]
    chA = ar[0:64, 2816:4352]
    chB = ar[0:64, 4352:5888]
    chC = ar[0:64, 5888:7424]
    chD = ar[0:64, 7424:8960]
    am_t = ar[0:64, 8960:8971]
    rq_t = ar[0:64, 8971:8982]
    zacc = [ar[:, i * 2048:(i + 1) * 2048] for i in range(4)]
    acc32 = ar[:, 2816:5632]
    sgn32 = ar[:, 5632:8448]
    xres = ar[:, 4 * 2048:5 * 2048]
    scl_t = ar[:, 8982:8982 + 88]
    deq_t = ar[:, 9070:9070 + 88]
    wtile = br[:, 0:2816]
    sc_bf = br[:, 2816:5632]
    acc = br[:, 5632:8448]
    sgn = br[:, 8448:11264]
    tstage = br[:, 11264:14080]
    repb = br[:, 14080:14168]
    w1s = br[:, 0:8192]
    w2s = br[:, 8192:16384]
    w3s = br[:, 16384:24576]
    htile = br[:, 24576:25088]
    hslab = br[:, 25088:25600]
    WCFG = [
        ("w1", 44, 2048, 32, "w1T"),
        ("w2", 44, 2048, 32, "w2T"),
        ("w3", 16, 5632, 88, "w3T"),
    ]

    SPATH = list(range(8))
    VPATH = list(range(8, 15))
    CPRIME = NF4_C0 + sum(float(NF4_DELTAS[i]) / 2.0 for i in SPATH)

    cst = S["cst"]
    p.emit("sync", lambda e: e.dma_start(out=S["mids"][:], in_=dram["mids"]),
           writes=["mids"], dma=True)
    p.emit("vector", lambda e: e.memset(S["tmp_col"][:], 2.0), writes=["tmp_col"])
    p.emit("scalar", lambda e: e.activation(S["ln2_col"][:], S["tmp_col"][:], AF.Ln),
           reads=["tmp_col"], writes=["ln2_col"])
    p.emit("vector", lambda e: e.reciprocal(S["invln2_col"][:], S["ln2_col"][:]),
           reads=["ln2_col"], writes=["invln2_col"])
    p.emit("vector", lambda e: e.memset(S["ones_row_f"][0:1, :], 1.0), writes=["ones_row_f"])
    p.emit("vector", lambda e: e.memset(S["cnt_col"][:], float(NSC)), writes=["cnt_col"])
    p.emit("vector", lambda e: e.reciprocal(S["cnt_col"][:], S["cnt_col"][:]),
           reads=["cnt_col"], writes=["cnt_col"])

    def floor_chain(eng, out, src, tmp, r, keys, top_bit, nbits):
        ko, ks, ktmp, kr = keys
        emit_floor_pos(p, eng, out, src, ko, ks, tmp, ktmp, r, kr, top_bit, nbits)

    for wi, (wn, ntile, W, nbr, wTn) in enumerate(WCFG):
        for t in range(ntile):
            nsub = W // 2816 if W % 2816 == 0 else 1
            if W == 2048:
                p.emit("sync", lambda e, wn=wn, t=t: e.dma_start(
                    out=wtile[:, 0:2048], in_=dram[wn][t * 128:(t + 1) * 128, :]),
                    writes=["wtile"], dma=True)
                p.emit("vector", lambda e: e.tensor_reduce(
                    scl_t[:, 0:32],
                    wtile[:, 0:2048].rearrange("p (b k) -> p b k", k=64),
                    AX.X, ALU.max, apply_absolute_value=True),
                    reads=["wtile"], writes=["scl_t"])
                p.emit("sync", lambda e, wi=wi, t=t: e.dma_start(
                    out=dram["scl"][wi:wi + 1, t * 4096:(t + 1) * 4096]
                        .rearrange("u (p b) -> (u p) b", p=128),
                    in_=scl_t[:, 0:32]),
                    reads=["scl_t"], writes=["sclscr"], dma=True)
            else:
                for hf in range(2):
                    p.emit("sync", lambda e, wn=wn, t=t, hf=hf: e.dma_start(
                        out=wtile[:, 0:2816],
                        in_=dram[wn][t * 128:(t + 1) * 128, hf * 2816:(hf + 1) * 2816]),
                        writes=["wtile"], dma=True)
                    p.emit("vector", lambda e: e.tensor_reduce(
                        scl_t[:, 0:44],
                        wtile[:, 0:2816].rearrange("p (b k) -> p b k", k=64),
                        AX.X, ALU.max, apply_absolute_value=True),
                        reads=["wtile"], writes=["scl_t"])
                    p.emit("sync", lambda e, wi=wi, t=t, hf=hf: e.dma_start(
                        out=dram["scl"][wi:wi + 1, t * 11264:(t + 1) * 11264]
                            .rearrange("u (p b) -> (u p) b", p=128)[:, hf * 44:(hf + 1) * 44],
                        in_=scl_t[:, 0:44]),
                        reads=["scl_t"], writes=["sclscr"], dma=True)
        p.emit("sync", lambda e, wi=wi: e.dma_start(
            out=sc32[0:64, 0:2816],
            in_=dram["scl"][wi:wi + 1, :].rearrange("u (p i) -> (u p) i", p=64)),
            reads=["sclscr"], writes=["sc32"], dma=True)
        p.emit("vector", lambda e: e.reduce_sum(S["tmp_col"][0:64, :], sc32[0:64, 0:2816], AX.X),
               reads=["sc32"], writes=["tmp_col"])
        p.emit("tensor", lambda e: e.matmul(
            psSIG[0:1, 0:1], S["tmp_col"][0:64, :], S["cnt_col"][0:64, 0:1],
            start=True, stop=True),
            reads=["tmp_col", "cnt_col"], writes=["psSIG"])
        p.emit("vector", lambda e: e.tensor_copy(S["mean1"][0:1, 0:1], psSIG[0:1, 0:1]),
               reads=["psSIG"], writes=["mean1"])
        p.emit("tensor", lambda e: e.matmul(
            psD[:, 0:1], S["ones_row_f"][0:1, :], S["mean1"][0:1, 0:1],
            start=True, stop=True),
            reads=["ones_row_f", "mean1"], writes=["psD"])
        p.emit("vector", lambda e: e.tensor_copy(S["mean_col"][:], psD[:, 0:1]),
               reads=["psD"], writes=["mean_col"])
        for c0, cw2 in ((0, 1536), (1536, 1280)):
            sseg = sc32[0:64, c0:c0 + cw2]
            ng = cw2 // 256
            cA = chA[:, 0:cw2]; cB = chB[:, 0:cw2]
            cC = chC[:, 0:cw2]; cD = chD[:, 0:cw2]
            amS = am_t[:, 0:ng]; rqS = rq_t[:, 0:ng]
            p.emit("vector", lambda e, sseg=sseg, cA=cA: e.tensor_scalar(
                cA, sseg, S["mean_col"][0:64, :], None, ALU.subtract),
                reads=["sc32", "mean_col"], writes=["chA"])
            p.emit("vector", lambda e, cA=cA, amS=amS, ng=ng: e.tensor_reduce(
                amS, cA.rearrange("p (g k) -> p g k", k=256), AX.X, ALU.max,
                apply_absolute_value=True),
                reads=["chA"], writes=["amrq"])
            p.emit("vector", lambda e, amS=amS, rqS=rqS: e.reciprocal(rqS, amS),
                   reads=["amrq"], writes=["amrq"])
            p.emit("vector", lambda e, cA=cA, cB=cB, rqS=rqS, ng=ng: e.tensor_tensor(
                cB.rearrange("p (g k) -> p g k", k=256),
                cA.rearrange("p (g k) -> p g k", k=256),
                rqS.rearrange("p (g u) -> p g u", u=1).broadcast_to([64, ng, 256]),
                ALU.mult), reads=["chA", "amrq"], writes=["chB"])
            p.emit("vector", lambda e, cB=cB: e.tensor_scalar(
                cB, cB, 128.0, 256.5, ALU.mult, ALU.add),
                reads=["chB"], writes=["chB"])
            emit_floor_pos(p, "vector", cC, cB, "chC", "chB",
                           cD, "chD", cB, "chB", 8, 9)
            p.emit("vector", lambda e, cC=cC: e.tensor_scalar(
                cC, cC, 256.0, None, ALU.subtract), reads=["chC"], writes=["chC"])
            p.emit("vector", lambda e, cC=cC: e.tensor_scalar(
                cC, cC, -128.0, 127.0, ALU.max, ALU.min), reads=["chC"], writes=["chC"])
            p.emit("vector", lambda e, cB=cB, cC=cC, amS=amS, ng=ng: e.tensor_tensor(
                cB.rearrange("p (g k) -> p g k", k=256),
                cC.rearrange("p (g k) -> p g k", k=256),
                amS.rearrange("p (g u) -> p g u", u=1).broadcast_to([64, ng, 256]),
                ALU.mult), reads=["chC", "amrq"], writes=["chB"])
            p.emit("vector", lambda e, cB=cB: e.tensor_single_scalar(
                cB, cB, 1.0 / 128.0, ALU.mult), reads=["chB"], writes=["chB"])
            p.emit("scalar", lambda e, cB=cB, cD=cD: e.activation(cD, cB, AF.Sign),
                   reads=["chB"], writes=["chD"])
            p.emit("vector", lambda e, cA=cA, cB=cB: e.tensor_single_scalar(
                cA, cB, -1.0, ALU.mult), reads=["chB"], writes=["chA"])
            p.emit("vector", lambda e, cA=cA, cB=cB: e.tensor_tensor(
                cA, cA, cB, ALU.max), reads=["chB", "chA"], writes=["chA"])
            p.emit("vector", lambda e, cA=cA: e.tensor_single_scalar(
                cA, cA, 1e-30, ALU.max), reads=["chA"], writes=["chA"])
            p.emit("scalar", lambda e, cA=cA, cB=cB: e.activation(cB, cA, AF.Ln),
                   reads=["chA"], writes=["chB"])
            p.emit("vector", lambda e, cB=cB: e.tensor_scalar(
                cB, cB, S["invln2_col"][0:64, :], 192.0, ALU.mult, ALU.add),
                reads=["chB", "invln2_col"], writes=["chB"])
            tmpS = sseg
            emit_floor_pos(p, "vector", cC, cB, "chC", "chB",
                           tmpS, "sc32", cB, "chB", 7, 8)
            p.emit("vector", lambda e, cC=cC: e.tensor_scalar(
                cC, cC, 199.0, -120.0, ALU.subtract, ALU.max),
                reads=["chC"], writes=["chC"])
            p.emit("scalar", lambda e, cB=cB, cC=cC: e.activation(
                cB, cC, AF.Exp, bias=0.0, scale=S["ln2_col"][0:64, :]),
                reads=["chC", "ln2_col"], writes=["chB"])
            p.emit("vector", lambda e, cB=cB, cC=cC: e.reciprocal(cC, cB),
                   reads=["chB"], writes=["chC"])
            p.emit("vector", lambda e, cA=cA, cC=cC: e.tensor_tensor(
                cC, cA, cC, ALU.mult), reads=["chA", "chC"], writes=["chC"])
            p.emit("vector", lambda e, cC=cC: e.tensor_single_scalar(
                cC, cC, 0.5, ALU.add), reads=["chC"], writes=["chC"])
            emit_floor_pos(p, "vector", cA, cC, "chA", "chC",
                           tmpS, "sc32", cC, "chC", 8, 9)
            p.emit("vector", lambda e, cA=cA, cB=cB: e.tensor_tensor(
                cA, cA, cB, ALU.mult), reads=["chA", "chB"], writes=["chA"])
            p.emit("vector", lambda e, cA=cA, cD=cD: e.tensor_tensor(
                cA, cA, cD, ALU.mult), reads=["chA", "chD"], writes=["chA"])
            p.emit("vector", lambda e, cA=cA: e.tensor_scalar(
                cA, cA, S["mean_col"][0:64, :], None, ALU.add),
                reads=["chA", "mean_col"], writes=["chA"])
            p.emit("sync", lambda e, wi=wi, c0=c0, cw2=cw2, cA=cA: e.dma_start(
                out=dram["deq"][wi:wi + 1, :].rearrange("u (p i) -> (u p) i", p=64)[:, c0:c0 + cw2],
                in_=cA), reads=["chA"], writes=["deqscr"], dma=True)

        for t in range(ntile):
            halves = 1 if W == 2048 else 2
            for hf in range(halves):
                cw = 2048 if W == 2048 else 2816
                nb = nbr if W == 2048 else 44
                if W == 2048:
                    src = dram[wn][t * 128:(t + 1) * 128, :]
                    sclsrc = dram["scl"][wi:wi + 1, t * 4096:(t + 1) * 4096] \
                        .rearrange("u (p b) -> (u p) b", p=128)
                    deqsrc = dram["deq"][wi:wi + 1, t * 4096:(t + 1) * 4096] \
                        .rearrange("u (p b) -> (u p) b", p=128)
                else:
                    src = dram[wn][t * 128:(t + 1) * 128, hf * 2816:(hf + 1) * 2816]
                    sclsrc = dram["scl"][wi:wi + 1, t * 11264:(t + 1) * 11264] \
                        .rearrange("u (p b) -> (u p) b", p=128)[:, hf * 44:(hf + 1) * 44]
                    deqsrc = dram["deq"][wi:wi + 1, t * 11264:(t + 1) * 11264] \
                        .rearrange("u (p b) -> (u p) b", p=128)[:, hf * 44:(hf + 1) * 44]
                p.emit("sync", lambda e, src=src, cw=cw: e.dma_start(
                    out=wtile[:, 0:cw], in_=src), writes=["wtile"], dma=True)
                p.emit("sync", lambda e, sclsrc=sclsrc, nb=nb: e.dma_start(
                    out=scl_t[:, 0:nb], in_=sclsrc),
                    reads=["sclscr"], writes=["scl_t"], dma=True)
                p.emit("sync", lambda e, deqsrc=deqsrc, nb=nb: e.dma_start(
                    out=deq_t[:, 0:nb], in_=deqsrc),
                    reads=["deqscr"], writes=["deq_t"], dma=True)
                p.emit("vector", lambda e, nb=nb: e.reciprocal(
                    scl_t[:, 0:nb], scl_t[:, 0:nb]), reads=["scl_t"], writes=["scl_t"])
                p.emit("vector", lambda e, cw=cw, nb=nb: e.tensor_tensor(
                    sc32[:, 0:cw].rearrange("p (b k) -> p b k", k=64),
                    wtile[:, 0:cw].rearrange("p (b k) -> p b k", k=64),
                    scl_t[:, 0:nb].rearrange("p (b u) -> p b u", u=1).broadcast_to([128, nb, 64]),
                    ALU.mult), reads=["wtile", "scl_t"], writes=["sc32"])
                first = True
                for i in VPATH:
                    if first:
                        p.emit("vector", lambda e, i=i, cw=cw: e.tensor_scalar(
                            acc32[:, 0:cw], sc32[:, 0:cw], float(NF4_MIDS[i]),
                            float(NF4_DELTAS[i]), ALU.is_gt, ALU.mult),
                            reads=["sc32"], writes=["chA", "chB"])
                        first = False
                    else:
                        p.emit("vector", lambda e, i=i, cw=cw: e.tensor_scalar(
                            sgn32[:, 0:cw], sc32[:, 0:cw], float(NF4_MIDS[i]),
                            float(NF4_DELTAS[i]), ALU.is_gt, ALU.mult),
                            reads=["sc32"], writes=["chC", "chD"])
                        p.emit("vector", lambda e, cw=cw: e.tensor_tensor(
                            acc32[:, 0:cw], acc32[:, 0:cw], sgn32[:, 0:cw], ALU.add),
                            reads=["chC", "chD", "chA", "chB"], writes=["chA", "chB"])
                for i in SPATH:
                    p.emit("scalar", lambda e, i=i, cw=cw: e.activation(
                        sgn32[:, 0:cw], sc32[:, 0:cw], AF.Sign,
                        bias=S["mids"][:, i:i + 1]),
                        reads=["sc32", "mids"], writes=["chB"])
                    p.emit("vector", lambda e, i=i, cw=cw: e.scalar_tensor_tensor(
                        acc32[:, 0:cw], sgn32[:, 0:cw], float(NF4_DELTAS[i]) / 2.0,
                        acc32[:, 0:cw], ALU.mult, ALU.add),
                        reads=["chB", "chA"], writes=["chA"])
                p.emit("vector", lambda e, cw=cw: e.tensor_scalar(
                    acc32[:, 0:cw], acc32[:, 0:cw], CPRIME, None, ALU.add),
                    reads=["chA"], writes=["chA"])
                p.emit("vector", lambda e, cw=cw, nb=nb: e.tensor_tensor(
                    acc[:, 0:cw].rearrange("p (b k) -> p b k", k=64),
                    acc32[:, 0:cw].rearrange("p (b k) -> p b k", k=64),
                    deq_t[:, 0:nb].rearrange("p (b u) -> p b u", u=1).broadcast_to([128, nb, 64]),
                    ALU.mult), reads=["chA", "chB", "deq_t"], writes=["acc"])
                nck = cw // 128
                for ck in range(nck):
                    p.emit("tensor", lambda e, ck=ck: e.transpose(
                        psT[:], acc[:, ck * 128:(ck + 1) * 128], S["ident"][:]),
                        reads=["acc", "ident"], writes=["psT"])
                    p.emit("vector", lambda e, ck=ck: e.tensor_copy(
                        tstage[:, ck * 128:(ck + 1) * 128], psT[:]),
                        reads=["psT"], writes=["tstage"])
                if W == 2048:
                    p.emit("sync", lambda e, wTn=wTn, t=t, nck=nck: e.dma_start(
                        out=dram[wTn].rearrange("(c p) f -> p c f", p=128)[:, :, t * 128:(t + 1) * 128],
                        in_=tstage[:, 0:2048].rearrange("p (c f) -> p c f", f=128)),
                        reads=["tstage"], writes=[wTn], dma=True)
                else:
                    p.emit("sync", lambda e, t=t, hf=hf, nck=nck: e.dma_start(
                        out=dram["w3T"].rearrange("(c p) d -> p c d", p=128)[:, hf * 22:(hf + 1) * 22, t * 128:(t + 1) * 128],
                        in_=tstage[:, 0:2816].rearrange("p (c f) -> p c f", f=128)),
                        reads=["tstage"], writes=["w3T"], dma=True)

    p.alias_barrier(["chA", "chB", "chC", "chD", "chE", "sc32", "amrq", "big"],
                    ["zacc0", "zacc1", "zacc2", "zacc3", "xres"])
    p.alias_barrier(["wtile", "sc_bf", "acc", "sgn", "tstage", "repb"],
                    ["w1s", "w2s", "w3s", "htile", "hslab"])
    Xf = S["Xf"]
    for tt in range(NT):
        p.emit("sync", lambda e, tt=tt: e.dma_start(
            out=xres, in_=dram["x"][tt * 128:(tt + 1) * 128, :]),
            writes=["xres"], dma=True)
        p.emit("vector", lambda e: e.tensor_copy(S["xb_row"][:], xres),
               reads=["xres"], writes=["xb_row"])
        for dc in range(ND):
            p.emit("tensor", lambda e, dc=dc: e.transpose(
                psT[:], S["xb_row"][:, dc * 128:(dc + 1) * 128], S["ident"][:]),
                reads=["xb_row", "ident"], writes=["psT"])
            p.emit("vector", lambda e, dc=dc, tt=tt: e.tensor_copy(
                Xf[:, dc * T + tt * 128: dc * T + (tt + 1) * 128], psT[:]),
                reads=["psT"], writes=["Xf"])

    for tg in range(4):
        for fb in range(11):
            f0 = fb * 512
            for wTn, slab, skey in (("w1T", w1s, "w1s"), ("w2T", w2s, "w2s")):
                p.emit("sync", lambda e, wTn=wTn, slab=slab, f0=f0: e.dma_start(
                    out=slab.rearrange("p (c f) -> p c f", f=512),
                    in_=dram[wTn].rearrange("(c p) f -> p c f", p=128)[:, :, f0:f0 + 512]),
                    reads=[wTn], writes=[skey], dma=True)
            p.emit("sync", lambda e, f0=f0: e.dma_start(
                out=w3s.rearrange("p (c d) -> p c d", d=2048),
                in_=dram["w3T"].rearrange("(c p) d -> p c d", p=128)[:, (f0 // 128):(f0 // 128) + 4, :]),
                reads=["w3T"], writes=["w3s"], dma=True)
            for ti in range(4):
                tt = tg * 4 + ti
                for ps, pskey, slab, skey in ((psA, "psA", w1s, "w1s"),
                                              (psB, "psB", w2s, "w2s")):
                    for dc in range(ND):
                        p.emit("tensor", lambda e, ps=ps, dc=dc, tt=tt, slab=slab: e.matmul(
                            ps[:], Xf[:, dc * T + tt * 128: dc * T + (tt + 1) * 128],
                            slab[:, dc * 512:(dc + 1) * 512],
                            start=(dc == 0), stop=(dc == ND - 1)),
                            reads=["Xf", skey], writes=[pskey])
                p.emit("scalar", lambda e: e.activation(htile[:, 0:512], psA[:], AF.Silu),
                       reads=["psA"], writes=["htile"])
                p.emit("vector", lambda e: e.tensor_copy(hslab[:, 0:512], psB[:]),
                       reads=["psB"], writes=["hslab"])
                p.emit("vector", lambda e: e.tensor_tensor(
                    htile[:, 0:512], htile[:, 0:512], hslab[:, 0:512], ALU.mult),
                    reads=["htile", "hslab"], writes=["htile"])
                for fc in range(4):
                    p.emit("tensor", lambda e, fc=fc: e.transpose(
                        psT[:], htile[:, fc * 128:(fc + 1) * 128], S["ident"][:]),
                        reads=["htile", "ident"], writes=["psT"])
                    p.emit("vector", lambda e, fc=fc: e.tensor_copy(
                        hslab[:, fc * 128:(fc + 1) * 128], psT[:]),
                        reads=["psT"], writes=["hslab"])
                for dg in range(4):
                    for fc in range(4):
                        p.emit("tensor", lambda e, dg=dg, fc=fc: e.matmul(
                            psD[:], hslab[:, fc * 128:(fc + 1) * 128],
                            w3s[:, fc * 2048 + dg * 512: fc * 2048 + (dg + 1) * 512],
                            start=(fc == 0), stop=(fc == 3)),
                            reads=["hslab", "w3s"], writes=["psD"])
                    zkey = "zacc%d" % ti
                    if fb == 0:
                        p.emit("vector", lambda e, ti=ti, dg=dg: e.tensor_copy(
                            zacc[ti][:, dg * 512:(dg + 1) * 512], psD[:]),
                            reads=["psD"], writes=[zkey])
                    else:
                        p.emit("vector", lambda e, ti=ti, dg=dg: e.tensor_tensor(
                            zacc[ti][:, dg * 512:(dg + 1) * 512],
                            zacc[ti][:, dg * 512:(dg + 1) * 512], psD[:], ALU.add),
                            reads=["psD", zkey], writes=[zkey])
        for ti in range(4):
            tt = tg * 4 + ti
            zti = zacc[ti]
            zkey = "zacc%d" % ti
            p.emit("sync", lambda e, tt=tt: e.dma_start(
                out=xres, in_=dram["x"][tt * 128:(tt + 1) * 128, :]),
                writes=["xres"], dma=True)
            p.emit("vector", lambda e, zti=zti: e.tensor_tensor(
                zti, zti, xres, ALU.add), reads=[zkey, "xres"], writes=[zkey])
            p.emit("vector", lambda e, zti=zti: e.reduce_sum(S["tmp_col"][:], zti, AX.X),
                   reads=[zkey], writes=["tmp_col"])
            p.emit("scalar", lambda e, zti=zti: e.activation(
                S["lnjunk"][:], zti, AF.Square, accum_out=S["tmp2_col"][:]),
                reads=[zkey], writes=["lnjunk", "tmp2_col"])
            p.emit("vector", lambda e: e.tensor_single_scalar(
                S["tmp_col"][:], S["tmp_col"][:], 1.0 / D, ALU.mult),
                reads=["tmp_col"], writes=["tmp_col"])
            p.emit("scalar", lambda e: e.activation(S["one_col"][:], S["tmp_col"][:], AF.Square),
                   reads=["tmp_col"], writes=["one_col"])
            p.emit("vector", lambda e: e.scalar_tensor_tensor(
                S["tmp2_col"][:], S["tmp2_col"][:], 1.0 / D, S["one_col"][:],
                ALU.mult, ALU.subtract),
                reads=["tmp2_col", "one_col"], writes=["tmp2_col"])
            p.emit("scalar", lambda e: e.activation(
                S["tmp2_col"][:], S["tmp2_col"][:], AF.Ln, bias=cst[:, C_EPS:C_EPS + 1], scale=1.0),
                reads=["tmp2_col", "cst"], writes=["tmp2_col"])
            p.emit("scalar", lambda e: e.activation(
                S["tmp2_col"][:], S["tmp2_col"][:], AF.Exp, bias=0.0,
                scale=cst[:, C_NEGHALF:C_NEGHALF + 1]),
                reads=["tmp2_col", "cst"], writes=["tmp2_col"])
            p.emit("vector", lambda e, zti=zti: e.tensor_scalar(
                zti, zti, S["tmp_col"][:], S["tmp2_col"][:], ALU.subtract, ALU.mult),
                reads=[zkey, "tmp_col", "tmp2_col"], writes=[zkey])
            p.emit("sync", lambda e, zti=zti, tt=tt: e.dma_start(
                out=dram["y"][tt * 128:(tt + 1) * 128, :], in_=zti),
                reads=[zkey], writes=["y_out"], dma=True)




def build_l0_ffn():
    nc = bass.Bass("TRN2", target_bir_lowering=False)
    es = contextlib.ExitStack()
    dram = {}
    dram["x"] = nc.dram_tensor("x", [T, D], F32, kind="ExternalInput").ap()
    dram["w1"] = nc.dram_tensor("w1", [5632, D], BF16, kind="ExternalInput").ap()
    dram["w2"] = nc.dram_tensor("w2", [5632, D], BF16, kind="ExternalInput").ap()
    dram["w3"] = nc.dram_tensor("w3", [D, 5632], BF16, kind="ExternalInput").ap()
    _dram_consts(nc, dram)
    dram["mids"] = nc.dram_tensor("hc_mids", [128, 16], F32, kind="ExternalInput").ap()
    dram["y"] = nc.dram_tensor("y", [T, D], F32, kind="ExternalOutput").ap()
    dram["w1T"] = nc.dram_tensor("w1T", [D, 5632], BF16, kind="Internal").ap()
    dram["w2T"] = nc.dram_tensor("w2T", [D, 5632], BF16, kind="Internal").ap()
    dram["w3T"] = nc.dram_tensor("w3T", [5632, D], BF16, kind="Internal").ap()
    dram["scl"] = nc.dram_tensor("sclscr", [3, NSC], F32, kind="Internal").ap()
    dram["deq"] = nc.dram_tensor("deqscr", [3, NSC], F32, kind="Internal").ap()

    S = {"_dram": dram}

    def sb(name, shape, dt):
        S[name] = es.enter_context(nc.sbuf_tensor("s_" + name, list(shape), dt))
        return S[name]

    AR_COLS = 17200
    arh = es.enter_context(nc.sbuf_tensor("s_arena", [128, AR_COLS], F32))
    ar = arh[:]
    big = ar[:, 0:4096]
    _alloc_common(S, sb, big)
    sb("mids", (128, 16), F32)
    brh = es.enter_context(nc.sbuf_tensor("s_barena", [128, 26000], BF16))
    br = brh[:]
    mean_bf = es.enter_context(nc.sbuf_tensor("s_meanbf", [1, 1], BF16))

    sb("Xf", (128, ND * T), BF16)
    sb("xb_row", (128, D), BF16)
    S["lnjunk"] = S["xb_row"]
    sb("mean1", (1, 1), F32)
    sb("ones_row_f", (1, 128), F32)
    sb("cnt_col", (128, 1), F32)
    sb("mean_col", (128, 1), F32)
    sb("ln2_col", (128, 1), F32)
    sb("invln2_col", (128, 1), F32)

    psA = es.enter_context(nc.psum_tensor("psA", [128, 512], F32))
    psB = es.enter_context(nc.psum_tensor("psB", [128, 512], F32))
    psC = es.enter_context(nc.psum_tensor("psC", [128, 512], F32))
    psD = es.enter_context(nc.psum_tensor("psD", [128, 512], F32))
    psSIG = es.enter_context(nc.psum_tensor("psSIG", [1, 512], F32))
    psT = es.enter_context(nc.psum_tensor("psT", [128, 128], BF16))

    with es:
        p = Prog(nc, es)
        p.alias = {"xst": "big", "rt_ang": "big", "lnjunk": "xb_row"}
        emit_prelude(p, S)
        emit_l0f(p, S, dram, ar, br, mean_bf, psA, psB, psC, psD, psSIG, psT)
        with nc.Block() as block:
            p.finalize(block)
    return nc


def _in_maps_l0ffn(x_core, inputs, hc):
    m = dict(x=np.ascontiguousarray(x_core),
             w1=inputs["L0_ffn_weight1"], w2=inputs["L0_ffn_weight2"],
             w3=inputs["L0_ffn_weight3"])
    m.update(hc)
    m["hc_mids"] = _mid_col_consts()
    return m



def _sylvester(n):
    H = np.array([[1.0]], dtype=np.float64)
    while H.shape[0] < n:
        H = np.block([[H, H], [H, -H]])
    return H


def _l1ffn_consts():
    H2048 = _sylvester(2048).astype(bf16)
    H64 = _sylvester(64)
    bd = np.zeros((128, 128), dtype=np.float64)
    bd[0:64, 0:64] = H64
    bd[64:128, 64:128] = H64
    H64bd = bd.astype(bf16)
    H128 = _sylvester(128).astype(bf16)
    return dict(Hb=np.ascontiguousarray(H2048), H64bd=H64bd, H128=H128)


def emit_l1f_prep(p, S, dram, ar, br, psA, psB, psC, psD, psT):
    wslab1 = br[:, 0:8192]
    wslab2 = br[:, 8192:16384]
    ternb = br[:, 16384:24576]
    ternc = br[:, 24576:32768]
    Hconst = S["Hconst"]
    cst = S["cst"]
    p.emit("sync", lambda e: e.dma_start(out=Hconst[:, 0:128], in_=dram["H64bd"]),
           writes=["bHc"], dma=True)
    p.emit("sync", lambda e: e.dma_start(out=Hconst[:, 128:256], in_=dram["H128"]),
           writes=["bHc"], dma=True)
    p.emit("vector", lambda e: e.memset(S["ones_row_f"][0:1, :], 1.0), writes=["ones_row_f"])
    p.emit("vector", lambda e: e.memset(S["rc254_col"][:], 254.0), writes=["rc254_col"])
    p.emit("vector", lambda e: e.reciprocal(S["rc254_col"][:], S["rc254_col"][:]),
           reads=["rc254_col"], writes=["rc254_col"])

    def floor_frac(eng, r, key_r, tmp, key_tmp, top_bit, nbits):
        for i in range(nbits):
            b = float(2 ** (top_bit - i))
            p.emit(eng, lambda e, b=b: e.tensor_scalar(tmp, r, b, b, ALU.is_ge, ALU.mult),
                   reads=[key_r], writes=[key_tmp])
            p.emit(eng, lambda e: e.tensor_tensor(r, r, tmp, ALU.subtract),
                   reads=[key_tmp, key_r], writes=[key_r])

    WL = [("wg", 64, 2048, 0), ("wu", 64, 2048, 1), ("wd", 16, 8192, 2)]
    for wn, ntile, W, wi in WL:
        for t in range(ntile):
            p.emit("sync", lambda e, wn=wn, t=t, W=W: e.dma_start(
                out=ar[:, 4096:4096 + W], in_=dram[wn][t * 128:(t + 1) * 128, :]),
                writes=["arB"], dma=True)
            p.emit("vector", lambda e, W=W: e.tensor_reduce(
                ar[:, 0:1], ar[:, 4096:4096 + W], AX.X, ALU.add,
                apply_absolute_value=True),
                reads=["arB"], writes=["arA"])
            if t == 0:
                p.emit("vector", lambda e: e.tensor_copy(ar[:, 1:2], ar[:, 0:1]),
                       reads=["arA"], writes=["arA"])
            else:
                p.emit("vector", lambda e: e.tensor_tensor(
                    ar[:, 1:2], ar[:, 1:2], ar[:, 0:1], ALU.add),
                    reads=["arA"], writes=["arA"])
        p.emit("vector", lambda e, W=W, ntile=ntile: e.memset(
            S["cnt_col"][:], float(W * ntile * 128)), writes=["cnt_col"])
        p.emit("vector", lambda e: e.reciprocal(S["cnt_col"][:], S["cnt_col"][:]),
               reads=["cnt_col"], writes=["cnt_col"])
        p.emit("tensor", lambda e: e.matmul(
            psD[0:1, 0:1], ar[:, 1:2], S["cnt_col"][:, 0:1], start=True, stop=True),
            reads=["arA", "cnt_col"], writes=["psD"])
        p.emit("vector", lambda e: e.tensor_single_scalar(
            ar[:, 2:3][0:1, :], psD[0:1, 0:1], 1e-5, ALU.max),
            reads=["psD"], writes=["arA"])
        p.emit("tensor", lambda e: e.matmul(
            psC[:, 0:1], S["ones_row_f"][0:1, :], ar[0:1, 2:3], start=True, stop=True),
            reads=["ones_row_f", "arA"], writes=["psC"])
        p.emit("vector", lambda e, wi=wi: e.tensor_copy(
            S["sw_col"][:, wi:wi + 1], psC[:, 0:1]), reads=["psC"], writes=["sw_col"])
    p.emit("vector", lambda e: e.reciprocal(S["fcol"][:, 0:3], S["sw_col"][:, 0:3]),
           reads=["sw_col"], writes=["fcol"])

    for wn, ntile, W, wi in WL:
        wTn = {"wg": "wgT", "wu": "wuT", "wd": "wdT"}[wn]
        for t in range(ntile):
            p.emit("sync", lambda e, wn=wn, t=t, W=W: e.dma_start(
                out=ar[:, 4096:4096 + W], in_=dram[wn][t * 128:(t + 1) * 128, :]),
                writes=["arB"], dma=True)
            p.emit("scalar", lambda e, W=W, wi=wi: e.activation(
                ternb[:, 0:W], ar[:, 4096:4096 + W], AF.Sign,
                bias=cst[:, C_NEGHALF:C_NEGHALF + 1], scale=S["fcol"][:, wi:wi + 1]),
                reads=["arB", "fcol", "cst"], writes=["bt1"])
            p.emit("scalar", lambda e, W=W, wi=wi: e.activation(
                ternc[:, 0:W], ar[:, 4096:4096 + W], AF.Sign,
                bias=cst[:, C_HALF:C_HALF + 1], scale=S["fcol"][:, wi:wi + 1]),
                reads=["arB", "fcol", "cst"], writes=["bt2"])
            p.emit("vector", lambda e, W=W: e.tensor_tensor(
                ternb[:, 0:W], ternb[:, 0:W], ternc[:, 0:W], ALU.add),
                reads=["bt1", "bt2"], writes=["bt1"])
            nck = W // 128
            for ck in range(nck):
                p.emit("tensor", lambda e, ck=ck: e.transpose(
                    psT[:], ternb[:, ck * 128:(ck + 1) * 128], S["ident"][:]),
                    reads=["bt1", "ident"], writes=["psT"])
                p.emit("vector", lambda e, ck=ck: e.tensor_copy(
                    ternc[:, ck * 128:(ck + 1) * 128], psT[:]),
                    reads=["psT"], writes=["bt2"])
            if wn in ("wg", "wu"):
                p.emit("sync", lambda e, wTn=wTn, t=t: e.dma_start(
                    out=dram[wTn].rearrange("(c p) f -> p c f", p=128)[:, :, t * 128:(t + 1) * 128],
                    in_=ternc[:, 0:2048].rearrange("p (c f) -> p c f", f=128)),
                    reads=["bt2"], writes=[wTn], dma=True)
            else:
                for j in (0, 1):
                    p.emit("sync", lambda e, t=t, j=j: e.dma_start(
                        out=dram["wdT"].rearrange("(b a2 j) d -> j b a2 d", b=64, j=2)[j, :, :, t * 128:(t + 1) * 128],
                        in_=ternc[j * 64:(j + 1) * 64, 0:8192].rearrange("p (c f) -> p c f", f=128)),
                        reads=["bt2"], writes=["wdT"], dma=True)




def build_l1_ffn():
    nc = bass.Bass("TRN2", target_bir_lowering=False)
    es = contextlib.ExitStack()
    dram = {}
    dram["x"] = nc.dram_tensor("x", [T, D], F32, kind="ExternalInput").ap()
    dram["wg"] = nc.dram_tensor("wg", [8192, D], F32, kind="ExternalInput").ap()
    dram["wu"] = nc.dram_tensor("wu", [8192, D], F32, kind="ExternalInput").ap()
    dram["wd"] = nc.dram_tensor("wd", [D, 8192], F32, kind="ExternalInput").ap()
    _dram_consts(nc, dram)
    dram["Hb"] = nc.dram_tensor("hc_Hb", [D, D], BF16, kind="ExternalInput").ap()
    dram["H64bd"] = nc.dram_tensor("hc_H64bd", [128, 128], BF16, kind="ExternalInput").ap()
    dram["H128"] = nc.dram_tensor("hc_H128", [128, 128], BF16, kind="ExternalInput").ap()
    dram["y"] = nc.dram_tensor("y", [T, D], F32, kind="ExternalOutput").ap()
    dram["wgT"] = nc.dram_tensor("wgT", [D, 8192], BF16, kind="Internal").ap()
    dram["wuT"] = nc.dram_tensor("wuT", [D, 8192], BF16, kind="Internal").ap()
    dram["wdT"] = nc.dram_tensor("wdT", [8192, D], BF16, kind="Internal").ap()
    dram["h_scr"] = nc.dram_tensor("h_scr", [T, 8192], F32, kind="Internal").ap()
    dram["y1_scr"] = nc.dram_tensor("y1_scr", [128, 64 * T], BF16, kind="Internal").ap()
    dram["xq_scr"] = nc.dram_tensor("xq_scr", [128, 64 * T], BF16, kind="Internal").ap()

    S = {"_dram": dram}

    def sb(name, shape, dt):
        S[name] = es.enter_context(nc.sbuf_tensor("s_" + name, list(shape), dt))
        return S[name]

    arh = es.enter_context(nc.sbuf_tensor("s_arena", [128, 18000], F32))
    ar = arh[:]
    big = ar[:, 0:4096]
    _alloc_common(S, sb, big)
    sb("invsqrt2048_col", (128, 1), F32)
    sb("invsqrt8192_col", (128, 1), F32)
    fw32 = ar[:, 0:2048]
    htile = ar[:, 4096:12288]
    y2f = ar[:, 4096:12288]
    zacc = [ar[:, 12288 + i * 2048:12288 + (i + 1) * 2048] for i in range(2)]
    xres = ar[:, 2048:4096]
    mx_col = es.enter_context(nc.sbuf_tensor("s_mx", [128, 8], F32))

    brh = es.enter_context(nc.sbuf_tensor("s_barena", [128, 33000], BF16))
    br = brh[:]
    wslab1 = br[:, 0:8192]
    wslab2 = br[:, 8192:16384]
    wdslab = br[:, 0:16384]
    ternb = br[:, 16384:16384 + 8192]
    ternc = br[:, 24576:24576 + 8192]
    Hconst = es.enter_context(nc.sbuf_tensor("s_Hconst", [128, 256], BF16))[:]
    S["Hconst"] = Hconst

    sb("Xf", (128, ND * T), BF16)
    sb("xb_row", (128, D), BF16)
    S["lnjunk"] = S["xb_row"]
    sb("ones_row_f", (1, 128), F32)
    sb("cnt_col", (128, 1), F32)
    sb("sw_col", (128, 3), F32)
    sb("rc254_col", (128, 1), F32)
    sb("sc_col", (128, 1), F32)
    sb("sc_row", (1, 128), BF16)
    sb("fcol", (128, 3), F32)
    sb("fscl", (128, 32), F32)
    sb("fscl2", (128, 16), F32)

    psA = es.enter_context(nc.psum_tensor("psA", [128, 512], F32))
    psB = es.enter_context(nc.psum_tensor("psB", [128, 512], F32))
    psC = es.enter_context(nc.psum_tensor("psC", [128, 512], F32))
    psD = es.enter_context(nc.psum_tensor("psD", [128, 512], F32))
    psT = es.enter_context(nc.psum_tensor("psT", [128, 128], BF16))

    with es:
        p = Prog(nc, es)
        p.alias = {"xst": "big", "rt_ang": "big", "lnjunk": "xb_row"}
        emit_prelude(p, S)
        cst = S["cst"]
        emit_l1f_prep(p, S, dram, ar, br, psA, psB, psC, psD, psT)
        _emit_l1ffn_main(p, S, dram, ar, br, zacc, xres, mx_col,
                         (psA, psB, psC, psD, psT), cst)

        with nc.Block() as block:
            p.finalize(block)
    return nc




def _emit_l1ffn_main(p, S, dram, ar, br, zacc, xres, mx_col, pss, cst):
    psA, psB, psC, psD, psT = pss
    fw32 = ar[:, 0:2048]
    htile = ar[:, 4096:12288]
    y2f = ar[:, 4096:12288]
    wslab1 = br[:, 0:8192]
    wslab2 = br[:, 8192:16384]
    wdslab = br[:, 0:16384]
    ternb = br[:, 16384:24576]
    ternc = br[:, 24576:32768]
    Hconst = S["Hconst"]
    Xf = S["Xf"]
    r2048 = S["invsqrt2048_col"]
    r8192 = S["invsqrt8192_col"]

    def floor_frac(eng, r, key_r, tmp, key_tmp, top_bit, nbits):
        for i in range(nbits):
            b = float(2 ** (top_bit - i))
            p.emit(eng, lambda e, b=b, r=r, tmp=tmp: e.tensor_scalar(
                tmp, r, b, b, ALU.is_ge, ALU.mult),
                reads=[key_r], writes=[key_tmp])
            p.emit(eng, lambda e, r=r, tmp=tmp: e.tensor_tensor(
                r, r, tmp, ALU.subtract),
                reads=[key_tmp, key_r], writes=[key_r])

    p.alias_barrier(["big"], ["arA", "arB"])

    for tt in range(NT):
        p.emit("sync", lambda e, tt=tt: e.dma_start(
            out=xres, in_=dram["x"][tt * 128:(tt + 1) * 128, :]),
            writes=["big"], dma=True)
        p.emit("vector", lambda e: e.tensor_copy(S["xb_row"][:], xres),
               reads=["big"], writes=["xb_row"])
        for dc in range(ND):
            p.emit("tensor", lambda e, dc=dc: e.transpose(
                psT[:], S["xb_row"][:, dc * 128:(dc + 1) * 128], S["ident"][:]),
                reads=["xb_row", "ident"], writes=["psT"])
            p.emit("vector", lambda e, dc=dc, tt=tt: e.tensor_copy(
                Xf[:, dc * T + tt * 128: dc * T + (tt + 1) * 128], psT[:]),
                reads=["psT"], writes=["Xf"])

    for pr in range(8):
        for fs in range(4):
            p.emit("sync", lambda e, fs=fs: e.dma_start(
                out=wslab1.rearrange("p (c f) -> p c f", f=512),
                in_=dram["Hb"].rearrange("(c p) f -> p c f", p=128)[:, :, fs * 512:(fs + 1) * 512]),
                writes=["bsl1"], dma=True)
            for u in range(2):
                tt = pr * 2 + u
                ps, pskey = (psA, "psA") if u == 0 else (psB, "psB")
                for dc in range(ND):
                    p.emit("tensor", lambda e, ps=ps, dc=dc, tt=tt: e.matmul(
                        ps[:], Xf[:, dc * T + tt * 128: dc * T + (tt + 1) * 128],
                        wslab1[:, dc * 512:(dc + 1) * 512],
                        start=(dc == 0), stop=(dc == ND - 1)),
                        reads=["Xf", "bsl1"], writes=[pskey])
                p.emit("vector", lambda e, ps=ps, u=u, fs=fs: e.tensor_scalar(
                    ar[:, u * 2048 + fs * 512: u * 2048 + (fs + 1) * 512],
                    ps[:], r2048[:], None, ALU.mult),
                    reads=[pskey, "invsqrt2048_col"], writes=["arA"])
        for u in range(2):
            tt = pr * 2 + u
            fw = ar[:, u * 2048:(u + 1) * 2048]
            p.emit("vector", lambda e, fw=fw: e.tensor_reduce(
                mx_col[:, 0:1], fw, AX.X, ALU.max, apply_absolute_value=True),
                reads=["arA"], writes=["mx"])
            p.emit("vector", lambda e: e.tensor_single_scalar(
                mx_col[:, 0:1], mx_col[:, 0:1], 1e-5, ALU.max), reads=["mx"], writes=["mx"])
            p.emit("vector", lambda e: e.reciprocal(mx_col[:, 1:2], mx_col[:, 0:1]),
                   reads=["mx"], writes=["mx"])
            p.emit("vector", lambda e: e.tensor_single_scalar(
                mx_col[:, 1:2], mx_col[:, 1:2], 127.0, ALU.mult), reads=["mx"], writes=["mx"])
            yb = ar[:, 4096:6144]
            tmpb = ar[:, 6144:8192]
            rb = ar[:, 8192:10240]
            p.emit("vector", lambda e, fw=fw, yb=yb: e.tensor_scalar(
                yb, fw, mx_col[:, 1:2], 256.5, ALU.mult, ALU.add),
                reads=["arA", "mx"], writes=["arB"])
            p.emit("vector", lambda e, yb=yb, rb=rb: e.tensor_copy(rb, yb),
                   reads=["arB"], writes=["arB"])
            floor_frac("vector", rb, "arB", tmpb, "arB", 8, 9)
            p.emit("vector", lambda e, yb=yb, rb=rb: e.tensor_tensor(
                yb, yb, rb, ALU.subtract), reads=["arB"], writes=["arB"])
            p.emit("vector", lambda e, yb=yb: e.tensor_scalar(
                yb, yb, 256.0, None, ALU.subtract), reads=["arB"], writes=["arB"])
            p.emit("vector", lambda e, yb=yb: e.tensor_scalar(
                yb, yb, -128.0, 127.0, ALU.max, ALU.min), reads=["arB"], writes=["arB"])
            p.emit("vector", lambda e, yb=yb: e.tensor_copy(ternb[:, 0:2048], yb),
                   reads=["arB"], writes=["bt1"])
            for dc in range(ND):
                p.emit("tensor", lambda e, dc=dc: e.transpose(
                    psT[:], ternb[:, dc * 128:(dc + 1) * 128], S["ident"][:]),
                    reads=["bt1", "ident"], writes=["psT"])
                p.emit("vector", lambda e, dc=dc, tt=tt: e.tensor_copy(
                    Xf[:, dc * T + tt * 128: dc * T + (tt + 1) * 128], psT[:]),
                    reads=["psT"], writes=["Xf"])
            for wi in (0, 1):
                p.emit("vector", lambda e, wi=wi, tt=tt: e.tensor_tensor(
                    S["fscl"][:, tt * 2 + wi: tt * 2 + wi + 1], mx_col[:, 0:1],
                    S["sw_col"][:, wi:wi + 1], ALU.mult),
                    reads=["mx", "sw_col"], writes=["fscl"])
                p.emit("vector", lambda e, wi=wi, tt=tt: e.tensor_scalar(
                    S["fscl"][:, tt * 2 + wi: tt * 2 + wi + 1],
                    S["fscl"][:, tt * 2 + wi: tt * 2 + wi + 1],
                    S["rc254_col"][:], None, ALU.mult),
                    reads=["fscl", "rc254_col"], writes=["fscl"])

    for fo in range(16):
        for wTn, slab, skey in (("wgT", wslab1, "bsl1"), ("wuT", wslab2, "bsl2")):
            p.emit("sync", lambda e, wTn=wTn, slab=slab, fo=fo: e.dma_start(
                out=slab.rearrange("p (c f) -> p c f", f=512),
                in_=dram[wTn].rearrange("(c p) f -> p c f", p=128)[:, :, fo * 512:(fo + 1) * 512]),
                reads=[wTn], writes=[skey], dma=True)
        for tt in range(NT):
            for ps, pskey, slab, skey in ((psA, "psA", wslab1, "bsl1"),
                                          (psB, "psB", wslab2, "bsl2")):
                for dc in range(ND):
                    p.emit("tensor", lambda e, ps=ps, dc=dc, tt=tt, slab=slab: e.matmul(
                        ps[:], Xf[:, dc * T + tt * 128: dc * T + (tt + 1) * 128],
                        slab[:, dc * 512:(dc + 1) * 512],
                        start=(dc == 0), stop=(dc == ND - 1)),
                        reads=["Xf", skey], writes=[pskey])
            p.emit("scalar", lambda e, tt=tt: e.activation(
                ar[:, 0:512], psA[:], AF.Silu, bias=0.0,
                scale=S["fscl"][:, tt * 2: tt * 2 + 1]),
                reads=["psA", "fscl"], writes=["arA"])
            p.emit("vector", lambda e, tt=tt: e.tensor_scalar(
                ar[:, 512:1024], psB[:], S["fscl"][:, tt * 2 + 1: tt * 2 + 2],
                None, ALU.mult),
                reads=["psB", "fscl"], writes=["arA"])
            p.emit("vector", lambda e: e.tensor_tensor(
                ar[:, 1024:1536], ar[:, 0:512], ar[:, 512:1024], ALU.mult),
                reads=["arA"], writes=["arA"])
            p.emit("sync", lambda e, tt=tt, fo=fo: e.dma_start(
                out=dram["h_scr"][tt * 128:(tt + 1) * 128, fo * 512:(fo + 1) * 512],
                in_=ar[:, 1024:1536]),
                reads=["arA"], writes=["h_scr"], dma=True)

    for tt in range(NT):
        p.emit("sync", lambda e, tt=tt: e.dma_start(
            out=htile, in_=dram["h_scr"][tt * 128:(tt + 1) * 128, :]),
            reads=["h_scr"], writes=["arB"], dma=True)
        p.emit("vector", lambda e: e.tensor_copy(ternb[:, 0:8192], htile),
               reads=["arB"], writes=["bt1"])
        for c in range(64):
            p.emit("tensor", lambda e, c=c: e.transpose(
                psT[:], ternb[:, c * 128:(c + 1) * 128], S["ident"][:]),
                reads=["bt1", "ident"], writes=["psT"])
            p.emit("vector", lambda e, c=c: e.tensor_copy(
                ternc[:, c * 128:(c + 1) * 128], psT[:]),
                reads=["psT"], writes=["bt2"])
        for c in range(64):
            p.emit("tensor", lambda e, c=c: e.matmul(
                psC[:, 0:128], Hconst[:, 0:128], ternc[:, c * 128:(c + 1) * 128],
                start=True, stop=True),
                reads=["bHc", "bt2"], writes=["psC"])
            p.emit("vector", lambda e, c=c: e.tensor_copy(
                ternb[:, c * 128:(c + 1) * 128], psC[:, 0:128]),
                reads=["psC"], writes=["bt1"])
        for j in (0, 1):
            p.emit("sync", lambda e, tt=tt, j=j: e.dma_start(
                out=dram["y1_scr"].rearrange("(c2 j) (b t) -> j b c2 t", j=2, t=T)[j, :, :, tt * 128:(tt + 1) * 128],
                in_=ternb[j * 64:(j + 1) * 64, 0:8192].rearrange("p (c f) -> p c f", f=128)),
                reads=["bt1"], writes=["y1_scr"], dma=True)
    for tt in range(NT):
        p.emit("sync", lambda e, tt=tt: e.dma_start(
            out=ternc[:, 0:8192].rearrange("p (b t) -> p b t", t=128),
            in_=dram["y1_scr"].rearrange("a (b t) -> a b t", t=T)[:, :, tt * 128:(tt + 1) * 128]),
            reads=["y1_scr"], writes=["bt2"], dma=True)
        for ns in range(16):
            p.emit("tensor", lambda e, ns=ns: e.matmul(
                psC[:], Hconst[:, 128:256], ternc[:, ns * 512:(ns + 1) * 512],
                start=True, stop=True),
                reads=["bHc", "bt2"], writes=["psC"])
            p.emit("vector", lambda e, ns=ns: e.tensor_scalar(
                y2f[:, ns * 512:(ns + 1) * 512], psC[:], r8192[:], None, ALU.mult),
                reads=["psC", "invsqrt8192_col"], writes=["arB"])
        p.emit("vector", lambda e: e.tensor_reduce(
            ar[:, 0:128], y2f.rearrange("p (b t) -> p t b", t=128), AX.X, ALU.max,
            apply_absolute_value=True),
            reads=["arB"], writes=["arA"])
        p.emit("vector", lambda e: e.tensor_copy(ternb[:, 0:128], ar[:, 0:128]),
               reads=["arA"], writes=["bt1"])
        p.emit("tensor", lambda e: e.transpose(
            psT[:], ternb[:, 0:128], S["ident"][:]),
            reads=["bt1", "ident"], writes=["psT"])
        p.emit("vector", lambda e: e.tensor_copy(ternb[:, 128:256], psT[:]),
               reads=["psT"], writes=["bt1"])
        p.emit("vector", lambda e: e.tensor_reduce(
            mx_col[:, 0:1], ternb[:, 128:256], AX.X, ALU.max, apply_absolute_value=True),
            reads=["bt1"], writes=["mx"])
        p.emit("vector", lambda e: e.tensor_single_scalar(
            mx_col[:, 0:1], mx_col[:, 0:1], 1e-5, ALU.max), reads=["mx"], writes=["mx"])
        p.emit("vector", lambda e: e.reciprocal(mx_col[:, 1:2], mx_col[:, 0:1]),
               reads=["mx"], writes=["mx"])
        p.emit("vector", lambda e: e.tensor_single_scalar(
            mx_col[:, 1:2], mx_col[:, 1:2], 127.0, ALU.mult), reads=["mx"], writes=["mx"])
        p.emit("vector", lambda e, tt=tt: e.tensor_tensor(
            S["fscl2"][:, tt:tt + 1], mx_col[:, 0:1], S["sw_col"][:, 2:3], ALU.mult),
            reads=["mx", "sw_col"], writes=["fscl2"])
        p.emit("vector", lambda e, tt=tt: e.tensor_scalar(
            S["fscl2"][:, tt:tt + 1], S["fscl2"][:, tt:tt + 1], S["rc254_col"][:],
            None, ALU.mult),
            reads=["fscl2", "rc254_col"], writes=["fscl2"])
        p.emit("vector", lambda e: e.tensor_copy(ternb[:, 0:1], mx_col[:, 1:2]),
               reads=["mx"], writes=["bt1"])
        p.emit("tensor", lambda e: e.transpose(
            psT[0:1, :], ternb[:, 0:1], S["ident"][:]),
            reads=["bt1", "ident"], writes=["psT"])
        p.emit("vector", lambda e: e.tensor_copy(S["sc_row"][0:1, :], psT[0:1, :]),
               reads=["psT"], writes=["sc_row"])
        p.emit("tensor", lambda e: e.matmul(
            psC[:, 0:128], S["ones_row"][0:1, :], S["sc_row"][0:1, :],
            start=True, stop=True),
            reads=["ones_row", "sc_row"], writes=["psC"])
        p.emit("vector", lambda e: e.tensor_copy(ar[:, 0:128], psC[:, 0:128]),
               reads=["psC"], writes=["arA"])
        p.emit("vector", lambda e: e.tensor_tensor(
            y2f.rearrange("p (b t) -> p b t", t=128),
            y2f.rearrange("p (b t) -> p b t", t=128),
            ar[:, 0:128].rearrange("p (u t) -> p u t", u=1).broadcast_to([128, 64, 128]),
            ALU.mult),
            reads=["arB", "arA"], writes=["arB"])
        p.emit("vector", lambda e: e.tensor_scalar(
            y2f, y2f, 256.5, None, ALU.add), reads=["arB"], writes=["arB"])
        for cg in range(4):
            seg = y2f[:, cg * 2048:(cg + 1) * 2048]
            rseg = ar[:, 0:2048]
            tseg = ar[:, 2048:4096]
            p.emit("vector", lambda e, seg=seg, rseg=rseg: e.tensor_copy(rseg, seg),
                   reads=["arB"], writes=["arA"])
            floor_frac("vector", rseg, "arA", tseg, "big", 8, 9)
            p.emit("vector", lambda e, seg=seg, rseg=rseg: e.tensor_tensor(
                seg, seg, rseg, ALU.subtract), reads=["arA", "arB"], writes=["arB"])
            p.emit("vector", lambda e, seg=seg: e.tensor_scalar(
                seg, seg, 256.0, None, ALU.subtract), reads=["arB"], writes=["arB"])
            p.emit("vector", lambda e, seg=seg: e.tensor_scalar(
                seg, seg, -128.0, 127.0, ALU.max, ALU.min), reads=["arB"], writes=["arB"])
        p.emit("vector", lambda e: e.tensor_copy(ternb[:, 0:8192], y2f),
               reads=["arB"], writes=["bt1"])
        p.emit("sync", lambda e, tt=tt: e.dma_start(
            out=dram["xq_scr"].rearrange("a (b t) -> a b t", t=T)[:, :, tt * 128:(tt + 1) * 128],
            in_=ternb[:, 0:8192].rearrange("p (b t) -> p b t", t=128)),
            reads=["bt1"], writes=["xq_scr"], dma=True)

    p.alias_barrier(["arB"], ["zacc0", "zacc1"])
    for tg in range(8):
        for ti in range(2):
            tt = tg * 2 + ti
            dst = ternb if ti == 0 else ternc
            dkey = "bt1" if ti == 0 else "bt2"
            p.emit("sync", lambda e, tt=tt, dst=dst: e.dma_start(
                out=dst[:, 0:8192].rearrange("p (b t) -> p b t", t=128),
                in_=dram["xq_scr"].rearrange("a (b t) -> a b t", t=T)[:, :, tt * 128:(tt + 1) * 128]),
                reads=["xq_scr"], writes=[dkey], dma=True)
        for dg in range(4):
            for hb in range(2):
                p.emit("sync", lambda e, dg=dg, hb=hb: e.dma_start(
                    out=wdslab.rearrange("p (c f) -> p c f", f=512),
                    in_=dram["wdT"].rearrange("(c p) d -> p c d", p=128)[:, hb * 32:(hb + 1) * 32, dg * 512:(dg + 1) * 512]),
                    reads=["wdT"], writes=["bsl1", "bsl2"], dma=True)
                for ti in range(2):
                    xsrc = ternb if ti == 0 else ternc
                    skey2 = "bt1" if ti == 0 else "bt2"
                    ps, pskey = (psA, "psA") if ti == 0 else (psB, "psB")
                    for bi in range(32):
                        b = hb * 32 + bi
                        p.emit("tensor", lambda e, ps=ps, xsrc=xsrc, b=b, bi=bi,
                               first=(hb == 0 and bi == 0), last=(hb == 1 and bi == 31): e.matmul(
                            ps[:], xsrc[:, b * 128:(b + 1) * 128],
                            wdslab[:, bi * 512:(bi + 1) * 512],
                            start=first, stop=last),
                            reads=[skey2, "bsl1", "bsl2"], writes=[pskey])
            for ti in range(2):
                tt = tg * 2 + ti
                ps, pskey = (psA, "psA") if ti == 0 else (psB, "psB")
                if dg == 0:
                    p.emit("sync", lambda e, tt=tt: e.dma_start(
                        out=xres, in_=dram["x"][tt * 128:(tt + 1) * 128, :]),
                        writes=["big"], dma=True)
                    p.emit("vector", lambda e, ti=ti: e.tensor_copy(
                        zacc[ti], xres), reads=["big"], writes=["zacc%d" % ti])
                p.emit("vector", lambda e, ps=ps, ti=ti, dg=dg, tt=tt: e.scalar_tensor_tensor(
                    zacc[ti][:, dg * 512:(dg + 1) * 512], ps[:],
                    S["fscl2"][:, tt:tt + 1], zacc[ti][:, dg * 512:(dg + 1) * 512],
                    ALU.mult, ALU.add),
                    reads=[pskey, "fscl2", "zacc%d" % ti], writes=["zacc%d" % ti])
        for ti in range(2):
            tt = tg * 2 + ti
            zti = zacc[ti]
            zkey = "zacc%d" % ti
            p.emit("vector", lambda e, zti=zti: e.reduce_sum(S["tmp_col"][:], zti, AX.X),
                   reads=[zkey], writes=["tmp_col"])
            p.emit("scalar", lambda e, zti=zti: e.activation(
                S["lnjunk"][:], zti, AF.Square, accum_out=S["tmp2_col"][:]),
                reads=[zkey], writes=["lnjunk", "tmp2_col"])
            p.emit("vector", lambda e: e.tensor_single_scalar(
                S["tmp_col"][:], S["tmp_col"][:], 1.0 / D, ALU.mult),
                reads=["tmp_col"], writes=["tmp_col"])
            p.emit("scalar", lambda e: e.activation(S["one_col"][:], S["tmp_col"][:], AF.Square),
                   reads=["tmp_col"], writes=["one_col"])
            p.emit("vector", lambda e: e.scalar_tensor_tensor(
                S["tmp2_col"][:], S["tmp2_col"][:], 1.0 / D, S["one_col"][:],
                ALU.mult, ALU.subtract),
                reads=["tmp2_col", "one_col"], writes=["tmp2_col"])
            p.emit("scalar", lambda e: e.activation(
                S["tmp2_col"][:], S["tmp2_col"][:], AF.Ln, bias=cst[:, C_EPS:C_EPS + 1], scale=1.0),
                reads=["tmp2_col", "cst"], writes=["tmp2_col"])
            p.emit("scalar", lambda e: e.activation(
                S["tmp2_col"][:], S["tmp2_col"][:], AF.Exp, bias=0.0,
                scale=cst[:, C_NEGHALF:C_NEGHALF + 1]),
                reads=["tmp2_col", "cst"], writes=["tmp2_col"])
            p.emit("vector", lambda e, zti=zti: e.tensor_scalar(
                zti, zti, S["tmp_col"][:], S["tmp2_col"][:], ALU.subtract, ALU.mult),
                reads=[zkey, "tmp_col", "tmp2_col"], writes=[zkey])
            p.emit("sync", lambda e, zti=zti, tt=tt: e.dma_start(
                out=dram["y"][tt * 128:(tt + 1) * 128, :], in_=zti),
                reads=[zkey], writes=["y_out"], dma=True)


def _in_maps_l1ffn(x_core, inputs, hc):
    m = dict(x=np.ascontiguousarray(x_core),
             wg=inputs["L1_ffn_w_gate"], wu=inputs["L1_ffn_w_up"],
             wd=inputs["L1_ffn_w_down"])
    m.update(hc)
    for k, v in _l1ffn_consts().items():
        m["hc_" + k] = v
    return m






def build_fused():
    nc = bass.Bass("TRN2", target_bir_lowering=False)
    es = contextlib.ExitStack()
    dram = {}
    dram["x"] = nc.dram_tensor("x", [T, D], F32, kind="ExternalInput").ap()
    def din(name, shape, dt):
        dram[name] = nc.dram_tensor(name, list(shape), dt, kind="ExternalInput").ap()
    for nm in ("Wq", "Wk", "Wv", "Wo"):
        din("a0" + nm, [D, D], F32)
    for nm in ("bq", "bk", "bv", "bo"):
        din("a0" + nm, [1, D], F32)
    din("f0w1", [5632, D], BF16); din("f0w2", [5632, D], BF16)
    din("f0w3", [D, 5632], BF16)
    din("a1Wq", [D, D], F32); din("a1Wk", [D, 512], F32)
    din("a1Wv", [D, 512], F32); din("a1Wo", [D, D], F32); din("a1bo", [1, D], F32)
    din("f1wg", [8192, D], F32); din("f1wu", [8192, D], F32)
    din("f1wd", [D, 8192], F32)
    _dram_consts(nc, dram)
    din("hc_mids", [128, 16], F32)
    din("hc_Hb", [D, D], BF16); din("hc_H64bd", [128, 128], BF16)
    din("hc_H128", [128, 128], BF16)
    dram["mids"] = dram["hc_mids"]
    dram["Hb"] = dram["hc_Hb"]; dram["H64bd"] = dram["hc_H64bd"]
    dram["H128"] = dram["hc_H128"]
    dram["y"] = nc.dram_tensor("y", [T, D], F32, kind="ExternalOutput").ap()
    def dint(name, shape, dt):
        dram[name] = nc.dram_tensor(name, list(shape), dt, kind="Internal").ap()
    dint("xs1", [T, D], F32); dint("xs2", [T, D], F32); dint("xs3", [T, D], F32)
    dint("attn_scr0", [D, T], BF16); dint("attn_scr1", [D, T], BF16)
    dint("w1T", [D, 5632], BF16); dint("w2T", [D, 5632], BF16)
    dint("w3T", [5632, D], BF16)
    dint("sclscr", [3, NSC], F32); dint("deqscr", [3, NSC], F32)
    dram["scl"] = dram["sclscr"]; dram["deq"] = dram["deqscr"]
    dint("wgT", [D, 8192], BF16); dint("wuT", [D, 8192], BF16)
    dint("wdT", [8192, D], BF16)
    dint("h_scr", [T, 8192], F32)
    dint("y1_scr", [128, 64 * T], BF16)
    dint("xq_scr", [128, 64 * T], BF16)

    S = {"_dram": dram}

    def sb(name, shape, dt):
        S[name] = es.enter_context(nc.sbuf_tensor("s_" + name, list(shape), dt))
        return S[name]

    arh = es.enter_context(nc.sbuf_tensor("s_arena", [128, 12288], F32))
    ar = arh[:]
    brh = es.enter_context(nc.sbuf_tensor("s_barena", [128, 43392], BF16))
    br = brh[:]
    big = ar[:, 0:4096]
    _alloc_common(S, sb, big)
    for nm in ("invsqrt128_col", "invsqrt64_col", "invsqrt2048_col",
               "invsqrt8192_col", "ln2_col", "invln2_col", "cnt_col",
               "mean_col", "rc254_col"):
        sb(nm, (128, 1), F32)
    sb("mean1", (1, 1), F32)
    sb("ones_row_f", (1, 128), F32)
    sb("mids", (128, 16), F32)
    sb("sw_col", (128, 3), F32); sb("fcol", (128, 3), F32)
    sb("fscl", (128, 32), F32); sb("fscl2", (128, 16), F32)
    sb("sc_row", (1, 128), BF16)
    sb("kmax", (1, 1), F32)
    sb("rs_row", (1, 512), F32); sb("rsb", (1, 512), BF16)
    sb("rowb", (1, D), BF16)
    sb("Xf", (128, ND * T), BF16)
    sb("Hconst", (128, 256), BF16)
    mean_bf = es.enter_context(nc.sbuf_tensor("s_meanbf", [1, 1], BF16))
    mx_col = es.enter_context(nc.sbuf_tensor("s_mx", [128, 8], F32))
    S["mx"] = mx_col

    S["wslab"] = br[:, 0:8192]
    S["xb_row"] = br[:, 0:2048]
    S["Vb"] = br[:, 8192:16384]
    S["QTf"] = br[:, 16384:24576]
    S["KTf"] = br[:, 24576:32768]
    S["PT"] = br[:, 32768:33792]
    S["sigb"] = br[:, 33792:34304]
    S["attn_h"] = br[:, 34304:36352]
    S["sqb"] = S["attn_h"]
    S["lnjunk"] = S["attn_h"]
    S["qsc"] = br[:, 36352:36864]
    S["rot"] = br[:, 36864:37376]
    S["tmp512"] = br[:, 37376:37888]
    S["ropebuf"] = br[:, 37888:38400]
    S["cosT0"] = br[:, 38400:40448]
    S["sinT0"] = br[:, 40448:42496]
    S["cosT1"] = br[:, 38400:39424]
    S["sinT1"] = br[:, 40448:41472]
    S["masks"] = br[:, 42496:43392]
    S["rowf"] = ar[0:1, 4096:6144]
    S["wst"] = ar[:, 6144:6656]
    S["xres"] = ar[:, 6656:8704]

    AAL = {"rt_ang": "big", "rt_red": "big", "rt_k": "big", "rt_t": "big",
           "rt_tmp": "big", "rt_r": "big", "rt_sin": "big", "rt_cos": "big",
           "xst": "big", "zt0": "big", "zt1": "big", "sqb": "attn_h",
           "rowf": "big2", "lnjunk": "attn_h",
           "xb_row": "wslab", "wtile": "wslab", "bsl1": "wslab", "w1s": "wslab"}

    psA = es.enter_context(nc.psum_tensor("psA", [128, 512], F32))
    psB = es.enter_context(nc.psum_tensor("psB", [128, 512], F32))
    psC = es.enter_context(nc.psum_tensor("psC", [128, 512], F32))
    psD = es.enter_context(nc.psum_tensor("psD", [128, 512], F32))
    psAV = es.enter_context(nc.psum_tensor("psAV", [128, 512], F32))
    psSIG = es.enter_context(nc.psum_tensor("psSIG", [1, 512], F32))
    psT = es.enter_context(nc.psum_tensor("psT", [128, 128], BF16))

    def dmap(**over):
        d = dict(dram)
        d.update(over)
        return d

    with es:
        p = Prog(nc, es)
        p.alias = dict(AAL)
        emit_prelude(p, S, need_l0_rope=True)
        p.emit("sync", lambda e: e.dma_start(out=S["mids"][:], in_=dram["mids"]),
               writes=["mids"], dma=True)
        p.emit("sync", lambda e: e.dma_start(out=S["Hconst"][:, 0:128], in_=dram["H64bd"]),
               writes=["bHc"], dma=True)
        p.emit("sync", lambda e: e.dma_start(out=S["Hconst"][:, 128:256], in_=dram["H128"]),
               writes=["bHc"], dma=True)
        p.emit("vector", lambda e: e.memset(S["tmp_col"][:], 2.0), writes=["tmp_col"])
        p.emit("scalar", lambda e: e.activation(S["ln2_col"][:], S["tmp_col"][:], AF.Ln),
               reads=["tmp_col"], writes=["ln2_col"])
        p.emit("vector", lambda e: e.reciprocal(S["invln2_col"][:], S["ln2_col"][:]),
               reads=["ln2_col"], writes=["invln2_col"])
        p.emit("vector", lambda e: e.memset(S["ones_row_f"][0:1, :], 1.0), writes=["ones_row_f"])
        p.emit("vector", lambda e: e.memset(S["rc254_col"][:], 254.0), writes=["rc254_col"])
        p.emit("vector", lambda e: e.reciprocal(S["rc254_col"][:], S["rc254_col"][:]),
               reads=["rc254_col"], writes=["rc254_col"])
        p.emit("vector", lambda e: e.memset(S["cnt_col"][:], float(NSC)), writes=["cnt_col"])
        p.emit("vector", lambda e: e.reciprocal(S["cnt_col"][:], S["cnt_col"][:]),
               reads=["cnt_col"], writes=["cnt_col"])

        d0 = dmap(x=dram["x"], y=dram["xs1"], attn_scr=dram["attn_scr0"],
                  Wq=dram["a0Wq"], Wk=dram["a0Wk"], Wv=dram["a0Wv"], Wo=dram["a0Wo"],
                  bq=dram["a0bq"], bk=dram["a0bk"], bv=dram["a0bv"], bo=dram["a0bo"])
        S0 = dict(S); S0["_dram"] = d0
        emit_l0a(p, S0, d0, big, psA, psB, psC, psAV, psSIG, psT)
        p.epoch()

        d1 = dmap(x=dram["xs1"], y=dram["xs2"],
                  w1=dram["f0w1"], w2=dram["f0w2"], w3=dram["f0w3"])
        S1 = dict(S); S1["_dram"] = d1
        emit_l0f(p, S1, d1, ar, br, mean_bf, psA, psB, psC, psD, psSIG, psT)
        p.epoch()

        emit_prelude(p, S, need_l1_rope=True)
        d2 = dmap(x=dram["xs2"], y=dram["xs3"], attn_scr=dram["attn_scr1"],
                  Wq=dram["a1Wq"], Wk=dram["a1Wk"], Wv=dram["a1Wv"], Wo=dram["a1Wo"],
                  bo=dram["a1bo"])
        S2 = dict(S); S2["_dram"] = d2
        emit_l1a(p, S2, d2, big, psA, psB, psC, psAV, psSIG, psT)
        p.epoch()

        d3 = dmap(x=dram["xs3"], y=dram["y"],
                  wg=dram["f1wg"], wu=dram["f1wu"], wd=dram["f1wd"])
        S3 = dict(S); S3["_dram"] = d3
        zacc_f = [ar[:, 8192:10240], ar[:, 10240:12288]]
        cst = S["cst"]
        emit_l1f_prep(p, S3, d3, ar, br, psA, psB, psC, psD, psT)
        _emit_l1ffn_main(p, S3, d3, ar, br, zacc_f, ar[:, 2048:4096], mx_col,
                         (psA, psB, psC, psD, psT), cst)

        with nc.Block() as block:
            p.finalize(block)
    return nc


def _in_maps_fused(x_core, inputs, hc):
    m = dict(x=np.ascontiguousarray(x_core))
    m["a0Wq"] = inputs["L0_attn_Wq"]; m["a0Wk"] = inputs["L0_attn_Wk"]
    m["a0Wv"] = inputs["L0_attn_Wv"]; m["a0Wo"] = inputs["L0_attn_Wo"]
    m["a0bq"] = inputs["L0_attn_bq"].reshape(1, D)
    m["a0bk"] = inputs["L0_attn_bk"].reshape(1, D)
    m["a0bv"] = inputs["L0_attn_bv"].reshape(1, D)
    m["a0bo"] = inputs["L0_attn_bo"].reshape(1, D)
    m["f0w1"] = inputs["L0_ffn_weight1"]; m["f0w2"] = inputs["L0_ffn_weight2"]
    m["f0w3"] = inputs["L0_ffn_weight3"]
    m["a1Wq"] = inputs["L1_attn_Wq"]; m["a1Wk"] = inputs["L1_attn_Wk"]
    m["a1Wv"] = inputs["L1_attn_Wv"]; m["a1Wo"] = inputs["L1_attn_Wo"]
    m["a1bo"] = inputs["L1_attn_b_out"].reshape(1, D)
    m["f1wg"] = inputs["L1_ffn_w_gate"]; m["f1wu"] = inputs["L1_ffn_w_up"]
    m["f1wd"] = inputs["L1_ffn_w_down"]
    m.update(hc)
    m["hc_mids"] = _mid_col_consts()
    for k, v in _l1ffn_consts().items():
        m["hc_" + k] = v
    return m


_NC_CACHE = {}


def _get_nc(name):
    if name not in _NC_CACHE:
        _NC_CACHE[name] = {
            "l0a": build_l0_attn, "l0f": build_l0_ffn,
            "l1a": build_l1_attn, "l1f": build_l1_ffn,
            "fused": build_fused,
        }[name]()
    return _NC_CACHE[name]


def kernel(**inputs):
    hc = _hc_maps()
    x = np.asarray(inputs["L0_attn_x"])
    cores = list(range(8))
    nc = _get_nc("fused")
    maps = [_in_maps_fused(x[c], inputs, hc) for c in cores]
    res = run_bass_kernel_spmd(nc, maps, core_ids=cores)
    return np.stack([r["y"] for r in res.results]).astype(np.float32)
